# version 14
# baseline (speedup 1.0000x reference)
import numpy as np
from contextlib import ExitStack

import concourse.bass as bass
import concourse.mybir as mybir
from concourse.bass_utils import run_bass_kernel_spmd

F32 = mybir.dt.float32
BF16 = mybir.dt.bfloat16
AF = mybir.ActivationFunctionType
ALU = mybir.AluOpType
AX = mybir.AxisListType
AP = bass.AP

B, T, D = 4, 4096, 1024
OWN = 2048
NCH = 8
NHG = 8
FINNER = 512
HID = 2730
HIDP = 2816
H2P = 2 * HIDP
QK_SCALE = 8.0
L1_QKV_C0, L1_ATT_C0 = 6, 10
L2_QKV_C0, L2_ATT_C0 = 11, 15
NCHUNK = 32
TAB_T0 = L1_QKV_C0 * 128
TAB_N = T - TAB_T0
X1T0 = L1_ATT_C0 * 128
X1N = T - X1T0
NEG = -60000.0


class PSem:
    def __init__(self, ctx, nc, name):
        self.h = ctx.enter_context(nc.semaphore(name))
        self.n = 0
        self.name = name
        self.producers = set()


class Plan:
    def __init__(self):
        self.steps = []
        self.G = {}

    def gate(self, *keys):
        out = []
        for k in keys:
            out += self.G.get(k, [])
        return out

    def set_gate(self, key, sem):
        self.G[key] = [(sem, sem.n)]

    def step(self, eng, emit_fn, waits=(), incs=()):
        self.steps.append((eng, [(s, v) for (s, v) in waits if v > 0], emit_fn, list(incs)))
        for s, k in incs:
            s.n += k
            s.producers.add(eng)

    def emit(self, nc, block):
        by_eng = {}
        for eng, ws, fn, incs in self.steps:
            by_eng.setdefault(eng, []).append((ws, fn, incs))

        def mk_body(items, ename):
            def body(e):
                ee = _DrainProxy(e) if ename == "vector" else e
                last = {}
                for ws, fn, incs in items:
                    for s, v in ws:
                        if s.producers == {ename} and ename != "vector":
                            continue
                        if last.get(s.name, -1) >= v:
                            continue
                        e.wait_ge(s.h, v)
                        last[s.name] = v
                    inst = fn(ee)
                    for ii, (s, k) in enumerate(incs):
                        if ii == 0:
                            inst.then_inc(s.h, k)
                        else:
                            e.nop().then_inc(s.h, k)
            return body

        emap = {"sync": mybir.EngineType.SP, "gpsimd": mybir.EngineType.Pool,
                "tensor": mybir.EngineType.PE, "scalar": mybir.EngineType.Activation,
                "vector": mybir.EngineType.DVE}
        for ename in ("sync", "gpsimd", "tensor", "scalar", "vector"):
            if ename in by_eng:
                block._start_engine(mk_body(by_eng[ename], ename), emap[ename])


class _DrainProxy:

    _OPS = {"tensor_copy", "tensor_tensor", "tensor_add", "tensor_sub", "tensor_mul",
            "tensor_max", "tensor_scalar", "tensor_single_scalar", "tensor_scalar_add",
            "tensor_scalar_sub", "tensor_scalar_mul", "tensor_scalar_max",
            "tensor_scalar_min", "tensor_relu", "tensor_reduce", "reduce_max",
            "reduce_sum", "reciprocal", "memset", "memzero", "select", "transpose",
            "bn_stats", "bn_aggr", "stream_shuffle", "pool", "pool_avg", "pool_max"}

    def __init__(self, e):
        self._e = e

    def __getattr__(self, name):
        f = getattr(self._e, name)
        if name in _DrainProxy._OPS:
            def g(*a, **k):
                self._e.drain()
                return f(*a, **k)
            return g
        return f


def w(*sems):
    return [(s, s.n) for s in sems]


def wb(s, k):
    return [(s, max(0, s.n - k))]


def _rotary_tables(real_pos):
    tf = real_pos.astype(np.float32)
    try:
        import jax
        import jax.numpy as jnp
        with jax.default_device(jax.devices("cpu")[0]):
            half_idx = jnp.arange(0, 64, 2, dtype=jnp.float32)
            inv_freq = 1.0 / 10000.0 ** (half_idx / 64)
            t = jnp.asarray(tf)
            freqs = t[:, None] * inv_freq[None, :]
            base = (half_idx + 0.4 * 64) / (1.4 * 64)
            power = (t - T // 2) / float(512 // 2)
            scale = base[None, :] ** power[:, None]
            cos, sin = jnp.cos(freqs), jnp.sin(freqs)
            cos, sin, scale = (np.asarray(a, np.float64) for a in (cos, sin, scale))
    except Exception:
        half_idx = np.arange(0, 64, 2, dtype=np.float32)
        inv_freq = np.float32(1.0) / np.power(np.float32(10000.0), half_idx / np.float32(64.0))
        freqs = tf[:, None] * inv_freq[None, :]
        base = (half_idx + np.float32(0.4 * 64)) / np.float32(1.4 * 64)
        power = (tf - np.float32(T // 2)) / np.float32(512 // 2)
        scale = np.power(base[None, :], power[:, None], dtype=np.float32)
        cos, sin, scale = (a.astype(np.float64) for a in (np.cos(freqs), np.sin(freqs), scale))
    cos = np.concatenate([cos, cos], axis=1)
    sin = np.concatenate([sin, sin], axis=1)
    scale = np.concatenate([scale, scale], axis=1)
    his, los = [], []
    for a in (cos * scale, sin * scale, cos / scale, sin / scale):
        a = np.concatenate([a.T, a.T], axis=0)
        hi = a.astype(np.float32)
        lo = (a - hi.astype(np.float64)).astype(np.float32)
        his.append(np.ascontiguousarray(hi))
        los.append(np.ascontiguousarray(lo))
    return his + los


def _host_consts():
    c = {}
    c["identc"] = np.eye(128, dtype=np.float32)
    r = np.zeros((128, 128), dtype=np.float32)
    for blk in range(2):
        o = 64 * blk
        for j in range(32):
            r[o + j + 32, o + j] = -1.0
            r[o + j, o + j + 32] = 1.0
    c["r2sc"] = r
    kk, qq = np.meshgrid(np.arange(128), np.arange(128), indexing="ij")
    c["tric"] = (qq >= kk).astype(np.float32)
    c["antic"] = (kk >= qq).astype(np.float32)
    ha = np.zeros((1, 128), np.float32)
    ha[0, 0:64] = 1.0
    hb = np.zeros((1, 128), np.float32)
    hb[0, 64:128] = 1.0
    c["halfac"] = ha
    c["halfbc"] = hb
    return c


def _w_fm(wmat):
    K, N = wmat.shape
    return np.ascontiguousarray(wmat.reshape(K // 128, 128, N).transpose(1, 0, 2))


def build_program():
    nc = bass.Bass("TRN2", target_bir_lowering=False, detect_race_conditions=False)
    es = ExitStack()

    dram = {}

    def din(name, shape):
        dram[name] = nc.dram_tensor(name, shape, F32, kind="ExternalInput")

    din("x_ctx", [T, D])
    din("wqkv_a", [128, NCH, 3 * D])
    din("bqkv_a", [1, 3 * D])
    din("wproj", [128, NCH, D])
    din("bproj", [1, D])
    for l in range(2):
        din(f"fwqkv{l}", [128, NCH, 3 * FINNER])
        din(f"fwo{l}", [128, 4, D])
        din(f"fw1{l}", [128, NCH, H2P])
        din(f"fb1{l}", [1, H2P])
        din(f"fw2{l}", [128, HIDP // 128, D])
        din(f"fb2{l}", [1, D])
        din(f"lng{l}", [128, NCH])
        din(f"lnb{l}", [128, NCH])
        din(f"qs{l}", [128, 1])
        din(f"qssw{l}", [128, 1])
        din(f"ks{l}", [128, 1])
        din(f"kssw{l}", [128, 1])
        din(f"qsr{l}", [1, 128])
        din(f"ksr{l}", [1, 128])
    for i in range(4):
        din(f"tab{i}", [128, TAB_N])
        din(f"tab{i}lo", [128, TAB_N])
    for n in ("identc", "r2sc", "tric", "antic"):
        din(n, [128, 128])
    din("halfac", [1, 128])
    din("halfbc", [1, 128])
    din("bmask", [128, 1])
    din("logva", [128, 32])
    din("logvf", [128, 32])

    d_out = nc.dram_tensor("out_own", [OWN, D], F32, kind="ExternalOutput")
    d_acc = nc.dram_tensor("acc_scr", [OWN, D], F32, kind="Internal")
    d_x1 = nc.dram_tensor("x1_scr", [NCH, 128, X1N], F32, kind="Internal")

    sb = lambda name, shape, dt: es.enter_context(nc.sbuf_tensor(name, shape, dt))

    C = {}
    C["idb"] = sb("idb", [128, 128], BF16)
    C["idf"] = sb("idf", [128, 128], F32)
    C["r2sb"] = sb("r2sb", [128, 128], BF16)
    C["trib"] = sb("trib", [128, 128], BF16)
    C["antib"] = sb("antib", [128, 128], BF16)
    C["halfa"] = sb("halfa", [1, 128], F32)
    C["halfb"] = sb("halfb", [1, 128], F32)
    C["bmask"] = sb("bmask_s", [128, 1], F32)
    C["logva"] = sb("logva_s", [128, 32], F32)
    C["logvf"] = sb("logvf_s", [128, 32], F32)
    C["onesrow"] = sb("onesrow", [1, 512], BF16)
    C["onesf1"] = sb("onesf1", [1, 128], F32)
    C["onescol"] = sb("onescol", [128, 1], F32)
    per_layer = {}
    for l in range(2):
        pc = {}
        pc["lng"] = sb(f"lng_s{l}", [128, NCH], F32)
        pc["lnb"] = sb(f"lnb_s{l}", [128, NCH], F32)
        pc["qs"] = sb(f"qs_s{l}", [128, 1], F32)
        pc["qssw"] = sb(f"qssw_s{l}", [128, 1], F32)
        pc["ks"] = sb(f"ks_s{l}", [128, 1], F32)
        pc["kssw"] = sb(f"kssw_s{l}", [128, 1], F32)
        pc["qsr"] = sb(f"qsr_s{l}", [1, 128], F32)
        pc["ksr"] = sb(f"ksr_s{l}", [1, 128], F32)
        per_layer[l] = pc

    sems = {}

    def S(name):
        if name not in sems:
            sems[name] = PSem(es, nc, name)
        return sems[name]

    with nc.sbuf_tensor("xn", [128, NCH, T], BF16) as xn:
        emit_p0(nc, S, dram, xn, C, per_layer)
        emit_p1(nc, S, dram, xn, C, d_acc)
        cl = dict(C)
        cl.update(per_layer[0])
        dr0 = dict(fwqkv=dram["fwqkv0"], fwo=dram["fwo0"], fw1=dram["fw10"],
                   fw2=dram["fw20"], fb1=dram["fb10"], fb2=dram["fb20"], tabs=[dram[f"tab{i}"] for i in range(4)],
                   tabslo=[dram[f"tab{i}lo"] for i in range(4)],
                   x1=d_x1, x=dram["x_ctx"], acc=d_acc, out=d_out, lname="0")
        emit_ffn_layer(nc, S, 0, xn, cl, dr0)
    cl = dict(C)
    cl.update(per_layer[1])
    dr1 = dict(fwqkv=dram["fwqkv1"], fwo=dram["fwo1"], fw1=dram["fw11"],
               fw2=dram["fw21"], fb1=dram["fb11"], fb2=dram["fb21"], tabs=[dram[f"tab{i}"] for i in range(4)],
               tabslo=[dram[f"tab{i}lo"] for i in range(4)],
               x1=d_x1, x=dram["x_ctx"], acc=d_acc, out=d_out, lname="1")
    emit_ffn_layer(nc, S, 1, None, cl, dr1)

    es.close()
    return nc


def emit_p0(nc, S, dram, xn, C, per_layer):
    with (
        nc.sbuf_tensor("p0_xt", [128, 2, D], F32) as xt,
        nc.sbuf_tensor("p0_xnt", [128, 2, D], BF16) as xnt,
        nc.sbuf_tensor("p0_sq", [128, 2, D], F32) as sq0,
        nc.sbuf_tensor("p0_st", [128, 16], F32) as st8,
        nc.psum_tensor("p0_TPa", [128, 1024], F32) as TPa,
        nc.psum_tensor("p0_TPb", [128, 1024], F32) as TPb,
    ):
        TP = [TPa, TPb]
        pl = Plan()
        dxs = [S("p0x0"), S("p0x1")]
        dcon, dwq = S("p0c"), S("p0w")
        s_sq, v_st, s_sd, v_rs, s_nm = S("p0sq"), S("p0vst"), S("p0sd"), S("p0vrs"), S("p0nm")
        pe_T, v_ev = S("p0T"), S("p0ev")

        def cl(dst, src):
            pl.step("sync", lambda e, d=dst, s=src: e.dma_start(d, s), incs=[(dcon, 16)])

        cl(C["bmask"][:, :], dram["bmask"][:, :])
        cl(C["logva"][:, :], dram["logva"][:, :])
        cl(C["logvf"][:, :], dram["logvf"][:, :])
        cl(C["idf"][:, :], dram["identc"][:, :])
        cl(C["halfa"][:, :], dram["halfac"][:, :])
        cl(C["halfb"][:, :], dram["halfbc"][:, :])
        for l in range(2):
            for nm in ("lng", "lnb", "qs", "qssw", "ks", "kssw", "qsr", "ksr"):
                cl(per_layer[l][nm][:, :], dram[f"{nm}{l}"][:, :])

        def gl(dst, src):
            pl.step("gpsimd", lambda e, d=dst, s=src: e.dma_start(d, s), incs=[(dwq, 16)])

        gl(C["idb"][:, :], dram["identc"][:, :])
        gl(C["r2sb"][:, :], dram["r2sc"][:, :])
        gl(C["trib"][:, :], dram["tric"][:, :])
        gl(C["antib"][:, :], dram["antic"][:, :])

        pl.step("vector", lambda e: e.memset(C["onesrow"][:, :], 1.0), incs=[(v_ev, 1)])
        pl.step("vector", lambda e: e.memset(C["onesf1"][:, :], 1.0), incs=[(v_ev, 1)])
        pl.step("vector", lambda e: e.memset(C["onescol"][:, :], 1.0), incs=[(v_ev, 1)])

        for i in range(32):
            b = i % 2
            t0 = 128 * i
            pl.step("sync",
                    lambda e, b=b, t0=t0: e.dma_start(xt[:, b, :], dram["x_ctx"][t0:t0 + 128, :]),
                    waits=pl.gate(("xt", b)), incs=[(dxs[b], 16)])
            pl.step("vector",
                    lambda e, b=b: e.reduce_sum(st8[:, 8 * b + 0:8 * b + 1], xt[:, b, :], axis=AX.X),
                    waits=w(dxs[b]))
            pl.step("scalar",
                    lambda e, b=b: e.activation(sq0[:, b, :], xt[:, b, :], AF.Square,
                                                accum_out=st8[:, 8 * b + 1:8 * b + 2]),
                    waits=w(dxs[b]), incs=[(s_sq, 1)])

            def _stats(e, b=b):
                c0 = 8 * b
                e.tensor_scalar_mul(st8[:, c0 + 2:c0 + 3], st8[:, c0 + 0:c0 + 1], 1.0 / D)
                e.tensor_scalar_mul(st8[:, c0 + 3:c0 + 4], st8[:, c0 + 1:c0 + 2], 1.0 / D)
                e.tensor_tensor(st8[:, c0 + 4:c0 + 5], st8[:, c0 + 2:c0 + 3], st8[:, c0 + 2:c0 + 3], ALU.mult)
                e.tensor_tensor(st8[:, c0 + 5:c0 + 6], st8[:, c0 + 3:c0 + 4], st8[:, c0 + 4:c0 + 5], ALU.subtract)
                return e.tensor_scalar_add(st8[:, c0 + 5:c0 + 6], st8[:, c0 + 5:c0 + 6], 1e-5)
            pl.step("vector", _stats, waits=w(s_sq), incs=[(v_st, 1)])
            pl.step("scalar",
                    lambda e, b=b: e.activation(st8[:, 8 * b + 6:8 * b + 7], st8[:, 8 * b + 5:8 * b + 6], AF.Sqrt),
                    waits=w(v_st), incs=[(s_sd, 1)])

            def _rs(e, b=b):
                c0 = 8 * b
                e.reciprocal(st8[:, c0 + 7:c0 + 8], st8[:, c0 + 6:c0 + 7])
                return e.tensor_scalar(st8[:, c0 + 4:c0 + 5], st8[:, c0 + 2:c0 + 3],
                                       st8[:, c0 + 7:c0 + 8], -1.0, ALU.mult, ALU.mult)
            pl.step("vector", _rs, waits=w(s_sd), incs=[(v_rs, 1)])
            pl.step("scalar",
                    lambda e, b=b: e.activation(xnt[:, b, :], xt[:, b, :], AF.Identity,
                                                bias=st8[:, 8 * b + 4:8 * b + 5],
                                                scale=st8[:, 8 * b + 7:8 * b + 8]),
                    waits=w(v_rs), incs=[(s_nm, 1)])
            pl.set_gate(("xt", b), s_nm)

            def _tr(e, b=b):
                inst = None
                for c in range(8):
                    inst = e.matmul(TP[b][:, 128 * c:128 * (c + 1)],
                                    xnt[:, b, 128 * c:128 * (c + 1)], C["idb"][:, :],
                                    start=True, stop=True)
                return inst
            pl.step("tensor", _tr, waits=w(s_nm, dwq) + pl.gate(("TP", b)), incs=[(pe_T, 1)])
            def _evx(e, b=b, t0=t0):
                inst = None
                for c in range(8):
                    inst = e.tensor_copy(xn[:, c, t0:t0 + 128], TP[b][:, 128 * c:128 * (c + 1)])
                return inst
            pl.step("vector", _evx, waits=w(pe_T), incs=[(v_ev, 1)])
            pl.set_gate(("TP", b), v_ev)

        pl.step("sync", lambda e: e.nop(), waits=w(dcon))
        pl.step("gpsimd", lambda e: e.nop(), waits=w(dwq))

        with nc.Block() as blk:
            pl.emit(nc, blk)


def emit_p1(nc, S, dram, xn, C, d_acc):
    with ExitStack() as xs:
        wqsl = xs.enter_context(nc.sbuf_tensor("p1_wq", [128, 2, 3, NCH, 128], BF16))
        bqkv = xs.enter_context(nc.sbuf_tensor("p1_bq", [1, 3 * D], BF16))
        bproj = xs.enter_context(nc.sbuf_tensor("p1_bp", [1, D], BF16))
        Kfm = xs.enter_context(nc.sbuf_tensor("p1_K", [128, T], BF16))
        Vtm = xs.enter_context(nc.sbuf_tensor("p1_V", [128, NCHUNK * 130], BF16))
        Qfm = xs.enter_context(nc.sbuf_tensor("p1_Q", [128, OWN], BF16))
        vtmp = xs.enter_context(nc.sbuf_tensor("p1_vt", [128, 512], BF16))
        PT = xs.enter_context(nc.sbuf_tensor("p1_PT", [128, 2, 1024], BF16))
        bb = xs.enter_context(nc.sbuf_tensor("p1_bb", [128, 40], F32))
        rl = xs.enter_context(nc.sbuf_tensor("p1_rl", [128, 1024], F32))
        ofm = xs.enter_context(nc.sbuf_tensor("p1_o", [128, NHG * OWN], BF16))
        wps = xs.enter_context(nc.sbuf_tensor("p1_wp", [128, NCH, D], BF16))
        stg = xs.enter_context(nc.sbuf_tensor("p1_stg", [128, 2, D], F32))
        ST0 = xs.enter_context(nc.psum_tensor("p1_ST0", [128, 1024], F32))
        ST1 = xs.enter_context(nc.psum_tensor("p1_ST1", [128, 1024], F32))
        AVA = xs.enter_context(nc.psum_tensor("p1_AVA", [128, 512], F32))
        AVB = xs.enter_context(nc.psum_tensor("p1_AVB", [128, 512], F32))
        RRA = xs.enter_context(nc.psum_tensor("p1_RRA", [128, 512], F32))
        RRB = xs.enter_context(nc.psum_tensor("p1_RRB", [128, 512], F32))

        STb = [ST0, ST1]
        pl = Plan()
        d_c1 = S("p1c1")
        d_w = [S("p1w0"), S("p1w1")]
        pe_mm, ev_q = S("p1mm"), S("p1ev")
        pe_st, s_exp, v_msk, pe_av = S("p1st"), S("p1exp"), S("p1msk"), S("p1av")
        v_rl, pe_rr, v_on = S("p1rl"), S("p1rr"), S("p1on")
        v_bnd, pe_bnd = S("p1bnd"), S("p1pebnd")
        v_stg = S("p1stg")
        d_o1 = [S("p1out0"), S("p1out1")]

        pl.step("gpsimd", lambda e: e.dma_start(bqkv[:, :], dram["bqkv_a"][:, :]), incs=[(d_c1, 16)])
        pl.step("gpsimd", lambda e: e.dma_start(bproj[:, :], dram["bproj"][:, :]), incs=[(d_c1, 16)])
        pl.step("gpsimd", lambda e: e.dma_start(wps[:, :, :], dram["wproj"][:, :, :]), incs=[(d_c1, 16)])
        pl.step("vector", lambda e: e.memset(AP(Vtm, 64, [[130 * NCHUNK, 128], [130, NCHUNK]]), 1.0),
                incs=[(v_on, 1)])
        pl.set_gate("ones", v_on)

        wq_ready = {}

        def prefetch(hg):
            hb = hg % 2
            for s3, col in ((0, hg * 128), (1, D + hg * 128), (2, 2 * D + hg * 128)):
                pl.step("gpsimd",
                        lambda e, hb=hb, s3=s3, col=col: e.dma_start(
                            wqsl[:, hb, s3, :, :],
                            AP(dram["wqkv_a"], col, [[NCH * 3 * D, 128], [3 * D, NCH], [1, 128]])),
                        waits=(w(pe_mm) if hg >= 2 else []), incs=[(d_w[hb], 16)])
            wq_ready[hg] = (d_w[hg % 2], d_w[hg % 2].n)

        prefetch(0)
        prefetch(1)
        for hg in range(NHG):
            hb = hg % 2

            def build_fm(dst, s3, t0s, own_off, col):
                for gi, t0 in enumerate(t0s):
                    ps = STb[gi % 2]

                    def _mm(e, t0=t0, s3=s3, col=col, ps=ps, hb=hb):
                        for c in range(NCH):
                            e.matmul(ps[:, 0:512], wqsl[:, hb, s3, c, :],
                                     xn[:, c, t0:t0 + 512], start=(c == 0), stop=False)
                        return e.matmul(ps[:, 0:512], bqkv[:, col:col + 128],
                                        C["onesrow"][:, :], start=False, stop=True)
                    pl.step("tensor", _mm,
                            waits=[wq_ready[hg]] + w(d_c1) + pl.gate(("ST", gi % 2)) + pl.gate("KQV"),
                            incs=[(pe_mm, 1)])
                    pl.step("scalar",
                            lambda e, dst=dst, t0=t0, ps=ps, oo=own_off:
                            e.copy(dst[:, t0 - oo:t0 - oo + 512], ps[:, 0:512]),
                            waits=w(pe_mm), incs=[(ev_q, 1)])
                    pl.set_gate(("ST", gi % 2), ev_q)

            build_fm(Kfm, 1, [512 * g for g in range(8)], 0, D + hg * 128)
            build_fm(Qfm, 0, [OWN + 512 * g for g in range(4)], OWN, hg * 128)

            for gi in range(8):
                t0 = 512 * gi
                ps = STb[gi % 2]
                av = [AVA, AVB][gi % 2]

                def _mmv(e, t0=t0, ps=ps, hb=hb, hg=hg):
                    for c in range(NCH):
                        e.matmul(ps[:, 0:512], wqsl[:, hb, 2, c, :],
                                 xn[:, c, t0:t0 + 512], start=(c == 0), stop=False)
                    return e.matmul(ps[:, 0:512], bqkv[:, 2 * D + hg * 128:2 * D + hg * 128 + 128],
                                    C["onesrow"][:, :], start=False, stop=True)
                pl.step("tensor", _mmv, waits=pl.gate(("ST", gi % 2)) + pl.gate("KQV") + pl.gate("vtmp"),
                        incs=[(pe_mm, 1)])
                pl.step("scalar", lambda e, ps=ps: e.copy(vtmp[:, :], ps[:, 0:512]),
                        waits=w(pe_mm), incs=[(ev_q, 1)])
                pl.set_gate(("ST", gi % 2), ev_q)

                def _trv(e, av=av):
                    inst = None
                    for j in range(4):
                        inst = e.matmul(av[:, 128 * j:128 * (j + 1)], vtmp[:, 128 * j:128 * (j + 1)],
                                        C["idb"][:, :], start=True, stop=True)
                    return inst
                pl.step("tensor", _trv, waits=w(ev_q) + pl.gate(("AV", gi % 2)) + pl.gate("ones"),
                        incs=[(pe_mm, 1)])
                pl.set_gate("vtmp", pe_mm)

                def _evv(e, gi=gi, av=av):
                    inst = None
                    for j in range(4):
                        base = (gi * 4 + j) * 130
                        inst = e.tensor_copy(AP(Vtm, base, [[130 * NCHUNK, 128], [1, 64]]),
                                             av[:, 128 * j:128 * j + 64])
                        inst = e.tensor_copy(AP(Vtm, base + 65, [[130 * NCHUNK, 128], [1, 64]]),
                                             av[:, 128 * j + 64:128 * j + 128])
                    return inst
                pl.step("vector", _evv, waits=w(pe_mm), incs=[(v_on, 1), (ev_q, 1)])
                pl.set_gate(("AV", gi % 2), v_on)

            if hg + 2 < NHG:
                prefetch(hg + 2)

            pl.step("vector", lambda e: e.reduce_max(bb[:, 33:34], Kfm[:, :], axis=AX.X,
                                                     apply_absolute_value=True),
                    waits=w(ev_q, s_exp), incs=[(v_bnd, 1)])
            pl.step("vector", lambda e: e.reduce_max(bb[:, 34:35], Qfm[:, :], axis=AX.X,
                                                     apply_absolute_value=True),
                    waits=w(ev_q), incs=[(v_bnd, 1)])
            pl.step("vector", lambda e: e.tensor_tensor(bb[:, 35:36], bb[:, 33:34], bb[:, 34:35], ALU.mult),
                    waits=w(v_bnd), incs=[(v_bnd, 1)])
            pl.step("tensor", lambda e: e.matmul(RRA[0:1, 0:1], C["onescol"][:, :], bb[:, 35:36],
                                                 start=True, stop=True),
                    waits=w(v_bnd) + pl.gate("RRA"), incs=[(pe_bnd, 1)])
            pl.step("vector", lambda e: e.tensor_copy(bb[0:1, 36:37], RRA[0:1, 0:1]),
                    waits=w(pe_bnd), incs=[(v_bnd, 1)])
            pl.set_gate("RRA", v_bnd)
            pl.step("tensor", lambda e: e.matmul(RRB[:, 0:1], C["onesf1"][:, :], bb[0:1, 36:37],
                                                 start=True, stop=True),
                    waits=w(v_bnd) + pl.gate("RRB"), incs=[(pe_bnd, 1)])

            def _fixb(e):
                e.tensor_scalar(bb[:, 37:38], RRB[:, 0:1], -0.125, 40.0, ALU.mult, ALU.add)
                return e.tensor_scalar_add(bb[:, 0:32], C["logva"][:, :], bb[:, 37:38])
            pl.step("vector", _fixb, waits=w(pe_bnd), incs=[(v_bnd, 1)])
            pl.set_gate("RRB", v_bnd)

            for qg in range(4):
                q0 = 512 * qg
                qt0 = 16 + 4 * qg
                kcmax = qt0 + 4
                for kc in range(kcmax):
                    bu = kc % 2
                    stp = STb[bu]

                    def _st(e, kc=kc, q0=q0, stp=stp):
                        e.matmul(stp[:, 0:512], Kfm[0:64, 128 * kc:128 * (kc + 1)],
                                 Qfm[0:64, q0:q0 + 512], start=True, stop=True)
                        return e.matmul(stp[:, 512:1024], Kfm[64:128, 128 * kc:128 * (kc + 1)],
                                        Qfm[64:128, q0:q0 + 512], start=True, stop=True)
                    pl.step("tensor", _st, waits=pl.gate(("ST", bu)) + w(v_bnd), incs=[(pe_st, 1)])
                    pl.step("scalar",
                            lambda e, bu=bu, kc=kc, stp=stp: e.activation(
                                PT[:, bu, :], stp[:, :], AF.Exp,
                                bias=bb[:, kc:kc + 1], scale=0.125),
                            waits=w(pe_st) + pl.gate(("PT", bu)), incs=[(s_exp, 1)])
                    pl.set_gate(("ST", bu), s_exp)
                    need_mask = kc >= qt0
                    if need_mask:
                        def _mask(e, kc=kc, qt0=qt0, bu=bu):
                            inst = None
                            for h in range(2):
                                for i4 in range(4):
                                    qt = qt0 + i4
                                    sl = PT[:, bu, 512 * h + 128 * i4: 512 * h + 128 * (i4 + 1)]
                                    if kc == qt:
                                        inst = e.tensor_tensor(sl, sl, C["trib"][:, :], ALU.mult)
                                    elif kc > qt:
                                        inst = e.memset(sl, 0.0)
                            return inst
                        pl.step("vector", _mask, waits=w(s_exp), incs=[(v_msk, 1)])

                    def _av(e, kc=kc, bu=bu, first=(kc == 0), last=(kc == kcmax - 1)):
                        e.matmul(AVA[0:65, :], Vtm[:, 130 * kc:130 * kc + 65], PT[:, bu, 0:512],
                                 start=first, stop=last)
                        e.matmul(AVB[0:1, :], Vtm[:, 130 * kc + 64:130 * kc + 65], PT[:, bu, 512:1024],
                                 start=first, stop=last)
                        return e.matmul(AVB[64:128, :], Vtm[:, 130 * kc + 65:130 * kc + 129],
                                        PT[:, bu, 512:1024], start=first, stop=last)
                    avw = (w(v_msk) if need_mask else w(s_exp))
                    if kc == 0:
                        avw = avw + pl.gate(("AV", 0)) + pl.gate(("AV", 1))
                    pl.step("tensor", _av, waits=avw, incs=[(pe_av, 1)])
                    pl.set_gate(("PT", bu), pe_av)

                def _rlf(e):
                    e.tensor_scalar_add(rl[0:1, 0:512], AVA[64:65, :], 1e-30)
                    e.reciprocal(rl[0:1, 0:512], rl[0:1, 0:512])
                    e.tensor_scalar_add(rl[0:1, 512:1024], AVB[0:1, :], 1e-30)
                    return e.reciprocal(rl[0:1, 512:1024], rl[0:1, 512:1024])
                pl.step("vector", _rlf, waits=w(pe_av), incs=[(v_rl, 1)])

                def _rep(e):
                    e.matmul(RRA[:, :], C["onesf1"][:, :], rl[0:1, 0:512], start=True, stop=True)
                    return e.matmul(RRB[:, :], C["onesf1"][:, :], rl[0:1, 512:1024], start=True, stop=True)
                pl.step("tensor", _rep, waits=w(v_rl) + pl.gate("RRA") + pl.gate("RRB"), incs=[(pe_rr, 1)])
                pl.step("vector", lambda e: e.tensor_copy(rl[0:64, 0:512], RRA[0:64, :]),
                        waits=w(pe_rr), incs=[(v_rl, 1)])
                pl.step("vector", lambda e: e.tensor_copy(rl[64:128, 512:1024], RRB[64:128, :]),
                        waits=w(pe_rr), incs=[(v_rl, 1)])
                pl.set_gate("RRA", v_rl)
                pl.set_gate("RRB", v_rl)

                def _norm(e, hg=hg, q0=q0):
                    o0 = hg * OWN + q0
                    e.tensor_tensor(ofm[0:64, o0:o0 + 512], AVA[0:64, :], rl[0:64, 0:512], ALU.mult)
                    return e.tensor_tensor(ofm[64:128, o0:o0 + 512], AVB[64:128, :],
                                           rl[64:128, 512:1024], ALU.mult)
                pl.step("vector", _norm, waits=w(v_rl), incs=[(v_on, 1), (v_rl, 1)])
                pl.set_gate(("AV", 0), v_on)
                pl.set_gate(("AV", 1), v_on)
            pl.set_gates = None
            pl.G["KQV"] = w(pe_st) + w(pe_av) + w(s_exp)

        for tt in range(16):
            bu = tt % 2
            ps = STb[bu]

            def _pmm(e, tt=tt, ps=ps):
                inst = None
                for half in range(2):
                    for c in range(NCH):
                        e.matmul(ps[:, 512 * half:512 * (half + 1)],
                                 ofm[:, c * OWN + 128 * tt: c * OWN + 128 * (tt + 1)],
                                 wps[:, c, 512 * half:512 * (half + 1)],
                                 start=(c == 0), stop=False)
                    inst = e.matmul(ps[:, 512 * half:512 * (half + 1)],
                                    C["onesrow"][:, 0:128], bproj[:, 512 * half:512 * (half + 1)],
                                    start=False, stop=True)
                return inst
            pl.step("tensor", _pmm, waits=w(v_on) + pl.gate(("ST", bu)), incs=[(pe_mm, 1)])
            pl.step("vector", lambda e, bu=bu, ps=ps: e.tensor_copy(stg[:, bu, :], ps[:, :]),
                    waits=w(pe_mm) + pl.gate(("stg", bu)), incs=[(v_stg, 1)])
            pl.set_gate(("ST", bu), v_stg)
            pl.step("sync", lambda e, tt=tt, bu=bu: e.dma_start(d_acc[128 * tt:128 * (tt + 1), :], stg[:, bu, :]),
                    waits=w(v_stg), incs=[(d_o1[bu], 16)])
            pl.set_gate(("stg", bu), d_o1[bu])
        pl.step("sync", lambda e: e.nop(), waits=w(d_o1[0]) + w(d_o1[1]))

        with nc.Block() as blk:
            pl.emit(nc, blk)


def emit_ffn_layer(nc, S, L, xn, C, dr):
    QC0 = L1_QKV_C0 if L == 0 else L2_QKV_C0
    AC0 = L1_ATT_C0 if L == 0 else L2_ATT_C0
    DOMQ = (NCHUNK - QC0) * 128
    DOMA = (NCHUNK - AC0) * 128
    QT0 = QC0 * 128
    AT0 = AC0 * 128
    NKC = NCHUNK - QC0

    qgs = []
    c = AC0
    while c < NCHUNK:
        qgs.append((c, min(c + 4, NCHUNK)))
        c = min(c + 4, NCHUNK)

    with ExitStack() as outer:
        off = outer.enter_context(nc.sbuf_tensor(f"f{L}_o", [128, 4 * DOMA], BF16))
        _ffn_attn(nc, S, L, xn, C, dr, off, QC0, AC0, DOMQ, DOMA, QT0, NKC, qgs)
        _ffn_mlp(nc, S, L, xn, C, dr, off, AC0, DOMA, AT0, qgs)


def _ffn_attn(nc, S, L, xn, C, dr, off, QC0, AC0, DOMQ, DOMA, QT0, NKC, qgs):
    with ExitStack() as xs:
        srcbf = None
        if L == 1:
            srcbf = xs.enter_context(nc.sbuf_tensor(f"f{L}_src", [128, NCH, DOMQ + 128], BF16))
        wqf = xs.enter_context(nc.sbuf_tensor(f"f{L}_wq", [128, NCH, 3 * FINNER], BF16))
        Dt = xs.enter_context(nc.sbuf_tensor(f"f{L}_Dt", [128, 4, TAB_N], F32))
        qk = xs.enter_context(nc.sbuf_tensor(f"f{L}_qk", [128, 2, DOMQ], BF16))
        Vtf = xs.enter_context(nc.sbuf_tensor(f"f{L}_V", [128, NKC * 130], BF16))
        PTf = xs.enter_context(nc.sbuf_tensor(f"f{L}_PT", [128, 2, 1024], BF16))
        bbf = xs.enter_context(nc.sbuf_tensor(f"f{L}_bb", [128, 40], F32))
        rlw = xs.enter_context(nc.sbuf_tensor(f"f{L}_rl", [128, 1024], F32))
        vtmpf = xs.enter_context(nc.sbuf_tensor(f"f{L}_vt", [128, 512], BF16))
        q2 = xs.enter_context(nc.sbuf_tensor(f"f{L}_q2", [128, 512], F32))
        sst = xs.enter_context(nc.sbuf_tensor(f"f{L}_sst", [1, 2, 512], F32))
        tbuf = xs.enter_context(nc.sbuf_tensor(f"f{L}_tb", [128, 2, 512], F32))
        ST0 = xs.enter_context(nc.psum_tensor(f"f{L}_ST0", [128, 1024], F32))
        ST1 = xs.enter_context(nc.psum_tensor(f"f{L}_ST1", [128, 1024], F32))
        AVA = xs.enter_context(nc.psum_tensor(f"f{L}_AVA", [128, 512], F32))
        AVB = xs.enter_context(nc.psum_tensor(f"f{L}_AVB", [128, 512], F32))
        RRA = xs.enter_context(nc.psum_tensor(f"f{L}_RRA", [128, 512], F32))
        RRB = xs.enter_context(nc.psum_tensor(f"f{L}_RRB", [128, 512], F32))

        STb = [ST0, ST1]
        pl = Plan()
        d_w, d_tab, d_src = S("fAw"), S("fAtab"), S("fAsrc")
        d_tlo, v_tlo = S("fAtlo"), S("fAvtlo")
        v_dt = S("fAdt")
        pe_mm, ev_q, s_ev = S("fAmm"), S("fAev"), S("fAsev")
        pe_rot, g_fin, v_fin = S("fArot"), S("fAgfin"), S("fAfin")
        pe_ssq, v_ssq, s_ssq = S("fApssq"), S("fAvssq"), S("fAsssq")
        pe_rep = S("fArep")
        pe_st, s_exp, v_msk, pe_av = S("fAst"), S("fAexp"), S("fAmsk"), S("fAav")
        v_rl, pe_rr, v_on = S("fArl"), S("fArr"), S("fAon")
        v_bnd, pe_bnd = S("fAbnd"), S("fApebnd")
        v_bfx = S("fAbfx")

        pl.step("gpsimd", lambda e: e.dma_start(wqf[:, :, :], dr["fwqkv"][:, :, :]), incs=[(d_w, 16)])
        for i in range(4):
            pl.step("sync", lambda e, i=i: e.dma_start(Dt[:, i, :], dr["tabs"][i][:, :]),
                    incs=[(d_tab, 16)])
        qkst = AP(qk, 0, [[2 * DOMQ, 128], [1, TAB_N]])
        for i in range(4):
            pl.step("gpsimd", lambda e, i=i: e.dma_start(qkst, dr["tabslo"][i][:, :]),
                    waits=pl.gate("tlo"), incs=[(d_tlo, 16)])
            pl.step("vector",
                    lambda e, i=i: e.tensor_tensor(Dt[:, i, :], Dt[:, i, :], qkst, ALU.add),
                    waits=w(d_tab) + w(d_tlo), incs=[(v_tlo, 1)])
            pl.set_gate("tlo", v_tlo)
        if L == 1:
            off0 = QT0 - 128 - X1T0
            pl.step("gpsimd",
                    lambda e: e.dma_start(srcbf[:, :, :],
                                          AP(dr["x1"], off0, [[X1N, 128], [128 * X1N, NCH], [1, DOMQ + 128]])),
                    incs=[(d_src, 16)])
            bcol = 2047 - (QT0 - 128)
            def _bfix(e, bcol=bcol):
                return e.tensor_scalar_mul(
                    AP(srcbf, bcol, [[(DOMQ + 128) * NCH, 128], [DOMQ + 128, NCH]]),
                    AP(srcbf, bcol, [[(DOMQ + 128) * NCH, 128], [DOMQ + 128, NCH]]),
                    C["bmask"][:, :])
            pl.step("vector", _bfix, waits=w(d_src), incs=[(v_bfx, 1)])

        def src_sl(c, t0, nn):
            sh = 1 if c >= 4 else 0
            if L == 0:
                return xn[:, c, t0 - sh:t0 - sh + nn]
            b0 = QT0 - 128
            return srcbf[:, c, t0 - sh - b0:t0 - sh - b0 + nn]

        def _dt(e):
            e.tensor_scalar_mul(Dt[:, 0, :], Dt[:, 0, :], C["qs"][:, :])
            e.tensor_scalar_mul(Dt[:, 1, :], Dt[:, 1, :], C["qssw"][:, :])
            e.tensor_scalar_mul(Dt[:, 2, :], Dt[:, 2, :], C["ks"][:, :])
            return e.tensor_scalar_mul(Dt[:, 3, :], Dt[:, 3, :], C["kssw"][:, :])
        pl.step("vector", _dt, waits=w(d_tab), incs=[(v_dt, 1)])
        pl.step("vector", lambda e: e.memset(AP(Vtf, 64, [[130 * NKC, 128], [130, NKC]]), 1.0),
                incs=[(v_on, 1)])
        pl.set_gate("fones", v_on)
        pl.step("vector", lambda e: e.reduce_max(bbf[0:1, 33:34], C["qsr"][:, :], axis=AX.X,
                                                 apply_absolute_value=True), incs=[(v_bnd, 1)])
        pl.step("vector", lambda e: e.reduce_max(bbf[0:1, 34:35], C["ksr"][:, :], axis=AX.X,
                                                 apply_absolute_value=True), incs=[(v_bnd, 1)])
        pl.step("vector", lambda e: e.tensor_tensor(bbf[0:1, 35:36], bbf[0:1, 33:34],
                                                    bbf[0:1, 34:35], ALU.mult),
                waits=w(v_bnd), incs=[(v_bnd, 1)])
        pl.step("tensor", lambda e: e.matmul(RRB[:, 0:1], C["onesf1"][:, :], bbf[0:1, 35:36],
                                             start=True, stop=True),
                waits=w(v_bnd), incs=[(pe_bnd, 1)])

        def _fixb(e):
            e.tensor_scalar(bbf[:, 36:37], RRB[:, 0:1], -QK_SCALE, 20.0, ALU.mult, ALU.add)
            return e.tensor_scalar_add(bbf[:, 0:32], C["logvf"][:, :], bbf[:, 36:37])
        pl.step("vector", _fixb, waits=w(pe_bnd), incs=[(v_bnd, 1)])

        grp = []
        t = QT0
        while t < T:
            grp.append((t, min(t + 512, T) - t))
            t = min(t + 512, T)

        for hg in range(4):
            qcol, kcol, vcol = hg * 128, FINNER + hg * 128, 2 * FINNER + hg * 128

            for qi, col in ((0, qcol), (1, kcol)):
                for gi, (t0, nn) in enumerate(grp):
                    ps = STb[gi % 2]

                    def _mm(e, t0=t0, nn=nn, col=col, ps=ps):
                        inst = None
                        for c in range(NCH):
                            inst = e.matmul(ps[:, 0:nn], wqf[:, c, col:col + 128],
                                            src_sl(c, t0, nn), start=(c == 0), stop=(c == NCH - 1))
                        return inst
                    wt = w(d_w) + pl.gate(("fST", gi % 2)) + pl.gate("fqkKV") + (w(d_src, v_bfx) if L == 1 else [])
                    pl.step("tensor", _mm, waits=wt, incs=[(pe_mm, 1)])
                    pl.step("scalar",
                            lambda e, qi=qi, t0=t0, nn=nn, ps=ps: e.copy(
                                qk[:, qi, t0 - QT0:t0 - QT0 + nn], ps[:, 0:nn]),
                            waits=w(pe_mm) + w(v_tlo), incs=[(ev_q, 1), (s_ev, 1)])
                    pl.set_gate(("fST", gi % 2), ev_q)

            for gi, (t0, nn) in enumerate(grp):
                ps = STb[gi % 2]
                av = [AVA, AVB][gi % 2]

                def _mmv(e, t0=t0, nn=nn, ps=ps, vcol=vcol):
                    inst = None
                    for c in range(NCH):
                        inst = e.matmul(ps[:, 0:nn], wqf[:, c, vcol:vcol + 128],
                                        src_sl(c, t0, nn), start=(c == 0), stop=(c == NCH - 1))
                    return inst
                pl.step("tensor", _mmv, waits=pl.gate(("fST", gi % 2)) + pl.gate("fqkKV") + pl.gate("fvt"),
                        incs=[(pe_mm, 1)])
                pl.step("scalar", lambda e, ps=ps, nn=nn: e.copy(vtmpf[:, 0:nn], ps[:, 0:nn]),
                        waits=w(pe_mm), incs=[(ev_q, 1)])
                pl.set_gate(("fST", gi % 2), ev_q)
                nt = nn // 128

                def _trv(e, nt=nt, av=av):
                    inst = None
                    for j in range(nt):
                        inst = e.matmul(av[:, 128 * j:128 * (j + 1)], vtmpf[:, 128 * j:128 * (j + 1)],
                                        C["idb"][:, :], start=True, stop=True)
                    return inst
                pl.step("tensor", _trv, waits=w(ev_q) + pl.gate(("fAV", gi % 2)) + pl.gate("fones"),
                        incs=[(pe_mm, 1)])
                pl.set_gate("fvt", pe_mm)

                def _evv(e, gi=gi, nt=nt, av=av):
                    inst = None
                    for j in range(nt):
                        base = (gi * 4 + j) * 130
                        inst = e.tensor_copy(AP(Vtf, base, [[130 * NKC, 128], [1, 64]]),
                                             av[:, 128 * j:128 * j + 64])
                        inst = e.tensor_copy(AP(Vtf, base + 65, [[130 * NKC, 128], [1, 64]]),
                                             av[:, 128 * j + 64:128 * j + 128])
                    return inst
                pl.step("vector", _evv, waits=w(pe_mm), incs=[(v_on, 1), (ev_q, 1)])
                pl.set_gate(("fAV", gi % 2), v_on)

            for qi in range(2):
                D1 = Dt[:, 0, :] if qi == 0 else Dt[:, 2, :]
                D2 = Dt[:, 1, :] if qi == 0 else Dt[:, 3, :]
                for gi, (t0, nn) in enumerate(grp):
                    o = t0 - QT0
                    to = t0 - TAB_T0
                    ps = STb[gi % 2]
                    pl.step("scalar",
                            lambda e, qi=qi, o=o, nn=nn: e.activation(
                                q2[:, 0:nn], qk[:, qi, o:o + nn], AF.Square),
                            waits=w(s_ev) + pl.gate("q2"), incs=[(s_ssq, 1)])

                    def _sm(e, o=o, nn=nn):
                        e.matmul(RRA[0:1, 0:nn], C["onescol"][0:64, :], q2[0:64, 0:nn],
                                 start=True, stop=True)
                        return e.matmul(RRB[0:1, 0:nn], C["onescol"][64:128, :], q2[64:128, 0:nn],
                                        start=True, stop=True)
                    pl.step("tensor", _sm, waits=w(s_ssq) + pl.gate("fRRA") + pl.gate("fRRB"),
                            incs=[(pe_ssq, 1)])
                    pl.set_gate("q2", pe_ssq)

                    def _cps(e, nn=nn):
                        e.tensor_copy(sst[0:1, 0, 0:nn], RRA[0:1, 0:nn])
                        return e.tensor_copy(sst[0:1, 1, 0:nn], RRB[0:1, 0:nn])
                    pl.step("vector", _cps, waits=w(pe_ssq) + pl.gate("sst"), incs=[(v_ssq, 1)])
                    pl.set_gate("fRRA", v_ssq)
                    sstf = sst.ap().rearrange("p a b -> p (a b)")
                    pl.step("vector", lambda e, sstf=sstf: e.tensor_scalar_add(
                        sstf[0:1, :], sstf[0:1, :], 1e-24),
                        waits=w(v_ssq), incs=[(v_ssq, 1)])
                    pl.step("scalar", lambda e, sstf=sstf: e.activation(
                        sstf[0:1, :], sstf[0:1, :], AF.Sqrt),
                        waits=w(v_ssq), incs=[(s_ssq, 1)])
                    pl.step("vector", lambda e, sstf=sstf: e.reciprocal(
                        sstf[0:1, :], sstf[0:1, :]),
                        waits=w(s_ssq), incs=[(v_ssq, 1)])

                    def _rep(e, nn=nn):
                        e.matmul(RRB[:, 0:nn], C["halfa"][:, :], sst[0:1, 0, 0:nn],
                                 start=True, stop=False)
                        return e.matmul(RRB[:, 0:nn], C["halfb"][:, :], sst[0:1, 1, 0:nn],
                                        start=False, stop=True)
                    pl.step("tensor", _rep, waits=w(v_ssq) + pl.gate("fRRB"), incs=[(pe_rep, 1)])
                    pl.step("tensor",
                            lambda e, qi=qi, o=o, nn=nn, ps=ps: e.matmul(
                                ps[:, 0:nn], C["r2sb"][:, :], qk[:, qi, o:o + nn],
                                start=True, stop=True),
                            waits=w(s_ssq) + pl.gate(("fST", gi % 2)), incs=[(pe_rot, 1)])
                    pl.step("gpsimd",
                            lambda e, qi=qi, o=o, nn=nn, D1=D1, to=to: e.tensor_tensor(
                                tbuf[:, 0, 0:nn], qk[:, qi, o:o + nn], D1[:, to:to + nn], ALU.mult),
                            waits=w(s_ssq, v_dt) + pl.gate("tb"), incs=[(g_fin, 1)])
                    pl.step("vector",
                            lambda e, nn=nn, ps=ps, D2=D2, to=to: e.tensor_tensor(
                                tbuf[:, 1, 0:nn], ps[:, 0:nn], D2[:, to:to + nn], ALU.mult),
                            waits=w(pe_rot, v_dt) + pl.gate("tb"), incs=[(v_fin, 1), (ev_q, 1)])
                    pl.set_gate(("fST", gi % 2), ev_q)

                    def _fin(e, qi=qi, o=o, nn=nn):
                        e.tensor_tensor(tbuf[:, 0, 0:nn], tbuf[:, 0, 0:nn], tbuf[:, 1, 0:nn], ALU.add)
                        return e.tensor_tensor(qk[:, qi, o:o + nn], tbuf[:, 0, 0:nn],
                                               RRB[:, 0:nn], ALU.mult)
                    pl.step("vector", _fin, waits=w(g_fin, pe_rep), incs=[(v_fin, 1)])
                    pl.set_gate("tb", v_fin)
                    pl.set_gate("sst", pe_rep)
                    pl.set_gate("fRRB", v_fin)

            for (qc0, qc1) in qgs:
                nq = (qc1 - qc0) * 128
                q0 = qc0 * 128 - QT0
                kclist = [kc for kc in range(qc0 - 4, qc1) if QC0 <= kc < NCHUNK]
                for ki, kc in enumerate(kclist):
                    bu = ki % 2
                    stp = STb[bu]
                    ko = (kc - QC0) * 128

                    def _st(e, ko=ko, q0=q0, nq=nq, stp=stp):
                        e.matmul(stp[:, 0:nq], qk[0:64, 1, ko:ko + 128],
                                 qk[0:64, 0, q0:q0 + nq], start=True, stop=True)
                        return e.matmul(stp[:, 512:512 + nq], qk[64:128, 1, ko:ko + 128],
                                        qk[64:128, 0, q0:q0 + nq], start=True, stop=True)
                    pl.step("tensor", _st, waits=w(v_fin, v_bnd) + pl.gate(("fST", bu)), incs=[(pe_st, 1)])

                    def _exp(e, bu=bu, kc=kc, nq=nq, stp=stp):
                        e.activation(PTf[:, bu, 0:nq], stp[:, 0:nq], AF.Exp,
                                     bias=bbf[:, kc:kc + 1], scale=QK_SCALE)
                        return e.activation(PTf[:, bu, 512:512 + nq], stp[:, 512:512 + nq], AF.Exp,
                                            bias=bbf[:, kc:kc + 1], scale=QK_SCALE)
                    pl.step("scalar", _exp, waits=w(pe_st) + pl.gate(("fPT", bu)),
                            incs=[(s_exp, 1)])
                    pl.set_gate(("fST", bu), s_exp)

                    acts = []
                    for i4 in range(qc1 - qc0):
                        qt = qc0 + i4
                        if kc == qt:
                            acts.append((i4, "tri"))
                        elif kc == qt - 4:
                            acts.append((i4, "anti"))
                        elif kc > qt or kc < qt - 4:
                            acts.append((i4, "zero"))
                    need_mask = bool(acts)
                    if need_mask:
                        def _mask(e, acts=tuple(acts), bu=bu):
                            inst = None
                            for h in range(2):
                                for i4, kind in acts:
                                    sl = PTf[:, bu, 512 * h + 128 * i4: 512 * h + 128 * (i4 + 1)]
                                    if kind == "tri":
                                        inst = e.tensor_tensor(sl, sl, C["trib"][:, :], ALU.mult)
                                    elif kind == "anti":
                                        inst = e.tensor_tensor(sl, sl, C["antib"][:, :], ALU.mult)
                                    else:
                                        inst = e.memset(sl, 0.0)
                            return inst
                        pl.step("vector", _mask, waits=w(s_exp), incs=[(v_msk, 1)])

                    def _av(e, kc=kc, bu=bu, nq=nq, first=(ki == 0), last=(ki == len(kclist) - 1)):
                        kb = (kc - QC0) * 130
                        e.matmul(AVA[0:65, 0:nq], Vtf[:, kb:kb + 65], PTf[:, bu, 0:nq],
                                 start=first, stop=last)
                        e.matmul(AVB[0:1, 0:nq], Vtf[:, kb + 64:kb + 65], PTf[:, bu, 512:512 + nq],
                                 start=first, stop=last)
                        return e.matmul(AVB[64:128, 0:nq], Vtf[:, kb + 65:kb + 129],
                                        PTf[:, bu, 512:512 + nq], start=first, stop=last)
                    avw = (w(v_msk) if need_mask else w(s_exp))
                    if ki == 0:
                        avw = avw + pl.gate(("fAV", 0)) + pl.gate(("fAV", 1))
                    pl.step("tensor", _av, waits=avw, incs=[(pe_av, 1)])
                    pl.set_gate(("fPT", bu), pe_av)

                def _rlf(e, nq=nq):
                    e.tensor_scalar_add(rlw[0:1, 0:nq], AVA[64:65, 0:nq], 1e-30)
                    e.reciprocal(rlw[0:1, 0:nq], rlw[0:1, 0:nq])
                    e.tensor_scalar_add(rlw[0:1, 512:512 + nq], AVB[0:1, 0:nq], 1e-30)
                    return e.reciprocal(rlw[0:1, 512:512 + nq], rlw[0:1, 512:512 + nq])
                pl.step("vector", _rlf, waits=w(pe_av) + pl.gate("rlw"), incs=[(v_rl, 1)])

                def _rrm(e, nq=nq):
                    e.matmul(RRA[:, 0:nq], C["onesf1"][:, :], rlw[0:1, 0:nq], start=True, stop=True)
                    return e.matmul(RRB[:, 0:nq], C["onesf1"][:, :], rlw[0:1, 512:512 + nq],
                                    start=True, stop=True)
                pl.step("tensor", _rrm, waits=w(v_rl) + pl.gate("fRRA") + pl.gate("fRRB"), incs=[(pe_rr, 1)])
                pl.step("vector", lambda e, nq=nq: e.tensor_copy(rlw[0:64, 0:nq], RRA[0:64, 0:nq]),
                        waits=w(pe_rr), incs=[(v_rl, 1)])
                pl.step("vector", lambda e, nq=nq: e.tensor_copy(rlw[64:128, 512:512 + nq], RRB[64:128, 0:nq]),
                        waits=w(pe_rr), incs=[(v_rl, 1)])
                pl.set_gate("fRRA", v_rl)
                pl.set_gate("fRRB", v_rl)

                def _norm(e, hg=hg, qc0=qc0, nq=nq):
                    o0 = hg * DOMA + (qc0 - AC0) * 128
                    e.tensor_tensor(off[0:64, o0:o0 + nq], AVA[0:64, 0:nq], rlw[0:64, 0:nq], ALU.mult)
                    return e.tensor_tensor(off[64:128, o0:o0 + nq], AVB[64:128, 0:nq],
                                           rlw[64:128, 512:512 + nq], ALU.mult)
                pl.step("vector", _norm, waits=w(v_rl), incs=[(v_on, 1), (v_rl, 1)])
                pl.set_gate(("fAV", 0), v_on)
                pl.set_gate(("fAV", 1), v_on)
                pl.set_gate("rlw", v_on)
            pl.G["fqkKV"] = w(pe_st) + w(pe_av)

        with nc.Block() as blk:
            pl.emit(nc, blk)


def _ffn_mlp(nc, S, L, xn, C, dr, off, AC0, DOMA, AT0, qgs):
    with ExitStack() as xs:
        wos = xs.enter_context(nc.sbuf_tensor(f"m{L}_wo", [128, 4, D], BF16))
        fb1 = xs.enter_context(nc.sbuf_tensor(f"m{L}_fb1", [1, H2P], BF16))
        fb2 = xs.enter_context(nc.sbuf_tensor(f"m{L}_fb2", [1, D], BF16))
        w1a = xs.enter_context(nc.sbuf_tensor(f"m{L}_w1a", [128, 2, NCH, 256], BF16))
        w1g = xs.enter_context(nc.sbuf_tensor(f"m{L}_w1g", [128, 2, NCH, 256], BF16))
        w2s = xs.enter_context(nc.sbuf_tensor(f"m{L}_w2", [128, 2, 2, D], BF16))
        xmid = xs.enter_context(nc.sbuf_tensor(f"m{L}_xm", [128, NCH, 513], F32))
        hln = xs.enter_context(nc.sbuf_tensor(f"m{L}_h", [128, NCH, 512], BF16))
        gel = xs.enter_context(nc.sbuf_tensor(f"m{L}_g", [128, 512], BF16))
        x1g = xs.enter_context(nc.sbuf_tensor(f"m{L}_x1", [128, NCH, 512], F32))
        xsq = xs.enter_context(nc.sbuf_tensor(f"m{L}_xsq", [128, 1, 512], F32))
        mbuf = xs.enter_context(nc.sbuf_tensor(f"m{L}_m", [128, 22 * 512], BF16))
        lnst = xs.enter_context(nc.sbuf_tensor(f"m{L}_ln", [1, 1024], F32))
        baseb = xacc = None
        if L == 1:
            baseb = xs.enter_context(nc.sbuf_tensor(f"m{L}_base", [128, 2, NCH, 512], F32))
            xacc = xs.enter_context(nc.sbuf_tensor(f"m{L}_xacc", [128, 2, 2 * D], F32))
        ST0 = xs.enter_context(nc.psum_tensor(f"m{L}_ST0", [128, 1024], F32))
        ST1 = xs.enter_context(nc.psum_tensor(f"m{L}_ST1", [128, 1024], F32))
        RRA = xs.enter_context(nc.psum_tensor(f"m{L}_RRA", [128, 512], F32))
        RRB = xs.enter_context(nc.psum_tensor(f"m{L}_RRB", [128, 512], F32))
        TPp = xs.enter_context(nc.psum_tensor(f"m{L}_TP", [128, 1024], F32))

        STb = [ST0, ST1]
        w2slot = [ST0[:, 0:512], ST0[:, 512:1024], ST1[:, 0:512], ST1[:, 512:1024],
                  TPp[:, 0:512], TPp[:, 512:1024], RRA[:, 0:512], RRB[:, 0:512]]
        pl = Plan()
        d_wo = S("mBwo")
        d_w1 = [S("mBw1a"), S("mBw1b")]
        d_w2 = [S("mBw2a"), S("mBw2b")]
        d_base = [S("mBbase0"), S("mBbase1")]
        d_xin = [S("mBxin0"), S("mBxin1")]
        d_oo = [S("mBoo0"), S("mBoo1")]
        d_x1o = S("mBx1o")
        pe_wo, v_xm = S("mBpewo"), S("mBvxm")
        s_sq2, pe_stat = S("mBssq"), S("mBstat")
        v_ln, s_ln = S("mBvln"), S("mBsln")
        pe_rep2, v_u, s_hln = S("mBrep"), S("mBu"), S("mBhln")
        pe_w1, s_gel, v_m = S("mBw1"), S("mBgel"), S("mBm")
        pe_w2, v_x1 = S("mBw2"), S("mBx1")
        pe_tp, v_fin2 = S("mBtp"), S("mBfin")
        v_bfx2 = S("mBbfx")

        pl.step("gpsimd", lambda e: e.dma_start(wos[:, :, :], dr["fwo"][:, :, :]), incs=[(d_wo, 16)])
        pl.step("gpsimd", lambda e: e.dma_start(fb1[:, :], dr["fb1"][:, :]), incs=[(d_wo, 16)])
        pl.step("gpsimd", lambda e: e.dma_start(fb2[:, :], dr["fb2"][:, :]), incs=[(d_wo, 16)])

        w1blocks = [(j * 256, (j + 1) * 256) for j in range(11)]

        OUT0 = 2048

        for g, (gc0, gc1) in enumerate(qgs):
            W = (gc1 - gc0) * 128
            t0 = gc0 * 128
            gb = g % 2

            if g == 0:
                pl.step("vector", lambda e: e.memset(AP(xmid, 0, [[513 * NCH, 128], [513, NCH]]), 0.0),
                        waits=pl.gate("xmid"), incs=[(v_xm, 1)])
            else:
                Wp = Wprev
                pl.step("vector",
                        lambda e, Wp=Wp: e.tensor_copy(
                            AP(xmid, 0, [[513 * NCH, 128], [513, NCH]]),
                            AP(xmid, Wp, [[513 * NCH, 128], [513, NCH]])),
                        waits=pl.gate("xmid"), incs=[(v_xm, 1)])
            Wprev = W

            if L == 1:
                pl.step("sync",
                        lambda e, t0=t0, gb=gb, W=W: e.dma_start(
                            baseb[:, gb, :, 0:W],
                            AP(dr["x1"], t0 - X1T0, [[X1N, 128], [128 * X1N, NCH], [1, W]])),
                        waits=pl.gate(("base", gb)), incs=[(d_base[gb], 16)])
                if t0 <= 2047 < t0 + W:
                    bc = 2047 - t0
                    def _bfix2(e, gb=gb, bc=bc):
                        return e.tensor_scalar_mul(baseb[:, gb, :, bc:bc + 1],
                                                   baseb[:, gb, :, bc:bc + 1],
                                                   C["bmask"][:, :])
                    pl.step("vector", _bfix2, waits=w(d_base[gb]), incs=[(v_bfx2, 1)])

            for oc in range(NCH):
                ps = STb[oc % 2]

                def _womm(e, oc=oc, t0=t0, W=W, ps=ps):
                    inst = None
                    for c in range(4):
                        inst = e.matmul(ps[:, 0:W], wos[:, c, 128 * oc:128 * (oc + 1)],
                                        off[:, c * DOMA + (t0 - AT0): c * DOMA + (t0 - AT0) + W],
                                        start=(c == 0), stop=(c == 3))
                    return inst
                pl.step("tensor", _womm, waits=w(d_wo) + pl.gate(("mST", oc % 2)),
                        incs=[(pe_wo, 1)])

                def _xm(e, oc=oc, t0=t0, gb=gb, W=W, ps=ps):
                    if L == 0:
                        return e.tensor_tensor(xmid[:, oc, 1:W + 1], ps[:, 0:W], xn[:, oc, t0:t0 + W], ALU.add)
                    return e.tensor_tensor(xmid[:, oc, 1:W + 1], ps[:, 0:W], baseb[:, gb, oc, 0:W], ALU.add)
                pl.step("vector", _xm, waits=w(pe_wo) + pl.gate("xmid") + (w(d_base[gb], v_bfx2) if L == 1 else []),
                        incs=[(v_xm, 1)])
                pl.set_gate(("mST", oc % 2), v_xm)

            def shv(oc, W=W):
                sh = 1 if oc < 4 else 0
                return xmid[:, oc, sh:sh + W]

            def _mu(e, W=W):
                inst = None
                for oc in range(NCH):
                    inst = e.matmul(RRA[0:1, 0:W], C["onescol"][:, :], shv(oc, W),
                                    start=(oc == 0), stop=(oc == NCH - 1))
                return inst
            pl.step("tensor", _mu, waits=w(v_xm) + pl.gate("mRRA"), incs=[(pe_stat, 1)])
            for oc in range(NCH):
                pl.step("scalar",
                        lambda e, oc=oc, W=W: e.activation(xsq[:, 0, 0:W], shv(oc, W), AF.Square),
                        waits=w(v_xm) + w(pe_stat), incs=[(s_sq2, 1)])
                pl.step("tensor",
                        lambda e, oc=oc, W=W: e.matmul(RRB[0:1, 0:W], C["onescol"][:, :],
                                                       xsq[:, 0, 0:W],
                                                       start=(oc == 0), stop=(oc == NCH - 1)),
                        waits=w(s_sq2) + (pl.gate("mRRB") if oc == 0 else []), incs=[(pe_stat, 1)])

            def _lnv(e, W=W):
                e.tensor_scalar_mul(lnst[:, 0:W], RRA[0:1, 0:W], 1.0 / D)
                e.tensor_scalar_mul(lnst[:, 512:512 + W], RRB[0:1, 0:W], 1.0 / D)
                e.tensor_tensor(xsq[0:1, 0, 0:W], lnst[:, 0:W], lnst[:, 0:W], ALU.mult)
                e.tensor_tensor(lnst[:, 512:512 + W], lnst[:, 512:512 + W], xsq[0:1, 0, 0:W], ALU.subtract)
                return e.tensor_scalar_add(lnst[:, 512:512 + W], lnst[:, 512:512 + W], 1e-5)
            pl.step("vector", _lnv, waits=w(pe_stat) + pl.gate("lnst"), incs=[(v_ln, 1)])
            pl.step("scalar", lambda e, W=W: e.activation(lnst[:, 512:512 + W], lnst[:, 512:512 + W], AF.Sqrt),
                    waits=w(v_ln), incs=[(s_ln, 1)])

            def _lnr(e, W=W):
                e.reciprocal(lnst[:, 512:512 + W], lnst[:, 512:512 + W])
                return e.tensor_tensor(lnst[:, 0:W], lnst[:, 0:W], lnst[:, 512:512 + W], ALU.mult)
            pl.step("vector", _lnr, waits=w(s_ln), incs=[(v_ln, 1)])

            def _repln(e, W=W):
                e.matmul(RRA[:, 0:W], C["onesf1"][:, :], lnst[0:1, 512:512 + W], start=True, stop=True)
                return e.matmul(RRB[:, 0:W], C["onesf1"][:, :], lnst[0:1, 0:W], start=True, stop=True)
            pl.step("tensor", _repln, waits=w(v_ln), incs=[(pe_rep2, 1)])
            pl.set_gate("lnst", pe_rep2)

            for oc in range(NCH):
                def _u(e, oc=oc, W=W):
                    e.tensor_tensor(x1g[:, oc, 0:W], shv(oc, W), RRA[:, 0:W], ALU.mult)
                    return e.tensor_tensor(x1g[:, oc, 0:W], x1g[:, oc, 0:W], RRB[:, 0:W], ALU.subtract)
                pl.step("vector", _u, waits=w(pe_rep2) + pl.gate("x1g"),
                        incs=[(v_u, 1)])
                pl.step("scalar",
                        lambda e, oc=oc, W=W: e.activation(hln[:, oc, 0:W], x1g[:, oc, 0:W], AF.Identity,
                                                           bias=C["lnb"][:, oc:oc + 1],
                                                           scale=C["lng"][:, oc:oc + 1]),
                        waits=w(v_u) + pl.gate("hln"), incs=[(s_hln, 1)])

            for j, (c0, c1) in enumerate(w1blocks):
                jb = j % 2
                pl.step("gpsimd",
                        lambda e, c0=c0, jb=jb: e.dma_start(
                            w1a[:, jb, :, :],
                            AP(dr["fw1"], c0, [[NCH * H2P, 128], [H2P, NCH], [1, 256]])),
                        waits=pl.gate(("w1", jb)), incs=[(d_w1[jb], 16)])
                pl.step("gpsimd",
                        lambda e, c0=c0, jb=jb: e.dma_start(
                            w1g[:, jb, :, :],
                            AP(dr["fw1"], HIDP + c0, [[NCH * H2P, 128], [H2P, NCH], [1, 256]])),
                        waits=[], incs=[(d_w1[jb], 16)])
                for jj in range(2):
                    mc = (c0 + 128 * jj) // 128

                    def _gmm(e, jj=jj, jb=jb, mc=mc, W=W):
                        for c in range(NCH):
                            e.matmul(ST0[:, 0:W], w1g[:, jb, c, 128 * jj:128 * (jj + 1)],
                                     hln[:, c, 0:W], start=(c == 0), stop=False)
                        return e.matmul(ST0[:, 0:W], fb1[:, HIDP + 128 * mc:HIDP + 128 * mc + 128],
                                        C["onesrow"][:, 0:W], start=False, stop=True)
                    pl.step("tensor", _gmm,
                            waits=w(s_hln) + w(d_w1[jb]) + pl.gate(("mST", 0)),
                            incs=[(pe_w1, 1)])
                    pl.step("scalar", lambda e, W=W: e.activation(gel[:, 0:W], ST0[:, 0:W], AF.Gelu),
                            waits=w(pe_w1) + pl.gate("gel"), incs=[(s_gel, 1)])
                    pl.set_gate(("mST", 0), s_gel)

                    def _amm(e, jj=jj, jb=jb, mc=mc, W=W):
                        for c in range(NCH):
                            e.matmul(ST1[:, 0:W], w1a[:, jb, c, 128 * jj:128 * (jj + 1)],
                                     hln[:, c, 0:W], start=(c == 0), stop=False)
                        return e.matmul(ST1[:, 0:W], fb1[:, 128 * mc:128 * mc + 128],
                                        C["onesrow"][:, 0:W], start=False, stop=True)
                    pl.step("tensor", _amm,
                            waits=w(s_hln) + w(d_w1[jb]) + pl.gate(("mST", 1)),
                            incs=[(pe_w1, 1)])
                    pl.step("vector",
                            lambda e, mc=mc, W=W: e.tensor_tensor(
                                mbuf[:, 512 * mc:512 * mc + W], ST1[:, 0:W], gel[:, 0:W], ALU.mult),
                            waits=w(pe_w1, s_gel) + pl.gate("mbuf"), incs=[(v_m, 1)])
                    pl.set_gate(("mST", 1), v_m)
                    pl.set_gate("gel", v_m)
                pl.set_gate(("w1", jb), pe_w1)

            for mc in range(22):
                blk = mc // 2
                jb = blk % 2
                rr = mc % 2
                if rr == 0:
                    pl.step("gpsimd",
                            lambda e, blk=blk, jb=jb: e.dma_start(
                                w2s[:, jb, :, :],
                                AP(dr["fw2"], blk * 2 * D, [[(HIDP // 128) * D, 128], [D, 2], [1, D]])),
                            waits=pl.gate(("w2", jb)), incs=[(d_w2[jb], 16)])

                def _w2mm(e, mc=mc, jb=jb, rr=rr, W=W, first=(mc == 0)):
                    inst = None
                    for oc in range(NCH):
                        inst = e.matmul(w2slot[oc][:, 0:W] if W != 512 else w2slot[oc],
                                        w2s[:, jb, rr, 128 * oc:128 * (oc + 1)],
                                        mbuf[:, 512 * mc:512 * mc + W],
                                        start=first, stop=False)
                    return inst
                extra = (pl.gate(("mST", 0)) + pl.gate(("mST", 1)) + pl.gate("mRRA")
                         + pl.gate("mRRB") + pl.gate("mTP")) if mc == 0 else []
                pl.step("tensor", _w2mm, waits=w(v_m) + w(d_w2[jb]) + extra, incs=[(pe_w2, 1)])
                if rr == 1:
                    pl.set_gate(("w2", jb), pe_w2)

            def _w2bias(e, W=W):
                inst = None
                for oc in range(NCH):
                    inst = e.matmul(w2slot[oc][:, 0:W] if W != 512 else w2slot[oc],
                                    fb2[:, 128 * oc:128 * (oc + 1)],
                                    C["onesrow"][:, 0:W], start=False, stop=True)
                return inst
            pl.step("tensor", _w2bias, waits=[], incs=[(pe_w2, 1)])

            for oc in range(NCH):
                pl.step("vector",
                        lambda e, oc=oc, W=W: e.tensor_tensor(
                            x1g[:, oc, 0:W], xmid[:, oc, 1:W + 1], w2slot[oc][:, 0:W], ALU.add),
                        waits=w(pe_w2, s_hln), incs=[(v_x1, 1)])
            pl.set_gate(("mST", 0), v_x1)
            pl.set_gate(("mST", 1), v_x1)
            pl.set_gate("mTP", v_x1)
            pl.set_gate("mRRA", v_x1)
            pl.set_gate("mRRB", v_x1)
            pl.set_gate("xmid", v_x1)
            pl.set_gate("mbuf", pe_w2)

            if L == 0:
                pl.step("sync",
                        lambda e, t0=t0, W=W: e.dma_start(
                            AP(dr["x1"], t0 - X1T0, [[X1N, 128], [128 * X1N, NCH], [1, W]]),
                            x1g[:, :, 0:W]),
                        waits=w(v_x1), incs=[(d_x1o, 16)])
                pl.set_gate("x1g", d_x1o)
                pl.G["x1g"] = pl.G["x1g"] + w(s_hln)
            else:
                ntt = W // 128
                for tt in range(ntt):
                    tg = t0 + 128 * tt
                    if tg < OUT0:
                        continue
                    b2 = tt % 2
                    pl.step("sync",
                            lambda e, tg=tg, b2=b2: e.dma_start(xacc[:, b2, 0:D], dr["x"][tg:tg + 128, :]),
                            waits=pl.gate(("xacc", b2)), incs=[(d_xin[b2], 16)])
                    pl.step("sync",
                            lambda e, tg=tg, b2=b2: e.dma_start(xacc[:, b2, D:2 * D],
                                                                dr["acc"][tg - OWN:tg - OWN + 128, :]),
                            waits=[], incs=[(d_xin[b2], 16)])

                    def _tp(e, tt=tt):
                        inst = None
                        for oc in range(NCH):
                            inst = e.matmul(TPp[:, 128 * oc:128 * (oc + 1)],
                                            x1g[:, oc, 128 * tt:128 * (tt + 1)], C["idf"][:, :],
                                            start=True, stop=True)
                        return inst
                    pl.step("tensor", _tp, waits=w(v_x1) + pl.gate("mTP"), incs=[(pe_tp, 1)])
                    pl.set_gate("x1g", pe_tp)
                    pl.G["x1g"] = pl.G["x1g"] + w(s_hln)

                    def _fin(e, b2=b2):
                        e.tensor_tensor(xacc[:, b2, 0:D], xacc[:, b2, 0:D], TPp[:, :], ALU.add)
                        return e.tensor_tensor(xacc[:, b2, 0:D], xacc[:, b2, 0:D],
                                               xacc[:, b2, D:2 * D], ALU.add)
                    pl.step("vector", _fin, waits=w(pe_tp) + w(d_xin[b2]), incs=[(v_fin2, 1)])
                    pl.set_gate("mTP", v_fin2)
                    pl.step("sync",
                            lambda e, tg=tg, b2=b2: e.dma_start(dr["out"][tg - OWN:tg - OWN + 128, :],
                                                                xacc[:, b2, 0:D]),
                            waits=w(v_fin2), incs=[(d_oo[b2], 16)])
                    pl.set_gate(("xacc", b2), d_oo[b2])

        if L == 0:
            pl.step("sync", lambda e: e.nop(), waits=w(d_x1o))
        else:
            pl.step("sync", lambda e: e.nop(), waits=w(d_oo[0]) + w(d_oo[1]))

        with nc.Block() as blk:
            pl.emit(nc, blk)


_NC_CACHE = {}


def kernel(L0_attn_x, L0_attn_w_qkv, L0_attn_b_qkv, L0_attn_w_proj, L0_attn_b_proj,
           L0_ffn_wqkv, L0_ffn_q_scale, L0_ffn_k_scale, L0_ffn_wo, L0_ffn_ln_g,
           L0_ffn_ln_b, L0_ffn_w1, L0_ffn_b1, L0_ffn_w2, L0_ffn_b2):
    x = np.ascontiguousarray(np.asarray(L0_attn_x, dtype=np.float32))
    consts = _host_consts()

    shared = dict(
        wqkv_a=_w_fm(np.asarray(L0_attn_w_qkv, np.float32)),
        bqkv_a=np.asarray(L0_attn_b_qkv, np.float32).reshape(1, 3 * D),
        wproj=_w_fm(np.asarray(L0_attn_w_proj, np.float32)),
        bproj=np.asarray(L0_attn_b_proj, np.float32).reshape(1, D),
        bmask=np.ones((128, 1), np.float32),
        logva=np.zeros((128, 32), np.float32),
        logvf=np.zeros((128, 32), np.float32),
        **consts,
    )
    for l in range(2):
        shared[f"fwqkv{l}"] = _w_fm(np.asarray(L0_ffn_wqkv[l], np.float32))
        shared[f"fwo{l}"] = _w_fm(np.asarray(L0_ffn_wo[l], np.float32))
        w1 = np.asarray(L0_ffn_w1[l], np.float32)
        w1p = np.zeros((D, H2P), np.float32)
        w1p[:, 0:HID] = w1[:, 0:HID]
        w1p[:, HIDP:HIDP + HID] = w1[:, HID:2 * HID]
        shared[f"fw1{l}"] = _w_fm(w1p)
        b1 = np.asarray(L0_ffn_b1[l], np.float32)
        b1p = np.zeros((1, H2P), np.float32)
        b1p[0, 0:HID] = b1[0:HID]
        b1p[0, HIDP:HIDP + HID] = b1[HID:2 * HID]
        shared[f"fb1{l}"] = b1p
        w2 = np.asarray(L0_ffn_w2[l], np.float32)
        w2p = np.zeros((HIDP, D), np.float32)
        w2p[0:HID, :] = w2
        shared[f"fw2{l}"] = _w_fm(w2p)
        shared[f"fb2{l}"] = np.asarray(L0_ffn_b2[l], np.float32).reshape(1, D)
        shared[f"lng{l}"] = np.ascontiguousarray(np.asarray(L0_ffn_ln_g[l], np.float32).reshape(NCH, 128).T)
        shared[f"lnb{l}"] = np.ascontiguousarray(np.asarray(L0_ffn_ln_b[l], np.float32).reshape(NCH, 128).T)
        qv = np.asarray(L0_ffn_q_scale[l], np.float32)
        kv = np.asarray(L0_ffn_k_scale[l], np.float32)
        shared[f"qs{l}"] = np.tile(qv, 2).reshape(128, 1)
        shared[f"qssw{l}"] = np.tile(np.concatenate([qv[32:64], qv[0:32]]), 2).reshape(128, 1)
        shared[f"ks{l}"] = np.tile(kv, 2).reshape(128, 1)
        shared[f"kssw{l}"] = np.tile(np.concatenate([kv[32:64], kv[0:32]]), 2).reshape(128, 1)
        shared[f"qsr{l}"] = np.tile(qv, 2).reshape(1, 128)
        shared[f"ksr{l}"] = np.tile(kv, 2).reshape(1, 128)

    if "nc" not in _NC_CACHE:
        _NC_CACHE["nc"] = build_program()
    nc = _NC_CACHE["nc"]

    in_maps = []
    for c in range(8):
        seq, half = c // 2, c % 2
        if half == 1:
            xc = x[seq]
        else:
            xc = np.concatenate([np.zeros((OWN, D), np.float32), x[seq, 0:OWN]], axis=0)
        real0 = TAB_T0 - (0 if half == 1 else OWN)
        tabs = _rotary_tables(np.arange(real0, real0 + TAB_N))
        lv_a = np.zeros((128, 32), np.float32)
        lv_f = np.zeros((128, 32), np.float32)
        if half == 0:
            lv_a[:, 0:16] = NEG
            lv_f[:, 0:16] = NEG
        m = dict(shared)
        m["x_ctx"] = xc
        m["bmask"] = np.full((128, 1), 0.0 if half == 0 else 1.0, np.float32)
        m["logva"] = lv_a
        m["logvf"] = lv_f
        for i in range(4):
            m[f"tab{i}"] = tabs[i]
            m[f"tab{i}lo"] = tabs[4 + i]
        in_maps.append(m)

    res = run_bass_kernel_spmd(nc, in_maps, core_ids=list(range(8)))
    out = np.empty((B, T, D), np.float32)
    for c in range(8):
        seq, half = c // 2, c % 2
        out[seq, half * OWN:(half + 1) * OWN] = res.results[c]["out_own"]
    return out


# version 15
# speedup vs baseline: 1.0129x; 1.0129x over previous
import numpy as np
from contextlib import ExitStack

import concourse.bass as bass
import concourse.mybir as mybir
from concourse.bass_utils import run_bass_kernel_spmd

F32 = mybir.dt.float32
BF16 = mybir.dt.bfloat16
AF = mybir.ActivationFunctionType
ALU = mybir.AluOpType
AX = mybir.AxisListType
AP = bass.AP

B, T, D = 4, 4096, 1024
OWN = 2048
NCH = 8
NHG = 8
FINNER = 512
HID = 2730
HIDP = 2816
H2P = 2 * HIDP
QK_SCALE = 8.0
L1_QKV_C0, L1_ATT_C0 = 6, 10
L2_QKV_C0, L2_ATT_C0 = 11, 15
NCHUNK = 32
TAB_T0 = L1_QKV_C0 * 128
TAB_N = T - TAB_T0
X1T0 = L1_ATT_C0 * 128
X1N = T - X1T0
NEG = -60000.0


class PSem:
    def __init__(self, ctx, nc, name):
        self.h = ctx.enter_context(nc.semaphore(name))
        self.n = 0
        self.name = name
        self.producers = set()


class Plan:
    def __init__(self):
        self.steps = []
        self.G = {}

    def gate(self, *keys):
        out = []
        for k in keys:
            out += self.G.get(k, [])
        return out

    def set_gate(self, key, sem):
        self.G[key] = [(sem, sem.n)]

    def step(self, eng, emit_fn, waits=(), incs=()):
        self.steps.append((eng, [(s, v) for (s, v) in waits if v > 0], emit_fn, list(incs)))
        for s, k in incs:
            s.n += k
            s.producers.add(eng)

    def emit(self, nc, block):
        by_eng = {}
        for eng, ws, fn, incs in self.steps:
            by_eng.setdefault(eng, []).append((ws, fn, incs))

        def mk_body(items, ename):
            def body(e):
                ee = _DrainProxy(e) if ename == "vector" else e
                last = {}
                for ws, fn, incs in items:
                    for s, v in ws:
                        if s.producers == {ename} and ename != "vector":
                            continue
                        if last.get(s.name, -1) >= v:
                            continue
                        e.wait_ge(s.h, v)
                        last[s.name] = v
                    inst = fn(ee)
                    for ii, (s, k) in enumerate(incs):
                        if ii == 0:
                            inst.then_inc(s.h, k)
                        else:
                            e.nop().then_inc(s.h, k)
            return body

        emap = {"sync": mybir.EngineType.SP, "gpsimd": mybir.EngineType.Pool,
                "tensor": mybir.EngineType.PE, "scalar": mybir.EngineType.Activation,
                "vector": mybir.EngineType.DVE}
        for ename in ("sync", "gpsimd", "tensor", "scalar", "vector"):
            if ename in by_eng:
                block._start_engine(mk_body(by_eng[ename], ename), emap[ename])


class _DrainProxy:

    _OPS = {"tensor_copy", "tensor_tensor", "tensor_add", "tensor_sub", "tensor_mul",
            "tensor_max", "tensor_scalar", "tensor_single_scalar", "tensor_scalar_add",
            "tensor_scalar_sub", "tensor_scalar_mul", "tensor_scalar_max",
            "tensor_scalar_min", "tensor_relu", "tensor_reduce", "reduce_max",
            "reduce_sum", "reciprocal", "memset", "memzero", "select", "transpose",
            "bn_stats", "bn_aggr", "stream_shuffle", "pool", "pool_avg", "pool_max"}

    def __init__(self, e):
        self._e = e

    def __getattr__(self, name):
        f = getattr(self._e, name)
        if name in _DrainProxy._OPS:
            def g(*a, **k):
                self._e.drain()
                return f(*a, **k)
            return g
        return f


def w(*sems):
    return [(s, s.n) for s in sems]


def wb(s, k):
    return [(s, max(0, s.n - k))]


def _rotary_tables(real_pos):
    tf = real_pos.astype(np.float32)
    try:
        import jax
        import jax.numpy as jnp
        with jax.default_device(jax.devices("cpu")[0]):
            half_idx = jnp.arange(0, 64, 2, dtype=jnp.float32)
            inv_freq = 1.0 / 10000.0 ** (half_idx / 64)
            t = jnp.asarray(tf)
            freqs = t[:, None] * inv_freq[None, :]
            base = (half_idx + 0.4 * 64) / (1.4 * 64)
            power = (t - T // 2) / float(512 // 2)
            scale = base[None, :] ** power[:, None]
            cos, sin = jnp.cos(freqs), jnp.sin(freqs)
            cos, sin, scale = (np.asarray(a, np.float64) for a in (cos, sin, scale))
    except Exception:
        half_idx = np.arange(0, 64, 2, dtype=np.float32)
        inv_freq = np.float32(1.0) / np.power(np.float32(10000.0), half_idx / np.float32(64.0))
        freqs = tf[:, None] * inv_freq[None, :]
        base = (half_idx + np.float32(0.4 * 64)) / np.float32(1.4 * 64)
        power = (tf - np.float32(T // 2)) / np.float32(512 // 2)
        scale = np.power(base[None, :], power[:, None], dtype=np.float32)
        cos, sin, scale = (a.astype(np.float64) for a in (np.cos(freqs), np.sin(freqs), scale))
    cos = np.concatenate([cos, cos], axis=1)
    sin = np.concatenate([sin, sin], axis=1)
    scale = np.concatenate([scale, scale], axis=1)
    his, los = [], []
    for a in (cos * scale, sin * scale, cos / scale, sin / scale):
        a = np.concatenate([a.T, a.T], axis=0)
        hi = a.astype(np.float32)
        lo = (a - hi.astype(np.float64)).astype(np.float32)
        his.append(np.ascontiguousarray(hi))
        los.append(np.ascontiguousarray(lo))
    return his + los


def _host_consts():
    c = {}
    c["identc"] = np.eye(128, dtype=np.float32)
    r = np.zeros((128, 128), dtype=np.float32)
    for blk in range(2):
        o = 64 * blk
        for j in range(32):
            r[o + j + 32, o + j] = -1.0
            r[o + j, o + j + 32] = 1.0
    c["r2sc"] = r
    kk, qq = np.meshgrid(np.arange(128), np.arange(128), indexing="ij")
    c["tric"] = (qq >= kk).astype(np.float32)
    c["antic"] = (kk >= qq).astype(np.float32)
    ha = np.zeros((1, 128), np.float32)
    ha[0, 0:64] = 1.0
    hb = np.zeros((1, 128), np.float32)
    hb[0, 64:128] = 1.0
    c["halfac"] = ha
    c["halfbc"] = hb
    return c


def _w_fm(wmat):
    K, N = wmat.shape
    return np.ascontiguousarray(wmat.reshape(K // 128, 128, N).transpose(1, 0, 2))


def build_program():
    nc = bass.Bass("TRN2", target_bir_lowering=False, detect_race_conditions=False)
    es = ExitStack()

    dram = {}

    def din(name, shape):
        dram[name] = nc.dram_tensor(name, shape, F32, kind="ExternalInput")

    din("x_ctx", [T, D])
    din("wqkv_a", [128, NCH, 3 * D])
    din("bqkv_a", [1, 3 * D])
    din("wproj", [128, NCH, D])
    din("bproj", [1, D])
    for l in range(2):
        din(f"fwqkv{l}", [128, NCH, 3 * FINNER])
        din(f"fwo{l}", [128, 4, D])
        din(f"fw1{l}", [128, NCH, H2P])
        din(f"fb1{l}", [1, H2P])
        din(f"fw2{l}", [128, HIDP // 128, D])
        din(f"fb2{l}", [1, D])
        din(f"lng{l}", [128, NCH])
        din(f"lnb{l}", [128, NCH])
        din(f"qs{l}", [128, 1])
        din(f"qssw{l}", [128, 1])
        din(f"ks{l}", [128, 1])
        din(f"kssw{l}", [128, 1])
        din(f"qsr{l}", [1, 128])
        din(f"ksr{l}", [1, 128])
    for i in range(4):
        din(f"tab{i}", [128, TAB_N])
        din(f"tab{i}lo", [128, TAB_N])
    for n in ("identc", "r2sc", "tric", "antic"):
        din(n, [128, 128])
    din("halfac", [1, 128])
    din("halfbc", [1, 128])
    din("bmask", [128, 1])
    din("logva", [128, 32])
    din("logvf", [128, 32])

    d_out = nc.dram_tensor("out_own", [OWN, D], F32, kind="ExternalOutput")
    d_acc = nc.dram_tensor("acc_scr", [OWN, D], F32, kind="Internal")
    d_x1 = nc.dram_tensor("x1_scr", [NCH, 128, X1N], F32, kind="Internal")

    sb = lambda name, shape, dt: es.enter_context(nc.sbuf_tensor(name, shape, dt))

    C = {}
    C["idb"] = sb("idb", [128, 128], BF16)
    C["idf"] = sb("idf", [128, 128], F32)
    C["r2sb"] = sb("r2sb", [128, 128], BF16)
    C["trib"] = sb("trib", [128, 128], BF16)
    C["antib"] = sb("antib", [128, 128], BF16)
    C["halfa"] = sb("halfa", [1, 128], F32)
    C["halfb"] = sb("halfb", [1, 128], F32)
    C["bmask"] = sb("bmask_s", [128, 1], F32)
    C["logva"] = sb("logva_s", [128, 32], F32)
    C["logvf"] = sb("logvf_s", [128, 32], F32)
    C["onesrow"] = sb("onesrow", [1, 512], BF16)
    C["onesf1"] = sb("onesf1", [1, 128], F32)
    C["onescol"] = sb("onescol", [128, 1], F32)
    per_layer = {}
    for l in range(2):
        pc = {}
        pc["lng"] = sb(f"lng_s{l}", [128, NCH], F32)
        pc["lnb"] = sb(f"lnb_s{l}", [128, NCH], F32)
        pc["qs"] = sb(f"qs_s{l}", [128, 1], F32)
        pc["qssw"] = sb(f"qssw_s{l}", [128, 1], F32)
        pc["ks"] = sb(f"ks_s{l}", [128, 1], F32)
        pc["kssw"] = sb(f"kssw_s{l}", [128, 1], F32)
        pc["qsr"] = sb(f"qsr_s{l}", [1, 128], F32)
        pc["ksr"] = sb(f"ksr_s{l}", [1, 128], F32)
        per_layer[l] = pc

    sems = {}

    def S(name):
        if name not in sems:
            sems[name] = PSem(es, nc, name)
        return sems[name]

    with nc.sbuf_tensor("xn", [128, NCH, T], BF16) as xn:
        emit_p0(nc, S, dram, xn, C, per_layer)
        emit_p1(nc, S, dram, xn, C, d_acc)
        cl = dict(C)
        cl.update(per_layer[0])
        dr0 = dict(fwqkv=dram["fwqkv0"], fwo=dram["fwo0"], fw1=dram["fw10"],
                   fw2=dram["fw20"], fb1=dram["fb10"], fb2=dram["fb20"], tabs=[dram[f"tab{i}"] for i in range(4)],
                   tabslo=[dram[f"tab{i}lo"] for i in range(4)],
                   x1=d_x1, x=dram["x_ctx"], acc=d_acc, out=d_out, lname="0")
        emit_ffn_layer(nc, S, 0, xn, cl, dr0)
    cl = dict(C)
    cl.update(per_layer[1])
    dr1 = dict(fwqkv=dram["fwqkv1"], fwo=dram["fwo1"], fw1=dram["fw11"],
               fw2=dram["fw21"], fb1=dram["fb11"], fb2=dram["fb21"], tabs=[dram[f"tab{i}"] for i in range(4)],
               tabslo=[dram[f"tab{i}lo"] for i in range(4)],
               x1=d_x1, x=dram["x_ctx"], acc=d_acc, out=d_out, lname="1")
    emit_ffn_layer(nc, S, 1, None, cl, dr1)

    es.close()
    return nc


def emit_p0(nc, S, dram, xn, C, per_layer):
    with (
        nc.sbuf_tensor("p0_xt", [128, 2, D], F32) as xt,
        nc.sbuf_tensor("p0_xnt", [128, 2, D], BF16) as xnt,
        nc.sbuf_tensor("p0_sq", [128, 2, D], F32) as sq0,
        nc.sbuf_tensor("p0_st", [128, 16], F32) as st8,
        nc.psum_tensor("p0_TPa", [128, 1024], F32) as TPa,
        nc.psum_tensor("p0_TPb", [128, 1024], F32) as TPb,
    ):
        TP = [TPa, TPb]
        pl = Plan()
        dxs = [S("p0x0"), S("p0x1")]
        dcon, dwq = S("p0c"), S("p0w")
        s_sq, v_st, s_sd, v_rs, s_nm = S("p0sq"), S("p0vst"), S("p0sd"), S("p0vrs"), S("p0nm")
        pe_T, v_ev = S("p0T"), S("p0ev")

        def cl(dst, src):
            pl.step("sync", lambda e, d=dst, s=src: e.dma_start(d, s), incs=[(dcon, 16)])

        cl(C["bmask"][:, :], dram["bmask"][:, :])
        cl(C["logva"][:, :], dram["logva"][:, :])
        cl(C["logvf"][:, :], dram["logvf"][:, :])
        cl(C["idf"][:, :], dram["identc"][:, :])
        cl(C["halfa"][:, :], dram["halfac"][:, :])
        cl(C["halfb"][:, :], dram["halfbc"][:, :])
        for l in range(2):
            for nm in ("lng", "lnb", "qs", "qssw", "ks", "kssw", "qsr", "ksr"):
                cl(per_layer[l][nm][:, :], dram[f"{nm}{l}"][:, :])

        def gl(dst, src):
            pl.step("gpsimd", lambda e, d=dst, s=src: e.dma_start(d, s), incs=[(dwq, 16)])

        gl(C["idb"][:, :], dram["identc"][:, :])
        gl(C["r2sb"][:, :], dram["r2sc"][:, :])
        gl(C["trib"][:, :], dram["tric"][:, :])
        gl(C["antib"][:, :], dram["antic"][:, :])

        pl.step("vector", lambda e: e.memset(C["onesrow"][:, :], 1.0), incs=[(v_ev, 1)])
        pl.step("vector", lambda e: e.memset(C["onesf1"][:, :], 1.0), incs=[(v_ev, 1)])
        pl.step("vector", lambda e: e.memset(C["onescol"][:, :], 1.0), incs=[(v_ev, 1)])

        for i in range(32):
            b = i % 2
            t0 = 128 * i
            pl.step("sync",
                    lambda e, b=b, t0=t0: e.dma_start(xt[:, b, :], dram["x_ctx"][t0:t0 + 128, :]),
                    waits=pl.gate(("xt", b)), incs=[(dxs[b], 16)])
            pl.step("vector",
                    lambda e, b=b: e.reduce_sum(st8[:, 8 * b + 0:8 * b + 1], xt[:, b, :], axis=AX.X),
                    waits=w(dxs[b]))
            pl.step("scalar",
                    lambda e, b=b: e.activation(sq0[:, b, :], xt[:, b, :], AF.Square,
                                                accum_out=st8[:, 8 * b + 1:8 * b + 2]),
                    waits=w(dxs[b]), incs=[(s_sq, 1)])

            def _stats(e, b=b):
                c0 = 8 * b
                e.tensor_scalar_mul(st8[:, c0 + 2:c0 + 3], st8[:, c0 + 0:c0 + 1], 1.0 / D)
                e.tensor_scalar_mul(st8[:, c0 + 3:c0 + 4], st8[:, c0 + 1:c0 + 2], 1.0 / D)
                e.tensor_tensor(st8[:, c0 + 4:c0 + 5], st8[:, c0 + 2:c0 + 3], st8[:, c0 + 2:c0 + 3], ALU.mult)
                e.tensor_tensor(st8[:, c0 + 5:c0 + 6], st8[:, c0 + 3:c0 + 4], st8[:, c0 + 4:c0 + 5], ALU.subtract)
                return e.tensor_scalar_add(st8[:, c0 + 5:c0 + 6], st8[:, c0 + 5:c0 + 6], 1e-5)
            pl.step("vector", _stats, waits=w(s_sq), incs=[(v_st, 1)])
            pl.step("scalar",
                    lambda e, b=b: e.activation(st8[:, 8 * b + 6:8 * b + 7], st8[:, 8 * b + 5:8 * b + 6], AF.Sqrt),
                    waits=w(v_st), incs=[(s_sd, 1)])

            def _rs(e, b=b):
                c0 = 8 * b
                e.reciprocal(st8[:, c0 + 7:c0 + 8], st8[:, c0 + 6:c0 + 7])
                return e.tensor_scalar(st8[:, c0 + 4:c0 + 5], st8[:, c0 + 2:c0 + 3],
                                       st8[:, c0 + 7:c0 + 8], -1.0, ALU.mult, ALU.mult)
            pl.step("vector", _rs, waits=w(s_sd), incs=[(v_rs, 1)])
            pl.step("scalar",
                    lambda e, b=b: e.activation(xnt[:, b, :], xt[:, b, :], AF.Identity,
                                                bias=st8[:, 8 * b + 4:8 * b + 5],
                                                scale=st8[:, 8 * b + 7:8 * b + 8]),
                    waits=w(v_rs), incs=[(s_nm, 1)])
            pl.set_gate(("xt", b), s_nm)

            def _tr(e, b=b):
                inst = None
                for c in range(8):
                    inst = e.matmul(TP[b][:, 128 * c:128 * (c + 1)],
                                    xnt[:, b, 128 * c:128 * (c + 1)], C["idb"][:, :],
                                    start=True, stop=True)
                return inst
            pl.step("tensor", _tr, waits=w(s_nm, dwq) + pl.gate(("TP", b)), incs=[(pe_T, 1)])
            def _evx(e, b=b, t0=t0):
                inst = None
                for c in range(8):
                    inst = e.tensor_copy(xn[:, c, t0:t0 + 128], TP[b][:, 128 * c:128 * (c + 1)])
                return inst
            pl.step("vector", _evx, waits=w(pe_T), incs=[(v_ev, 1)])
            pl.set_gate(("TP", b), v_ev)

        pl.step("sync", lambda e: e.nop(), waits=w(dcon))
        pl.step("gpsimd", lambda e: e.nop(), waits=w(dwq))

        with nc.Block() as blk:
            pl.emit(nc, blk)


def emit_p1(nc, S, dram, xn, C, d_acc):
    with ExitStack() as xs:
        wqsl = xs.enter_context(nc.sbuf_tensor("p1_wq", [128, 2, 3, NCH, 128], BF16))
        bqkv = xs.enter_context(nc.sbuf_tensor("p1_bq", [1, 3 * D], BF16))
        bproj = xs.enter_context(nc.sbuf_tensor("p1_bp", [1, D], BF16))
        Kfm = xs.enter_context(nc.sbuf_tensor("p1_K", [128, T], BF16))
        Vtm = xs.enter_context(nc.sbuf_tensor("p1_V", [128, NCHUNK * 130], BF16))
        Qfm = xs.enter_context(nc.sbuf_tensor("p1_Q", [128, OWN], BF16))
        vtmp = xs.enter_context(nc.sbuf_tensor("p1_vt", [128, 512], BF16))
        PT = xs.enter_context(nc.sbuf_tensor("p1_PT", [128, 2, 1024], BF16))
        bb = xs.enter_context(nc.sbuf_tensor("p1_bb", [128, 40], F32))
        rl = xs.enter_context(nc.sbuf_tensor("p1_rl", [128, 1024], F32))
        ofm = xs.enter_context(nc.sbuf_tensor("p1_o", [128, NHG * OWN], BF16))
        wps = xs.enter_context(nc.sbuf_tensor("p1_wp", [128, NCH, D], BF16))
        stg = xs.enter_context(nc.sbuf_tensor("p1_stg", [128, 2, D], F32))
        ST0 = xs.enter_context(nc.psum_tensor("p1_ST0", [128, 1024], F32))
        ST1 = xs.enter_context(nc.psum_tensor("p1_ST1", [128, 1024], F32))
        AVA = xs.enter_context(nc.psum_tensor("p1_AVA", [128, 512], F32))
        AVB = xs.enter_context(nc.psum_tensor("p1_AVB", [128, 512], F32))
        RRA = xs.enter_context(nc.psum_tensor("p1_RRA", [128, 512], F32))
        RRB = xs.enter_context(nc.psum_tensor("p1_RRB", [128, 512], F32))

        STb = [ST0, ST1]
        pl = Plan()
        d_c1 = S("p1c1")
        d_w = [S("p1w0"), S("p1w1")]
        pe_mm, ev_q = S("p1mm"), S("p1ev")
        pe_st, s_exp, v_msk, pe_av = S("p1st"), S("p1exp"), S("p1msk"), S("p1av")
        v_rl, pe_rr, v_on = S("p1rl"), S("p1rr"), S("p1on")
        v_bnd, pe_bnd = S("p1bnd"), S("p1pebnd")
        v_stg = S("p1stg")
        d_o1 = [S("p1out0"), S("p1out1")]

        pl.step("gpsimd", lambda e: e.dma_start(bqkv[:, :], dram["bqkv_a"][:, :]), incs=[(d_c1, 16)])
        pl.step("gpsimd", lambda e: e.dma_start(bproj[:, :], dram["bproj"][:, :]), incs=[(d_c1, 16)])
        pl.step("gpsimd", lambda e: e.dma_start(wps[:, :, :], dram["wproj"][:, :, :]), incs=[(d_c1, 16)])
        pl.step("vector", lambda e: e.memset(AP(Vtm, 64, [[130 * NCHUNK, 128], [130, NCHUNK]]), 1.0),
                incs=[(v_on, 1)])
        pl.set_gate("ones", v_on)

        wq_ready = {}

        def prefetch(hg):
            hb = hg % 2
            for s3, col in ((0, hg * 128), (1, D + hg * 128), (2, 2 * D + hg * 128)):
                pl.step("gpsimd",
                        lambda e, hb=hb, s3=s3, col=col: e.dma_start(
                            wqsl[:, hb, s3, :, :],
                            AP(dram["wqkv_a"], col, [[NCH * 3 * D, 128], [3 * D, NCH], [1, 128]])),
                        waits=(w(pe_mm) if hg >= 2 else []), incs=[(d_w[hb], 16)])
            wq_ready[hg] = (d_w[hg % 2], d_w[hg % 2].n)

        prefetch(0)
        prefetch(1)
        for hg in range(NHG):
            hb = hg % 2

            def build_fm(dst, s3, t0s, own_off, col):
                for gi, t0 in enumerate(t0s):
                    ps = STb[gi % 2]

                    def _mm(e, t0=t0, s3=s3, col=col, ps=ps, hb=hb):
                        for c in range(NCH):
                            e.matmul(ps[:, 0:512], wqsl[:, hb, s3, c, :],
                                     xn[:, c, t0:t0 + 512], start=(c == 0), stop=False)
                        return e.matmul(ps[:, 0:512], bqkv[:, col:col + 128],
                                        C["onesrow"][:, :], start=False, stop=True)
                    pl.step("tensor", _mm,
                            waits=[wq_ready[hg]] + w(d_c1) + pl.gate(("ST", gi % 2)) + pl.gate("KQV"),
                            incs=[(pe_mm, 1)])
                    pl.step("scalar",
                            lambda e, dst=dst, t0=t0, ps=ps, oo=own_off:
                            e.copy(dst[:, t0 - oo:t0 - oo + 512], ps[:, 0:512]),
                            waits=w(pe_mm), incs=[(ev_q, 1)])
                    pl.set_gate(("ST", gi % 2), ev_q)

            build_fm(Kfm, 1, [512 * g for g in range(8)], 0, D + hg * 128)
            build_fm(Qfm, 0, [OWN + 512 * g for g in range(4)], OWN, hg * 128)

            for gi in range(8):
                t0 = 512 * gi
                ps = STb[gi % 2]
                av = [AVA, AVB][gi % 2]

                def _mmv(e, t0=t0, ps=ps, hb=hb, hg=hg):
                    for c in range(NCH):
                        e.matmul(ps[:, 0:512], wqsl[:, hb, 2, c, :],
                                 xn[:, c, t0:t0 + 512], start=(c == 0), stop=False)
                    return e.matmul(ps[:, 0:512], bqkv[:, 2 * D + hg * 128:2 * D + hg * 128 + 128],
                                    C["onesrow"][:, :], start=False, stop=True)
                pl.step("tensor", _mmv, waits=pl.gate(("ST", gi % 2)) + pl.gate("KQV") + pl.gate("vtmp"),
                        incs=[(pe_mm, 1)])
                pl.step("scalar", lambda e, ps=ps: e.copy(vtmp[:, :], ps[:, 0:512]),
                        waits=w(pe_mm), incs=[(ev_q, 1)])
                pl.set_gate(("ST", gi % 2), ev_q)

                def _trv(e, av=av):
                    inst = None
                    for j in range(4):
                        inst = e.matmul(av[:, 128 * j:128 * (j + 1)], vtmp[:, 128 * j:128 * (j + 1)],
                                        C["idb"][:, :], start=True, stop=True)
                    return inst
                pl.step("tensor", _trv, waits=w(ev_q) + pl.gate(("AV", gi % 2)) + pl.gate("ones"),
                        incs=[(pe_mm, 1)])
                pl.set_gate("vtmp", pe_mm)

                def _evv(e, gi=gi, av=av):
                    inst = None
                    for j in range(4):
                        base = (gi * 4 + j) * 130
                        inst = e.tensor_copy(AP(Vtm, base, [[130 * NCHUNK, 128], [1, 64]]),
                                             av[:, 128 * j:128 * j + 64])
                        inst = e.tensor_copy(AP(Vtm, base + 65, [[130 * NCHUNK, 128], [1, 64]]),
                                             av[:, 128 * j + 64:128 * j + 128])
                    return inst
                pl.step("vector", _evv, waits=w(pe_mm), incs=[(v_on, 1), (ev_q, 1)])
                pl.set_gate(("AV", gi % 2), v_on)

            if hg + 2 < NHG:
                prefetch(hg + 2)

            pl.step("vector", lambda e: e.reduce_max(bb[:, 33:34], Kfm[:, :], axis=AX.X,
                                                     apply_absolute_value=True),
                    waits=w(ev_q, s_exp), incs=[(v_bnd, 1)])
            pl.step("vector", lambda e: e.reduce_max(bb[:, 34:35], Qfm[:, :], axis=AX.X,
                                                     apply_absolute_value=True),
                    waits=w(ev_q), incs=[(v_bnd, 1)])
            pl.step("vector", lambda e: e.tensor_tensor(bb[:, 35:36], bb[:, 33:34], bb[:, 34:35], ALU.mult),
                    waits=w(v_bnd), incs=[(v_bnd, 1)])
            pl.step("tensor", lambda e: e.matmul(RRA[0:1, 0:1], C["onescol"][:, :], bb[:, 35:36],
                                                 start=True, stop=True),
                    waits=w(v_bnd) + pl.gate("RRA"), incs=[(pe_bnd, 1)])
            pl.step("vector", lambda e: e.tensor_copy(bb[0:1, 36:37], RRA[0:1, 0:1]),
                    waits=w(pe_bnd), incs=[(v_bnd, 1)])
            pl.set_gate("RRA", v_bnd)
            pl.step("tensor", lambda e: e.matmul(RRB[:, 0:1], C["onesf1"][:, :], bb[0:1, 36:37],
                                                 start=True, stop=True),
                    waits=w(v_bnd) + pl.gate("RRB"), incs=[(pe_bnd, 1)])

            def _fixb(e):
                e.tensor_scalar(bb[:, 37:38], RRB[:, 0:1], -0.125, 40.0, ALU.mult, ALU.add)
                return e.tensor_scalar_add(bb[:, 0:32], C["logva"][:, :], bb[:, 37:38])
            pl.step("vector", _fixb, waits=w(pe_bnd), incs=[(v_bnd, 1)])
            pl.set_gate("RRB", v_bnd)

            for qg in range(4):
                q0 = 512 * qg
                qt0 = 16 + 4 * qg
                kcmax = qt0 + 4
                for kc in range(kcmax):
                    bu = kc % 2
                    stp = STb[bu]

                    def _st(e, kc=kc, q0=q0, stp=stp):
                        e.matmul(stp[:, 0:512], Kfm[0:64, 128 * kc:128 * (kc + 1)],
                                 Qfm[0:64, q0:q0 + 512], start=True, stop=True)
                        return e.matmul(stp[:, 512:1024], Kfm[64:128, 128 * kc:128 * (kc + 1)],
                                        Qfm[64:128, q0:q0 + 512], start=True, stop=True)
                    pl.step("tensor", _st, waits=pl.gate(("ST", bu)) + w(v_bnd), incs=[(pe_st, 1)])
                    pl.step("scalar",
                            lambda e, bu=bu, kc=kc, stp=stp: e.activation(
                                PT[:, bu, :], stp[:, :], AF.Exp,
                                bias=bb[:, kc:kc + 1], scale=0.125),
                            waits=w(pe_st) + pl.gate(("PT", bu)), incs=[(s_exp, 1)])
                    pl.set_gate(("ST", bu), s_exp)
                    need_mask = kc >= qt0
                    if need_mask:
                        def _mask(e, kc=kc, qt0=qt0, bu=bu):
                            inst = None
                            for h in range(2):
                                for i4 in range(4):
                                    qt = qt0 + i4
                                    sl = PT[:, bu, 512 * h + 128 * i4: 512 * h + 128 * (i4 + 1)]
                                    if kc == qt:
                                        inst = e.tensor_tensor(sl, sl, C["trib"][:, :], ALU.mult)
                                    elif kc > qt:
                                        inst = e.memset(sl, 0.0)
                            return inst
                        pl.step("vector", _mask, waits=w(s_exp), incs=[(v_msk, 1)])

                    def _av(e, kc=kc, bu=bu, first=(kc == 0), last=(kc == kcmax - 1)):
                        e.matmul(AVA[0:65, :], Vtm[:, 130 * kc:130 * kc + 65], PT[:, bu, 0:512],
                                 start=first, stop=last)
                        e.matmul(AVB[0:1, :], Vtm[:, 130 * kc + 64:130 * kc + 65], PT[:, bu, 512:1024],
                                 start=first, stop=last)
                        return e.matmul(AVB[64:128, :], Vtm[:, 130 * kc + 65:130 * kc + 129],
                                        PT[:, bu, 512:1024], start=first, stop=last)
                    avw = (w(v_msk) if need_mask else w(s_exp))
                    if kc == 0:
                        avw = avw + pl.gate(("AV", 0)) + pl.gate(("AV", 1))
                    pl.step("tensor", _av, waits=avw, incs=[(pe_av, 1)])
                    pl.set_gate(("PT", bu), pe_av)

                def _rlf(e):
                    e.tensor_scalar_add(rl[0:1, 0:512], AVA[64:65, :], 1e-38)
                    e.reciprocal(rl[0:1, 0:512], rl[0:1, 0:512])
                    e.tensor_scalar_add(rl[0:1, 512:1024], AVB[0:1, :], 1e-38)
                    return e.reciprocal(rl[0:1, 512:1024], rl[0:1, 512:1024])
                pl.step("vector", _rlf, waits=w(pe_av), incs=[(v_rl, 1)])

                def _rep(e):
                    e.matmul(RRA[:, :], C["onesf1"][:, :], rl[0:1, 0:512], start=True, stop=True)
                    return e.matmul(RRB[:, :], C["onesf1"][:, :], rl[0:1, 512:1024], start=True, stop=True)
                pl.step("tensor", _rep, waits=w(v_rl) + pl.gate("RRA") + pl.gate("RRB"), incs=[(pe_rr, 1)])
                pl.step("vector", lambda e: e.tensor_copy(rl[0:64, 0:512], RRA[0:64, :]),
                        waits=w(pe_rr), incs=[(v_rl, 1)])
                pl.step("vector", lambda e: e.tensor_copy(rl[64:128, 512:1024], RRB[64:128, :]),
                        waits=w(pe_rr), incs=[(v_rl, 1)])
                pl.set_gate("RRA", v_rl)
                pl.set_gate("RRB", v_rl)

                def _norm(e, hg=hg, q0=q0):
                    o0 = hg * OWN + q0
                    e.tensor_tensor(ofm[0:64, o0:o0 + 512], AVA[0:64, :], rl[0:64, 0:512], ALU.mult)
                    return e.tensor_tensor(ofm[64:128, o0:o0 + 512], AVB[64:128, :],
                                           rl[64:128, 512:1024], ALU.mult)
                pl.step("vector", _norm, waits=w(v_rl), incs=[(v_on, 1), (v_rl, 1)])
                pl.set_gate(("AV", 0), v_on)
                pl.set_gate(("AV", 1), v_on)
            pl.set_gates = None
            pl.G["KQV"] = w(pe_st) + w(pe_av) + w(s_exp)

        for tt in range(16):
            bu = tt % 2
            ps = STb[bu]

            def _pmm(e, tt=tt, ps=ps):
                inst = None
                for half in range(2):
                    for c in range(NCH):
                        e.matmul(ps[:, 512 * half:512 * (half + 1)],
                                 ofm[:, c * OWN + 128 * tt: c * OWN + 128 * (tt + 1)],
                                 wps[:, c, 512 * half:512 * (half + 1)],
                                 start=(c == 0), stop=False)
                    inst = e.matmul(ps[:, 512 * half:512 * (half + 1)],
                                    C["onesrow"][:, 0:128], bproj[:, 512 * half:512 * (half + 1)],
                                    start=False, stop=True)
                return inst
            pl.step("tensor", _pmm, waits=w(v_on) + pl.gate(("ST", bu)), incs=[(pe_mm, 1)])
            pl.step("vector", lambda e, bu=bu, ps=ps: e.tensor_copy(stg[:, bu, :], ps[:, :]),
                    waits=w(pe_mm) + pl.gate(("stg", bu)), incs=[(v_stg, 1)])
            pl.set_gate(("ST", bu), v_stg)
            pl.step("sync", lambda e, tt=tt, bu=bu: e.dma_start(d_acc[128 * tt:128 * (tt + 1), :], stg[:, bu, :]),
                    waits=w(v_stg), incs=[(d_o1[bu], 16)])
            pl.set_gate(("stg", bu), d_o1[bu])
        pl.step("sync", lambda e: e.nop(), waits=w(d_o1[0]) + w(d_o1[1]))

        with nc.Block() as blk:
            pl.emit(nc, blk)


def emit_ffn_layer(nc, S, L, xn, C, dr):
    QC0 = L1_QKV_C0 if L == 0 else L2_QKV_C0
    AC0 = L1_ATT_C0 if L == 0 else L2_ATT_C0
    DOMQ = (NCHUNK - QC0) * 128
    DOMA = (NCHUNK - AC0) * 128
    QT0 = QC0 * 128
    AT0 = AC0 * 128
    NKC = NCHUNK - QC0

    qgs = []
    c = AC0
    while c < NCHUNK:
        qgs.append((c, min(c + 4, NCHUNK)))
        c = min(c + 4, NCHUNK)

    with ExitStack() as outer:
        off = outer.enter_context(nc.sbuf_tensor(f"f{L}_o", [128, 4 * DOMA], BF16))
        _ffn_attn(nc, S, L, xn, C, dr, off, QC0, AC0, DOMQ, DOMA, QT0, NKC, qgs)
        _ffn_mlp(nc, S, L, xn, C, dr, off, AC0, DOMA, AT0, qgs)


def _ffn_attn(nc, S, L, xn, C, dr, off, QC0, AC0, DOMQ, DOMA, QT0, NKC, qgs):
    with ExitStack() as xs:
        srcbf = None
        if L == 1:
            srcbf = xs.enter_context(nc.sbuf_tensor(f"f{L}_src", [128, NCH, DOMQ + 128], BF16))
        wqf = xs.enter_context(nc.sbuf_tensor(f"f{L}_wq", [128, NCH, 3 * FINNER], BF16))
        Dt = xs.enter_context(nc.sbuf_tensor(f"f{L}_Dt", [128, 4, TAB_N], F32))
        qk = xs.enter_context(nc.sbuf_tensor(f"f{L}_qk", [128, 2, DOMQ], BF16))
        Vtf = xs.enter_context(nc.sbuf_tensor(f"f{L}_V", [128, NKC * 130], BF16))
        PTf = xs.enter_context(nc.sbuf_tensor(f"f{L}_PT", [128, 2, 1024], BF16))
        bbf = xs.enter_context(nc.sbuf_tensor(f"f{L}_bb", [128, 40], F32))
        rlw = xs.enter_context(nc.sbuf_tensor(f"f{L}_rl", [128, 1024], F32))
        vtmpf = xs.enter_context(nc.sbuf_tensor(f"f{L}_vt", [128, 512], BF16))
        q2 = xs.enter_context(nc.sbuf_tensor(f"f{L}_q2", [128, 512], F32))
        sst = xs.enter_context(nc.sbuf_tensor(f"f{L}_sst", [1, 2, 512], F32))
        tbuf = xs.enter_context(nc.sbuf_tensor(f"f{L}_tb", [128, 2, 512], F32))
        ST0 = xs.enter_context(nc.psum_tensor(f"f{L}_ST0", [128, 1024], F32))
        ST1 = xs.enter_context(nc.psum_tensor(f"f{L}_ST1", [128, 1024], F32))
        AVA = xs.enter_context(nc.psum_tensor(f"f{L}_AVA", [128, 512], F32))
        AVB = xs.enter_context(nc.psum_tensor(f"f{L}_AVB", [128, 512], F32))
        RRA = xs.enter_context(nc.psum_tensor(f"f{L}_RRA", [128, 512], F32))
        RRB = xs.enter_context(nc.psum_tensor(f"f{L}_RRB", [128, 512], F32))

        STb = [ST0, ST1]
        pl = Plan()
        d_w, d_tab, d_src = S("fAw"), S("fAtab"), S("fAsrc")
        d_tlo, v_tlo = S("fAtlo"), S("fAvtlo")
        v_dt = S("fAdt")
        pe_mm, ev_q, s_ev = S("fAmm"), S("fAev"), S("fAsev")
        pe_rot, g_fin, v_fin = S("fArot"), S("fAgfin"), S("fAfin")
        pe_ssq, v_ssq, s_ssq = S("fApssq"), S("fAvssq"), S("fAsssq")
        pe_rep = S("fArep")
        pe_st, s_exp, v_msk, pe_av = S("fAst"), S("fAexp"), S("fAmsk"), S("fAav")
        v_rl, pe_rr, v_on = S("fArl"), S("fArr"), S("fAon")
        v_bnd, pe_bnd = S("fAbnd"), S("fApebnd")
        v_bfx = S("fAbfx")

        pl.step("gpsimd", lambda e: e.dma_start(wqf[:, :, :], dr["fwqkv"][:, :, :]), incs=[(d_w, 16)])
        for i in range(4):
            pl.step("sync", lambda e, i=i: e.dma_start(Dt[:, i, :], dr["tabs"][i][:, :]),
                    incs=[(d_tab, 16)])
        qkst = AP(qk, 0, [[2 * DOMQ, 128], [1, TAB_N]])
        for i in range(4):
            pl.step("gpsimd", lambda e, i=i: e.dma_start(qkst, dr["tabslo"][i][:, :]),
                    waits=pl.gate("tlo"), incs=[(d_tlo, 16)])
            pl.step("vector",
                    lambda e, i=i: e.tensor_tensor(Dt[:, i, :], Dt[:, i, :], qkst, ALU.add),
                    waits=w(d_tab) + w(d_tlo), incs=[(v_tlo, 1)])
            pl.set_gate("tlo", v_tlo)
        if L == 1:
            off0 = QT0 - 128 - X1T0
            pl.step("gpsimd",
                    lambda e: e.dma_start(srcbf[:, :, :],
                                          AP(dr["x1"], off0, [[X1N, 128], [128 * X1N, NCH], [1, DOMQ + 128]])),
                    incs=[(d_src, 16)])
            bcol = 2047 - (QT0 - 128)
            def _bfix(e, bcol=bcol):
                return e.tensor_scalar_mul(
                    AP(srcbf, bcol, [[(DOMQ + 128) * NCH, 128], [DOMQ + 128, NCH]]),
                    AP(srcbf, bcol, [[(DOMQ + 128) * NCH, 128], [DOMQ + 128, NCH]]),
                    C["bmask"][:, :])
            pl.step("vector", _bfix, waits=w(d_src), incs=[(v_bfx, 1)])

        def src_sl(c, t0, nn):
            sh = 1 if c >= 4 else 0
            if L == 0:
                return xn[:, c, t0 - sh:t0 - sh + nn]
            b0 = QT0 - 128
            return srcbf[:, c, t0 - sh - b0:t0 - sh - b0 + nn]

        def _dt(e):
            e.tensor_scalar_mul(Dt[:, 0, :], Dt[:, 0, :], C["qs"][:, :])
            e.tensor_scalar_mul(Dt[:, 1, :], Dt[:, 1, :], C["qssw"][:, :])
            e.tensor_scalar_mul(Dt[:, 2, :], Dt[:, 2, :], C["ks"][:, :])
            return e.tensor_scalar_mul(Dt[:, 3, :], Dt[:, 3, :], C["kssw"][:, :])
        pl.step("vector", _dt, waits=w(d_tab), incs=[(v_dt, 1)])
        pl.step("vector", lambda e: e.memset(AP(Vtf, 64, [[130 * NKC, 128], [130, NKC]]), 1.0),
                incs=[(v_on, 1)])
        pl.set_gate("fones", v_on)
        pl.step("vector", lambda e: e.reduce_max(bbf[0:1, 33:34], C["qsr"][:, :], axis=AX.X,
                                                 apply_absolute_value=True), incs=[(v_bnd, 1)])
        pl.step("vector", lambda e: e.reduce_max(bbf[0:1, 34:35], C["ksr"][:, :], axis=AX.X,
                                                 apply_absolute_value=True), incs=[(v_bnd, 1)])
        pl.step("vector", lambda e: e.tensor_tensor(bbf[0:1, 35:36], bbf[0:1, 33:34],
                                                    bbf[0:1, 34:35], ALU.mult),
                waits=w(v_bnd), incs=[(v_bnd, 1)])
        pl.step("tensor", lambda e: e.matmul(RRB[:, 0:1], C["onesf1"][:, :], bbf[0:1, 35:36],
                                             start=True, stop=True),
                waits=w(v_bnd), incs=[(pe_bnd, 1)])

        def _fixb(e):
            e.tensor_scalar(bbf[:, 36:37], RRB[:, 0:1], -QK_SCALE, 20.0, ALU.mult, ALU.add)
            return e.tensor_scalar_add(bbf[:, 0:32], C["logvf"][:, :], bbf[:, 36:37])
        pl.step("vector", _fixb, waits=w(pe_bnd), incs=[(v_bnd, 1)])

        grp = []
        t = QT0
        while t < T:
            grp.append((t, min(t + 512, T) - t))
            t = min(t + 512, T)

        for hg in range(4):
            qcol, kcol, vcol = hg * 128, FINNER + hg * 128, 2 * FINNER + hg * 128

            for qi, col in ((0, qcol), (1, kcol)):
                for gi, (t0, nn) in enumerate(grp):
                    ps = STb[gi % 2]

                    def _mm(e, t0=t0, nn=nn, col=col, ps=ps):
                        inst = None
                        for c in range(NCH):
                            inst = e.matmul(ps[:, 0:nn], wqf[:, c, col:col + 128],
                                            src_sl(c, t0, nn), start=(c == 0), stop=(c == NCH - 1))
                        return inst
                    wt = w(d_w) + pl.gate(("fST", gi % 2)) + pl.gate("fqkKV") + (w(d_src, v_bfx) if L == 1 else [])
                    pl.step("tensor", _mm, waits=wt, incs=[(pe_mm, 1)])
                    pl.step("scalar",
                            lambda e, qi=qi, t0=t0, nn=nn, ps=ps: e.copy(
                                qk[:, qi, t0 - QT0:t0 - QT0 + nn], ps[:, 0:nn]),
                            waits=w(pe_mm) + w(v_tlo), incs=[(ev_q, 1), (s_ev, 1)])
                    pl.set_gate(("fST", gi % 2), ev_q)

            for gi, (t0, nn) in enumerate(grp):
                ps = STb[gi % 2]
                av = [AVA, AVB][gi % 2]

                def _mmv(e, t0=t0, nn=nn, ps=ps, vcol=vcol):
                    inst = None
                    for c in range(NCH):
                        inst = e.matmul(ps[:, 0:nn], wqf[:, c, vcol:vcol + 128],
                                        src_sl(c, t0, nn), start=(c == 0), stop=(c == NCH - 1))
                    return inst
                pl.step("tensor", _mmv, waits=pl.gate(("fST", gi % 2)) + pl.gate("fqkKV") + pl.gate("fvt"),
                        incs=[(pe_mm, 1)])
                pl.step("scalar", lambda e, ps=ps, nn=nn: e.copy(vtmpf[:, 0:nn], ps[:, 0:nn]),
                        waits=w(pe_mm), incs=[(ev_q, 1)])
                pl.set_gate(("fST", gi % 2), ev_q)
                nt = nn // 128

                def _trv(e, nt=nt, av=av):
                    inst = None
                    for j in range(nt):
                        inst = e.matmul(av[:, 128 * j:128 * (j + 1)], vtmpf[:, 128 * j:128 * (j + 1)],
                                        C["idb"][:, :], start=True, stop=True)
                    return inst
                pl.step("tensor", _trv, waits=w(ev_q) + pl.gate(("fAV", gi % 2)) + pl.gate("fones"),
                        incs=[(pe_mm, 1)])
                pl.set_gate("fvt", pe_mm)

                def _evv(e, gi=gi, nt=nt, av=av):
                    inst = None
                    for j in range(nt):
                        base = (gi * 4 + j) * 130
                        inst = e.tensor_copy(AP(Vtf, base, [[130 * NKC, 128], [1, 64]]),
                                             av[:, 128 * j:128 * j + 64])
                        inst = e.tensor_copy(AP(Vtf, base + 65, [[130 * NKC, 128], [1, 64]]),
                                             av[:, 128 * j + 64:128 * j + 128])
                    return inst
                pl.step("vector", _evv, waits=w(pe_mm), incs=[(v_on, 1), (ev_q, 1)])
                pl.set_gate(("fAV", gi % 2), v_on)

            for qi in range(2):
                D1 = Dt[:, 0, :] if qi == 0 else Dt[:, 2, :]
                D2 = Dt[:, 1, :] if qi == 0 else Dt[:, 3, :]
                for gi, (t0, nn) in enumerate(grp):
                    o = t0 - QT0
                    to = t0 - TAB_T0
                    ps = STb[gi % 2]
                    pl.step("scalar",
                            lambda e, qi=qi, o=o, nn=nn: e.activation(
                                q2[:, 0:nn], qk[:, qi, o:o + nn], AF.Square),
                            waits=w(s_ev) + pl.gate("q2"), incs=[(s_ssq, 1)])

                    def _sm(e, o=o, nn=nn):
                        e.matmul(RRA[0:1, 0:nn], C["onescol"][0:64, :], q2[0:64, 0:nn],
                                 start=True, stop=True)
                        return e.matmul(RRB[0:1, 0:nn], C["onescol"][64:128, :], q2[64:128, 0:nn],
                                        start=True, stop=True)
                    pl.step("tensor", _sm, waits=w(s_ssq) + pl.gate("fRRA") + pl.gate("fRRB"),
                            incs=[(pe_ssq, 1)])
                    pl.set_gate("q2", pe_ssq)

                    def _cps(e, nn=nn):
                        e.tensor_copy(sst[0:1, 0, 0:nn], RRA[0:1, 0:nn])
                        return e.tensor_copy(sst[0:1, 1, 0:nn], RRB[0:1, 0:nn])
                    pl.step("vector", _cps, waits=w(pe_ssq) + pl.gate("sst"), incs=[(v_ssq, 1)])
                    pl.set_gate("fRRA", v_ssq)
                    sstf = sst.ap().rearrange("p a b -> p (a b)")
                    pl.step("vector", lambda e, sstf=sstf: e.tensor_scalar_add(
                        sstf[0:1, :], sstf[0:1, :], 1e-24),
                        waits=w(v_ssq), incs=[(v_ssq, 1)])
                    pl.step("scalar", lambda e, sstf=sstf: e.activation(
                        sstf[0:1, :], sstf[0:1, :], AF.Sqrt),
                        waits=w(v_ssq), incs=[(s_ssq, 1)])
                    pl.step("vector", lambda e, sstf=sstf: e.reciprocal(
                        sstf[0:1, :], sstf[0:1, :]),
                        waits=w(s_ssq), incs=[(v_ssq, 1)])

                    def _rep(e, nn=nn):
                        e.matmul(RRB[:, 0:nn], C["halfa"][:, :], sst[0:1, 0, 0:nn],
                                 start=True, stop=False)
                        return e.matmul(RRB[:, 0:nn], C["halfb"][:, :], sst[0:1, 1, 0:nn],
                                        start=False, stop=True)
                    pl.step("tensor", _rep, waits=w(v_ssq) + pl.gate("fRRB"), incs=[(pe_rep, 1)])
                    pl.step("tensor",
                            lambda e, qi=qi, o=o, nn=nn, ps=ps: e.matmul(
                                ps[:, 0:nn], C["r2sb"][:, :], qk[:, qi, o:o + nn],
                                start=True, stop=True),
                            waits=w(s_ssq) + pl.gate(("fST", gi % 2)), incs=[(pe_rot, 1)])
                    pl.step("gpsimd",
                            lambda e, qi=qi, o=o, nn=nn, D1=D1, to=to: e.tensor_tensor(
                                tbuf[:, 0, 0:nn], qk[:, qi, o:o + nn], D1[:, to:to + nn], ALU.mult),
                            waits=w(s_ssq, v_dt) + pl.gate("tb"), incs=[(g_fin, 1)])
                    pl.step("vector",
                            lambda e, nn=nn, ps=ps, D2=D2, to=to: e.tensor_tensor(
                                tbuf[:, 1, 0:nn], ps[:, 0:nn], D2[:, to:to + nn], ALU.mult),
                            waits=w(pe_rot, v_dt) + pl.gate("tb"), incs=[(v_fin, 1), (ev_q, 1)])
                    pl.set_gate(("fST", gi % 2), ev_q)

                    def _fin(e, qi=qi, o=o, nn=nn):
                        e.tensor_tensor(tbuf[:, 0, 0:nn], tbuf[:, 0, 0:nn], tbuf[:, 1, 0:nn], ALU.add)
                        return e.tensor_tensor(qk[:, qi, o:o + nn], tbuf[:, 0, 0:nn],
                                               RRB[:, 0:nn], ALU.mult)
                    pl.step("vector", _fin, waits=w(g_fin, pe_rep), incs=[(v_fin, 1)])
                    pl.set_gate("tb", v_fin)
                    pl.set_gate("sst", pe_rep)
                    pl.set_gate("fRRB", v_fin)

            for (qc0, qc1) in qgs:
                nq = (qc1 - qc0) * 128
                q0 = qc0 * 128 - QT0
                kclist = [kc for kc in range(qc0 - 4, qc1) if QC0 <= kc < NCHUNK]
                for ki, kc in enumerate(kclist):
                    bu = ki % 2
                    stp = STb[bu]
                    ko = (kc - QC0) * 128

                    def _st(e, ko=ko, q0=q0, nq=nq, stp=stp):
                        e.matmul(stp[:, 0:nq], qk[0:64, 1, ko:ko + 128],
                                 qk[0:64, 0, q0:q0 + nq], start=True, stop=True)
                        return e.matmul(stp[:, 512:512 + nq], qk[64:128, 1, ko:ko + 128],
                                        qk[64:128, 0, q0:q0 + nq], start=True, stop=True)
                    pl.step("tensor", _st, waits=w(v_fin, v_bnd) + pl.gate(("fST", bu)), incs=[(pe_st, 1)])

                    def _exp(e, bu=bu, kc=kc, nq=nq, stp=stp):
                        e.activation(PTf[:, bu, 0:nq], stp[:, 0:nq], AF.Exp,
                                     bias=bbf[:, kc:kc + 1], scale=QK_SCALE)
                        return e.activation(PTf[:, bu, 512:512 + nq], stp[:, 512:512 + nq], AF.Exp,
                                            bias=bbf[:, kc:kc + 1], scale=QK_SCALE)
                    pl.step("scalar", _exp, waits=w(pe_st) + pl.gate(("fPT", bu)),
                            incs=[(s_exp, 1)])
                    pl.set_gate(("fST", bu), s_exp)

                    acts = []
                    for i4 in range(qc1 - qc0):
                        qt = qc0 + i4
                        if kc == qt:
                            acts.append((i4, "tri"))
                        elif kc == qt - 4:
                            acts.append((i4, "anti"))
                        elif kc > qt or kc < qt - 4:
                            acts.append((i4, "zero"))
                    need_mask = bool(acts)
                    if need_mask:
                        def _mask(e, acts=tuple(acts), bu=bu):
                            inst = None
                            for h in range(2):
                                for i4, kind in acts:
                                    sl = PTf[:, bu, 512 * h + 128 * i4: 512 * h + 128 * (i4 + 1)]
                                    if kind == "tri":
                                        inst = e.tensor_tensor(sl, sl, C["trib"][:, :], ALU.mult)
                                    elif kind == "anti":
                                        inst = e.tensor_tensor(sl, sl, C["antib"][:, :], ALU.mult)
                                    else:
                                        inst = e.memset(sl, 0.0)
                            return inst
                        pl.step("vector", _mask, waits=w(s_exp), incs=[(v_msk, 1)])

                    def _av(e, kc=kc, bu=bu, nq=nq, first=(ki == 0), last=(ki == len(kclist) - 1)):
                        kb = (kc - QC0) * 130
                        e.matmul(AVA[0:65, 0:nq], Vtf[:, kb:kb + 65], PTf[:, bu, 0:nq],
                                 start=first, stop=last)
                        e.matmul(AVB[0:1, 0:nq], Vtf[:, kb + 64:kb + 65], PTf[:, bu, 512:512 + nq],
                                 start=first, stop=last)
                        return e.matmul(AVB[64:128, 0:nq], Vtf[:, kb + 65:kb + 129],
                                        PTf[:, bu, 512:512 + nq], start=first, stop=last)
                    avw = (w(v_msk) if need_mask else w(s_exp))
                    if ki == 0:
                        avw = avw + pl.gate(("fAV", 0)) + pl.gate(("fAV", 1))
                    pl.step("tensor", _av, waits=avw, incs=[(pe_av, 1)])
                    pl.set_gate(("fPT", bu), pe_av)

                def _rlf(e, nq=nq):
                    e.tensor_scalar_add(rlw[0:1, 0:nq], AVA[64:65, 0:nq], 1e-30)
                    e.reciprocal(rlw[0:1, 0:nq], rlw[0:1, 0:nq])
                    e.tensor_scalar_add(rlw[0:1, 512:512 + nq], AVB[0:1, 0:nq], 1e-30)
                    return e.reciprocal(rlw[0:1, 512:512 + nq], rlw[0:1, 512:512 + nq])
                pl.step("vector", _rlf, waits=w(pe_av) + pl.gate("rlw"), incs=[(v_rl, 1)])

                def _rrm(e, nq=nq):
                    e.matmul(RRA[:, 0:nq], C["onesf1"][:, :], rlw[0:1, 0:nq], start=True, stop=True)
                    return e.matmul(RRB[:, 0:nq], C["onesf1"][:, :], rlw[0:1, 512:512 + nq],
                                    start=True, stop=True)
                pl.step("tensor", _rrm, waits=w(v_rl) + pl.gate("fRRA") + pl.gate("fRRB"), incs=[(pe_rr, 1)])
                pl.step("vector", lambda e, nq=nq: e.tensor_copy(rlw[0:64, 0:nq], RRA[0:64, 0:nq]),
                        waits=w(pe_rr), incs=[(v_rl, 1)])
                pl.step("vector", lambda e, nq=nq: e.tensor_copy(rlw[64:128, 512:512 + nq], RRB[64:128, 0:nq]),
                        waits=w(pe_rr), incs=[(v_rl, 1)])
                pl.set_gate("fRRA", v_rl)
                pl.set_gate("fRRB", v_rl)

                def _norm(e, hg=hg, qc0=qc0, nq=nq):
                    o0 = hg * DOMA + (qc0 - AC0) * 128
                    e.tensor_tensor(off[0:64, o0:o0 + nq], AVA[0:64, 0:nq], rlw[0:64, 0:nq], ALU.mult)
                    return e.tensor_tensor(off[64:128, o0:o0 + nq], AVB[64:128, 0:nq],
                                           rlw[64:128, 512:512 + nq], ALU.mult)
                pl.step("vector", _norm, waits=w(v_rl), incs=[(v_on, 1), (v_rl, 1)])
                pl.set_gate(("fAV", 0), v_on)
                pl.set_gate(("fAV", 1), v_on)
                pl.set_gate("rlw", v_on)
            pl.G["fqkKV"] = w(pe_st) + w(pe_av)

        with nc.Block() as blk:
            pl.emit(nc, blk)


def _ffn_mlp(nc, S, L, xn, C, dr, off, AC0, DOMA, AT0, qgs):
    with ExitStack() as xs:
        wos = xs.enter_context(nc.sbuf_tensor(f"m{L}_wo", [128, 4, D], BF16))
        fb1 = xs.enter_context(nc.sbuf_tensor(f"m{L}_fb1", [1, H2P], BF16))
        fb2 = xs.enter_context(nc.sbuf_tensor(f"m{L}_fb2", [1, D], BF16))
        w1a = xs.enter_context(nc.sbuf_tensor(f"m{L}_w1a", [128, 2, NCH, 256], BF16))
        w1g = xs.enter_context(nc.sbuf_tensor(f"m{L}_w1g", [128, 2, NCH, 256], BF16))
        w2s = xs.enter_context(nc.sbuf_tensor(f"m{L}_w2", [128, 2, 2, D], BF16))
        xmid = xs.enter_context(nc.sbuf_tensor(f"m{L}_xm", [128, NCH, 513], F32))
        hln = xs.enter_context(nc.sbuf_tensor(f"m{L}_h", [128, NCH, 512], BF16))
        gel = xs.enter_context(nc.sbuf_tensor(f"m{L}_g", [128, 512], BF16))
        x1g = xs.enter_context(nc.sbuf_tensor(f"m{L}_x1", [128, NCH, 512], F32))
        xsq = xs.enter_context(nc.sbuf_tensor(f"m{L}_xsq", [128, 1, 512], F32))
        mbuf = xs.enter_context(nc.sbuf_tensor(f"m{L}_m", [128, 22 * 512], BF16))
        lnst = xs.enter_context(nc.sbuf_tensor(f"m{L}_ln", [1, 1024], F32))
        baseb = xacc = None
        if L == 1:
            baseb = xs.enter_context(nc.sbuf_tensor(f"m{L}_base", [128, 2, NCH, 512], F32))
            xacc = xs.enter_context(nc.sbuf_tensor(f"m{L}_xacc", [128, 2, 2 * D], F32))
        ST0 = xs.enter_context(nc.psum_tensor(f"m{L}_ST0", [128, 1024], F32))
        ST1 = xs.enter_context(nc.psum_tensor(f"m{L}_ST1", [128, 1024], F32))
        RRA = xs.enter_context(nc.psum_tensor(f"m{L}_RRA", [128, 512], F32))
        RRB = xs.enter_context(nc.psum_tensor(f"m{L}_RRB", [128, 512], F32))
        TPp = xs.enter_context(nc.psum_tensor(f"m{L}_TP", [128, 1024], F32))

        STb = [ST0, ST1]
        w2slot = [ST0[:, 0:512], ST0[:, 512:1024], ST1[:, 0:512], ST1[:, 512:1024],
                  TPp[:, 0:512], TPp[:, 512:1024], RRA[:, 0:512], RRB[:, 0:512]]
        pl = Plan()
        d_wo = S("mBwo")
        d_w1 = [S("mBw1a"), S("mBw1b")]
        d_w2 = [S("mBw2a"), S("mBw2b")]
        d_base = [S("mBbase0"), S("mBbase1")]
        d_xin = [S("mBxin0"), S("mBxin1")]
        d_oo = [S("mBoo0"), S("mBoo1")]
        d_x1o = S("mBx1o")
        pe_wo, v_xm = S("mBpewo"), S("mBvxm")
        s_sq2, pe_stat = S("mBssq"), S("mBstat")
        v_ln, s_ln = S("mBvln"), S("mBsln")
        pe_rep2, v_u, s_hln = S("mBrep"), S("mBu"), S("mBhln")
        pe_w1, s_gel, v_m = S("mBw1"), S("mBgel"), S("mBm")
        pe_w2, v_x1 = S("mBw2"), S("mBx1")
        pe_tp, v_fin2 = S("mBtp"), S("mBfin")
        v_bfx2 = S("mBbfx")

        pl.step("gpsimd", lambda e: e.dma_start(wos[:, :, :], dr["fwo"][:, :, :]), incs=[(d_wo, 16)])
        pl.step("gpsimd", lambda e: e.dma_start(fb1[:, :], dr["fb1"][:, :]), incs=[(d_wo, 16)])
        pl.step("gpsimd", lambda e: e.dma_start(fb2[:, :], dr["fb2"][:, :]), incs=[(d_wo, 16)])

        w1blocks = [(j * 256, (j + 1) * 256) for j in range(11)]

        OUT0 = 2048

        for g, (gc0, gc1) in enumerate(qgs):
            W = (gc1 - gc0) * 128
            t0 = gc0 * 128
            gb = g % 2

            if g == 0:
                pl.step("vector", lambda e: e.memset(AP(xmid, 0, [[513 * NCH, 128], [513, NCH]]), 0.0),
                        waits=pl.gate("xmid"), incs=[(v_xm, 1)])
            else:
                Wp = Wprev
                pl.step("vector",
                        lambda e, Wp=Wp: e.tensor_copy(
                            AP(xmid, 0, [[513 * NCH, 128], [513, NCH]]),
                            AP(xmid, Wp, [[513 * NCH, 128], [513, NCH]])),
                        waits=pl.gate("xmid"), incs=[(v_xm, 1)])
            Wprev = W

            if L == 1:
                pl.step("sync",
                        lambda e, t0=t0, gb=gb, W=W: e.dma_start(
                            baseb[:, gb, :, 0:W],
                            AP(dr["x1"], t0 - X1T0, [[X1N, 128], [128 * X1N, NCH], [1, W]])),
                        waits=pl.gate(("base", gb)), incs=[(d_base[gb], 16)])
                if t0 <= 2047 < t0 + W:
                    bc = 2047 - t0
                    def _bfix2(e, gb=gb, bc=bc):
                        return e.tensor_scalar_mul(baseb[:, gb, :, bc:bc + 1],
                                                   baseb[:, gb, :, bc:bc + 1],
                                                   C["bmask"][:, :])
                    pl.step("vector", _bfix2, waits=w(d_base[gb]), incs=[(v_bfx2, 1)])

            for oc in range(NCH):
                ps = STb[oc % 2]

                def _womm(e, oc=oc, t0=t0, W=W, ps=ps):
                    inst = None
                    for c in range(4):
                        inst = e.matmul(ps[:, 0:W], wos[:, c, 128 * oc:128 * (oc + 1)],
                                        off[:, c * DOMA + (t0 - AT0): c * DOMA + (t0 - AT0) + W],
                                        start=(c == 0), stop=(c == 3))
                    return inst
                pl.step("tensor", _womm, waits=w(d_wo) + pl.gate(("mST", oc % 2)),
                        incs=[(pe_wo, 1)])

                def _xm(e, oc=oc, t0=t0, gb=gb, W=W, ps=ps):
                    if L == 0:
                        return e.tensor_tensor(xmid[:, oc, 1:W + 1], ps[:, 0:W], xn[:, oc, t0:t0 + W], ALU.add)
                    return e.tensor_tensor(xmid[:, oc, 1:W + 1], ps[:, 0:W], baseb[:, gb, oc, 0:W], ALU.add)
                pl.step("vector", _xm, waits=w(pe_wo) + pl.gate("xmid") + (w(d_base[gb], v_bfx2) if L == 1 else []),
                        incs=[(v_xm, 1)])
                pl.set_gate(("mST", oc % 2), v_xm)

            def shv(oc, W=W):
                sh = 1 if oc < 4 else 0
                return xmid[:, oc, sh:sh + W]

            def _mu(e, W=W):
                inst = None
                for oc in range(NCH):
                    inst = e.matmul(RRA[0:1, 0:W], C["onescol"][:, :], shv(oc, W),
                                    start=(oc == 0), stop=(oc == NCH - 1))
                return inst
            pl.step("tensor", _mu, waits=w(v_xm) + pl.gate("mRRA"), incs=[(pe_stat, 1)])
            for oc in range(NCH):
                pl.step("scalar",
                        lambda e, oc=oc, W=W: e.activation(xsq[:, 0, 0:W], shv(oc, W), AF.Square),
                        waits=w(v_xm) + w(pe_stat), incs=[(s_sq2, 1)])
                pl.step("tensor",
                        lambda e, oc=oc, W=W: e.matmul(RRB[0:1, 0:W], C["onescol"][:, :],
                                                       xsq[:, 0, 0:W],
                                                       start=(oc == 0), stop=(oc == NCH - 1)),
                        waits=w(s_sq2) + (pl.gate("mRRB") if oc == 0 else []), incs=[(pe_stat, 1)])

            def _lnv(e, W=W):
                e.tensor_scalar_mul(lnst[:, 0:W], RRA[0:1, 0:W], 1.0 / D)
                e.tensor_scalar_mul(lnst[:, 512:512 + W], RRB[0:1, 0:W], 1.0 / D)
                e.tensor_tensor(xsq[0:1, 0, 0:W], lnst[:, 0:W], lnst[:, 0:W], ALU.mult)
                e.tensor_tensor(lnst[:, 512:512 + W], lnst[:, 512:512 + W], xsq[0:1, 0, 0:W], ALU.subtract)
                return e.tensor_scalar_add(lnst[:, 512:512 + W], lnst[:, 512:512 + W], 1e-5)
            pl.step("vector", _lnv, waits=w(pe_stat) + pl.gate("lnst"), incs=[(v_ln, 1)])
            pl.step("scalar", lambda e, W=W: e.activation(lnst[:, 512:512 + W], lnst[:, 512:512 + W], AF.Sqrt),
                    waits=w(v_ln), incs=[(s_ln, 1)])

            def _lnr(e, W=W):
                e.reciprocal(lnst[:, 512:512 + W], lnst[:, 512:512 + W])
                return e.tensor_tensor(lnst[:, 0:W], lnst[:, 0:W], lnst[:, 512:512 + W], ALU.mult)
            pl.step("vector", _lnr, waits=w(s_ln), incs=[(v_ln, 1)])

            def _repln(e, W=W):
                e.matmul(RRA[:, 0:W], C["onesf1"][:, :], lnst[0:1, 512:512 + W], start=True, stop=True)
                return e.matmul(RRB[:, 0:W], C["onesf1"][:, :], lnst[0:1, 0:W], start=True, stop=True)
            pl.step("tensor", _repln, waits=w(v_ln), incs=[(pe_rep2, 1)])
            pl.set_gate("lnst", pe_rep2)

            for oc in range(NCH):
                def _u(e, oc=oc, W=W):
                    e.tensor_tensor(x1g[:, oc, 0:W], shv(oc, W), RRA[:, 0:W], ALU.mult)
                    return e.tensor_tensor(x1g[:, oc, 0:W], x1g[:, oc, 0:W], RRB[:, 0:W], ALU.subtract)
                pl.step("vector", _u, waits=w(pe_rep2) + pl.gate("x1g"),
                        incs=[(v_u, 1)])
                pl.step("scalar",
                        lambda e, oc=oc, W=W: e.activation(hln[:, oc, 0:W], x1g[:, oc, 0:W], AF.Identity,
                                                           bias=C["lnb"][:, oc:oc + 1],
                                                           scale=C["lng"][:, oc:oc + 1]),
                        waits=w(v_u) + pl.gate("hln"), incs=[(s_hln, 1)])

            for j, (c0, c1) in enumerate(w1blocks):
                jb = j % 2
                pl.step("gpsimd",
                        lambda e, c0=c0, jb=jb: e.dma_start(
                            w1a[:, jb, :, :],
                            AP(dr["fw1"], c0, [[NCH * H2P, 128], [H2P, NCH], [1, 256]])),
                        waits=pl.gate(("w1", jb)), incs=[(d_w1[jb], 16)])
                pl.step("gpsimd",
                        lambda e, c0=c0, jb=jb: e.dma_start(
                            w1g[:, jb, :, :],
                            AP(dr["fw1"], HIDP + c0, [[NCH * H2P, 128], [H2P, NCH], [1, 256]])),
                        waits=[], incs=[(d_w1[jb], 16)])
                for jj in range(2):
                    mc = (c0 + 128 * jj) // 128

                    def _gmm(e, jj=jj, jb=jb, mc=mc, W=W):
                        for c in range(NCH):
                            e.matmul(ST0[:, 0:W], w1g[:, jb, c, 128 * jj:128 * (jj + 1)],
                                     hln[:, c, 0:W], start=(c == 0), stop=False)
                        return e.matmul(ST0[:, 0:W], fb1[:, HIDP + 128 * mc:HIDP + 128 * mc + 128],
                                        C["onesrow"][:, 0:W], start=False, stop=True)
                    pl.step("tensor", _gmm,
                            waits=w(s_hln) + w(d_w1[jb]) + pl.gate(("mST", 0)),
                            incs=[(pe_w1, 1)])
                    pl.step("scalar", lambda e, W=W: e.activation(gel[:, 0:W], ST0[:, 0:W], AF.Gelu),
                            waits=w(pe_w1) + pl.gate("gel"), incs=[(s_gel, 1)])
                    pl.set_gate(("mST", 0), s_gel)

                    def _amm(e, jj=jj, jb=jb, mc=mc, W=W):
                        for c in range(NCH):
                            e.matmul(ST1[:, 0:W], w1a[:, jb, c, 128 * jj:128 * (jj + 1)],
                                     hln[:, c, 0:W], start=(c == 0), stop=False)
                        return e.matmul(ST1[:, 0:W], fb1[:, 128 * mc:128 * mc + 128],
                                        C["onesrow"][:, 0:W], start=False, stop=True)
                    pl.step("tensor", _amm,
                            waits=w(s_hln) + w(d_w1[jb]) + pl.gate(("mST", 1)),
                            incs=[(pe_w1, 1)])
                    pl.step("vector",
                            lambda e, mc=mc, W=W: e.tensor_tensor(
                                mbuf[:, 512 * mc:512 * mc + W], ST1[:, 0:W], gel[:, 0:W], ALU.mult),
                            waits=w(pe_w1, s_gel) + pl.gate("mbuf"), incs=[(v_m, 1)])
                    pl.set_gate(("mST", 1), v_m)
                    pl.set_gate("gel", v_m)
                pl.set_gate(("w1", jb), pe_w1)

            for mc in range(22):
                blk = mc // 2
                jb = blk % 2
                rr = mc % 2
                if rr == 0:
                    pl.step("gpsimd",
                            lambda e, blk=blk, jb=jb: e.dma_start(
                                w2s[:, jb, :, :],
                                AP(dr["fw2"], blk * 2 * D, [[(HIDP // 128) * D, 128], [D, 2], [1, D]])),
                            waits=pl.gate(("w2", jb)), incs=[(d_w2[jb], 16)])

                def _w2mm(e, mc=mc, jb=jb, rr=rr, W=W, first=(mc == 0)):
                    inst = None
                    for oc in range(NCH):
                        inst = e.matmul(w2slot[oc][:, 0:W] if W != 512 else w2slot[oc],
                                        w2s[:, jb, rr, 128 * oc:128 * (oc + 1)],
                                        mbuf[:, 512 * mc:512 * mc + W],
                                        start=first, stop=False)
                    return inst
                extra = (pl.gate(("mST", 0)) + pl.gate(("mST", 1)) + pl.gate("mRRA")
                         + pl.gate("mRRB") + pl.gate("mTP")) if mc == 0 else []
                pl.step("tensor", _w2mm, waits=w(v_m) + w(d_w2[jb]) + extra, incs=[(pe_w2, 1)])
                if rr == 1:
                    pl.set_gate(("w2", jb), pe_w2)

            def _w2bias(e, W=W):
                inst = None
                for oc in range(NCH):
                    inst = e.matmul(w2slot[oc][:, 0:W] if W != 512 else w2slot[oc],
                                    fb2[:, 128 * oc:128 * (oc + 1)],
                                    C["onesrow"][:, 0:W], start=False, stop=True)
                return inst
            pl.step("tensor", _w2bias, waits=[], incs=[(pe_w2, 1)])

            for oc in range(NCH):
                pl.step("vector",
                        lambda e, oc=oc, W=W: e.tensor_tensor(
                            x1g[:, oc, 0:W], xmid[:, oc, 1:W + 1], w2slot[oc][:, 0:W], ALU.add),
                        waits=w(pe_w2, s_hln), incs=[(v_x1, 1)])
            pl.set_gate(("mST", 0), v_x1)
            pl.set_gate(("mST", 1), v_x1)
            pl.set_gate("mTP", v_x1)
            pl.set_gate("mRRA", v_x1)
            pl.set_gate("mRRB", v_x1)
            pl.set_gate("xmid", v_x1)
            pl.set_gate("mbuf", pe_w2)

            if L == 0:
                pl.step("sync",
                        lambda e, t0=t0, W=W: e.dma_start(
                            AP(dr["x1"], t0 - X1T0, [[X1N, 128], [128 * X1N, NCH], [1, W]]),
                            x1g[:, :, 0:W]),
                        waits=w(v_x1), incs=[(d_x1o, 16)])
                pl.set_gate("x1g", d_x1o)
                pl.G["x1g"] = pl.G["x1g"] + w(s_hln)
            else:
                ntt = W // 128
                for tt in range(ntt):
                    tg = t0 + 128 * tt
                    if tg < OUT0:
                        continue
                    b2 = tt % 2
                    pl.step("sync",
                            lambda e, tg=tg, b2=b2: e.dma_start(xacc[:, b2, 0:D], dr["x"][tg:tg + 128, :]),
                            waits=pl.gate(("xacc", b2)), incs=[(d_xin[b2], 16)])
                    pl.step("sync",
                            lambda e, tg=tg, b2=b2: e.dma_start(xacc[:, b2, D:2 * D],
                                                                dr["acc"][tg - OWN:tg - OWN + 128, :]),
                            waits=[], incs=[(d_xin[b2], 16)])

                    def _tp(e, tt=tt):
                        inst = None
                        for oc in range(NCH):
                            inst = e.matmul(TPp[:, 128 * oc:128 * (oc + 1)],
                                            x1g[:, oc, 128 * tt:128 * (tt + 1)], C["idf"][:, :],
                                            start=True, stop=True)
                        return inst
                    pl.step("tensor", _tp, waits=w(v_x1) + pl.gate("mTP"), incs=[(pe_tp, 1)])
                    pl.set_gate("x1g", pe_tp)
                    pl.G["x1g"] = pl.G["x1g"] + w(s_hln)

                    def _fin(e, b2=b2):
                        e.tensor_tensor(xacc[:, b2, 0:D], xacc[:, b2, 0:D], TPp[:, :], ALU.add)
                        return e.tensor_tensor(xacc[:, b2, 0:D], xacc[:, b2, 0:D],
                                               xacc[:, b2, D:2 * D], ALU.add)
                    pl.step("vector", _fin, waits=w(pe_tp) + w(d_xin[b2]), incs=[(v_fin2, 1)])
                    pl.set_gate("mTP", v_fin2)
                    pl.step("sync",
                            lambda e, tg=tg, b2=b2: e.dma_start(dr["out"][tg - OWN:tg - OWN + 128, :],
                                                                xacc[:, b2, 0:D]),
                            waits=w(v_fin2), incs=[(d_oo[b2], 16)])
                    pl.set_gate(("xacc", b2), d_oo[b2])

        if L == 0:
            pl.step("sync", lambda e: e.nop(), waits=w(d_x1o))
        else:
            pl.step("sync", lambda e: e.nop(), waits=w(d_oo[0]) + w(d_oo[1]))

        with nc.Block() as blk:
            pl.emit(nc, blk)


_NC_CACHE = {}


def kernel(L0_attn_x, L0_attn_w_qkv, L0_attn_b_qkv, L0_attn_w_proj, L0_attn_b_proj,
           L0_ffn_wqkv, L0_ffn_q_scale, L0_ffn_k_scale, L0_ffn_wo, L0_ffn_ln_g,
           L0_ffn_ln_b, L0_ffn_w1, L0_ffn_b1, L0_ffn_w2, L0_ffn_b2):
    x = np.ascontiguousarray(np.asarray(L0_attn_x, dtype=np.float32))
    consts = _host_consts()

    shared = dict(
        wqkv_a=_w_fm(np.asarray(L0_attn_w_qkv, np.float32)),
        bqkv_a=np.asarray(L0_attn_b_qkv, np.float32).reshape(1, 3 * D),
        wproj=_w_fm(np.asarray(L0_attn_w_proj, np.float32)),
        bproj=np.asarray(L0_attn_b_proj, np.float32).reshape(1, D),
        bmask=np.ones((128, 1), np.float32),
        logva=np.zeros((128, 32), np.float32),
        logvf=np.zeros((128, 32), np.float32),
        **consts,
    )
    for l in range(2):
        shared[f"fwqkv{l}"] = _w_fm(np.asarray(L0_ffn_wqkv[l], np.float32))
        shared[f"fwo{l}"] = _w_fm(np.asarray(L0_ffn_wo[l], np.float32))
        w1 = np.asarray(L0_ffn_w1[l], np.float32)
        w1p = np.zeros((D, H2P), np.float32)
        w1p[:, 0:HID] = w1[:, 0:HID]
        w1p[:, HIDP:HIDP + HID] = w1[:, HID:2 * HID]
        shared[f"fw1{l}"] = _w_fm(w1p)
        b1 = np.asarray(L0_ffn_b1[l], np.float32)
        b1p = np.zeros((1, H2P), np.float32)
        b1p[0, 0:HID] = b1[0:HID]
        b1p[0, HIDP:HIDP + HID] = b1[HID:2 * HID]
        shared[f"fb1{l}"] = b1p
        w2 = np.asarray(L0_ffn_w2[l], np.float32)
        w2p = np.zeros((HIDP, D), np.float32)
        w2p[0:HID, :] = w2
        shared[f"fw2{l}"] = _w_fm(w2p)
        shared[f"fb2{l}"] = np.asarray(L0_ffn_b2[l], np.float32).reshape(1, D)
        shared[f"lng{l}"] = np.ascontiguousarray(np.asarray(L0_ffn_ln_g[l], np.float32).reshape(NCH, 128).T)
        shared[f"lnb{l}"] = np.ascontiguousarray(np.asarray(L0_ffn_ln_b[l], np.float32).reshape(NCH, 128).T)
        qv = np.asarray(L0_ffn_q_scale[l], np.float32)
        kv = np.asarray(L0_ffn_k_scale[l], np.float32)
        shared[f"qs{l}"] = np.tile(qv, 2).reshape(128, 1)
        shared[f"qssw{l}"] = np.tile(np.concatenate([qv[32:64], qv[0:32]]), 2).reshape(128, 1)
        shared[f"ks{l}"] = np.tile(kv, 2).reshape(128, 1)
        shared[f"kssw{l}"] = np.tile(np.concatenate([kv[32:64], kv[0:32]]), 2).reshape(128, 1)
        shared[f"qsr{l}"] = np.tile(qv, 2).reshape(1, 128)
        shared[f"ksr{l}"] = np.tile(kv, 2).reshape(1, 128)

    if "nc" not in _NC_CACHE:
        _NC_CACHE["nc"] = build_program()
    nc = _NC_CACHE["nc"]

    in_maps = []
    for c in range(8):
        seq, half = c // 2, c % 2
        if half == 1:
            xc = x[seq]
        else:
            xc = np.concatenate([np.zeros((OWN, D), np.float32), x[seq, 0:OWN]], axis=0)
        real0 = TAB_T0 - (0 if half == 1 else OWN)
        tabs = _rotary_tables(np.arange(real0, real0 + TAB_N))
        lv_a = np.zeros((128, 32), np.float32)
        lv_f = np.zeros((128, 32), np.float32)
        if half == 0:
            lv_a[:, 0:16] = NEG
            lv_f[:, 0:16] = NEG
        m = dict(shared)
        m["x_ctx"] = xc
        m["bmask"] = np.full((128, 1), 0.0 if half == 0 else 1.0, np.float32)
        m["logva"] = lv_a
        m["logvf"] = lv_f
        for i in range(4):
            m[f"tab{i}"] = tabs[i]
            m[f"tab{i}lo"] = tabs[4 + i]
        in_maps.append(m)

    res = run_bass_kernel_spmd(nc, in_maps, core_ids=list(range(8)))
    out = np.empty((B, T, D), res.results[0]["out_own"].dtype)
    for c in range(8):
        seq, half = c // 2, c % 2
        out[seq, half * OWN:(half + 1) * OWN] = res.results[c]["out_own"]
    return out


# version 20
# speedup vs baseline: 1.0143x; 1.0014x over previous
import numpy as np
from contextlib import ExitStack

import concourse.bass as bass
import concourse.mybir as mybir
from concourse.bass_utils import run_bass_kernel_spmd

F32 = mybir.dt.float32
BF16 = mybir.dt.bfloat16
AF = mybir.ActivationFunctionType
ALU = mybir.AluOpType
AX = mybir.AxisListType
AP = bass.AP

B, T, D = 4, 4096, 1024
OWN = 2048
NCH = 8
NHG = 8
FINNER = 512
HID = 2730
HIDP = 2816
H2P = 2 * HIDP
QK_SCALE = 8.0
L1_QKV_C0, L1_ATT_C0 = 6, 10
L2_QKV_C0, L2_ATT_C0 = 11, 15
NCHUNK = 32
TAB_T0 = L1_QKV_C0 * 128
TAB_N = T - TAB_T0
X1T0 = L1_ATT_C0 * 128
X1N = T - X1T0
NEG = -60000.0


class PSem:
    def __init__(self, ctx, nc, name):
        self.h = ctx.enter_context(nc.semaphore(name))
        self.n = 0
        self.name = name
        self.producers = set()


class Plan:
    def __init__(self):
        self.steps = []
        self.G = {}

    def gate(self, *keys):
        out = []
        for k in keys:
            out += self.G.get(k, [])
        return out

    def set_gate(self, key, sem):
        self.G[key] = [(sem, sem.n)]

    def step(self, eng, emit_fn, waits=(), incs=()):
        self.steps.append((eng, [(s, v) for (s, v) in waits if v > 0], emit_fn, list(incs)))
        for s, k in incs:
            s.n += k
            s.producers.add(eng)

    def emit(self, nc, block):
        by_eng = {}
        for eng, ws, fn, incs in self.steps:
            by_eng.setdefault(eng, []).append((ws, fn, incs))

        def mk_body(items, ename):
            def body(e):
                ee = _DrainProxy(e) if ename == "vector" else e
                last = {}
                for ws, fn, incs in items:
                    for s, v in ws:
                        if s.producers == {ename} and ename != "vector":
                            continue
                        if last.get(s.name, -1) >= v:
                            continue
                        e.wait_ge(s.h, v)
                        last[s.name] = v
                    inst = fn(ee)
                    for ii, (s, k) in enumerate(incs):
                        if ii == 0:
                            inst.then_inc(s.h, k)
                        else:
                            e.nop().then_inc(s.h, k)
            return body

        emap = {"sync": mybir.EngineType.SP, "gpsimd": mybir.EngineType.Pool,
                "tensor": mybir.EngineType.PE, "scalar": mybir.EngineType.Activation,
                "vector": mybir.EngineType.DVE}
        for ename in ("sync", "gpsimd", "tensor", "scalar", "vector"):
            if ename in by_eng:
                block._start_engine(mk_body(by_eng[ename], ename), emap[ename])


class _DrainProxy:

    _OPS = {"tensor_copy", "tensor_tensor", "tensor_add", "tensor_sub", "tensor_mul",
            "tensor_max", "tensor_scalar", "tensor_single_scalar", "tensor_scalar_add",
            "tensor_scalar_sub", "tensor_scalar_mul", "tensor_scalar_max",
            "tensor_scalar_min", "tensor_relu", "tensor_reduce", "reduce_max",
            "reduce_sum", "reciprocal", "memset", "memzero", "select", "transpose",
            "bn_stats", "bn_aggr", "stream_shuffle", "pool", "pool_avg", "pool_max"}

    def __init__(self, e):
        self._e = e

    def __getattr__(self, name):
        f = getattr(self._e, name)
        if name in _DrainProxy._OPS:
            def g(*a, **k):
                self._e.drain()
                return f(*a, **k)
            return g
        return f


def w(*sems):
    return [(s, s.n) for s in sems]


def wb(s, k):
    return [(s, max(0, s.n - k))]


def _rotary_tables(real_pos):
    tf = real_pos.astype(np.float32)
    try:
        import jax
        import jax.numpy as jnp
        with jax.default_device(jax.devices("cpu")[0]):
            half_idx = jnp.arange(0, 64, 2, dtype=jnp.float32)
            inv_freq = 1.0 / 10000.0 ** (half_idx / 64)
            t = jnp.asarray(tf)
            freqs = t[:, None] * inv_freq[None, :]
            base = (half_idx + 0.4 * 64) / (1.4 * 64)
            power = (t - T // 2) / float(512 // 2)
            scale = base[None, :] ** power[:, None]
            cos, sin = jnp.cos(freqs), jnp.sin(freqs)
            cos, sin, scale = (np.asarray(a, np.float64) for a in (cos, sin, scale))
    except Exception:
        half_idx = np.arange(0, 64, 2, dtype=np.float32)
        inv_freq = np.float32(1.0) / np.power(np.float32(10000.0), half_idx / np.float32(64.0))
        freqs = tf[:, None] * inv_freq[None, :]
        base = (half_idx + np.float32(0.4 * 64)) / np.float32(1.4 * 64)
        power = (tf - np.float32(T // 2)) / np.float32(512 // 2)
        scale = np.power(base[None, :], power[:, None], dtype=np.float32)
        cos, sin, scale = (a.astype(np.float64) for a in (np.cos(freqs), np.sin(freqs), scale))
    cos = np.concatenate([cos, cos], axis=1)
    sin = np.concatenate([sin, sin], axis=1)
    scale = np.concatenate([scale, scale], axis=1)
    his, los = [], []
    for a in (cos * scale, sin * scale, cos / scale, sin / scale):
        a = np.concatenate([a.T, a.T], axis=0)
        hi = a.astype(np.float32)
        lo = (a - hi.astype(np.float64)).astype(np.float32)
        his.append(np.ascontiguousarray(hi))
        los.append(np.ascontiguousarray(lo))
    return his + los


def _host_consts():
    c = {}
    c["identc"] = np.eye(128, dtype=np.float32)
    r = np.zeros((128, 128), dtype=np.float32)
    for blk in range(2):
        o = 64 * blk
        for j in range(32):
            r[o + j + 32, o + j] = -1.0
            r[o + j, o + j + 32] = 1.0
    c["r2sc"] = r
    kk, qq = np.meshgrid(np.arange(128), np.arange(128), indexing="ij")
    c["tric"] = (qq >= kk).astype(np.float32)
    c["antic"] = (kk >= qq).astype(np.float32)
    ha = np.zeros((1, 128), np.float32)
    ha[0, 0:64] = 1.0
    hb = np.zeros((1, 128), np.float32)
    hb[0, 64:128] = 1.0
    c["halfac"] = ha
    c["halfbc"] = hb
    return c


def _w_fm(wmat):
    K, N = wmat.shape
    return np.ascontiguousarray(wmat.reshape(K // 128, 128, N).transpose(1, 0, 2))


def build_program():
    nc = bass.Bass("TRN2", target_bir_lowering=False, detect_race_conditions=False)
    es = ExitStack()

    dram = {}

    def din(name, shape):
        dram[name] = nc.dram_tensor(name, shape, F32, kind="ExternalInput")

    din("x_ctx", [T, D])
    din("wqkv_a", [128, NCH, 3 * D])
    din("bqkv_a", [1, 3 * D])
    din("wproj", [128, NCH, D])
    din("bproj", [1, D])
    for l in range(2):
        din(f"fwqkv{l}", [128, NCH, 3 * FINNER])
        din(f"fwo{l}", [128, 4, D])
        din(f"fw1{l}", [128, NCH, H2P])
        din(f"fb1{l}", [1, H2P])
        din(f"fw2{l}", [128, HIDP // 128, D])
        din(f"fb2{l}", [1, D])
        din(f"lng{l}", [128, NCH])
        din(f"lnb{l}", [128, NCH])
        din(f"qs{l}", [128, 1])
        din(f"qssw{l}", [128, 1])
        din(f"ks{l}", [128, 1])
        din(f"kssw{l}", [128, 1])
        din(f"qsr{l}", [1, 128])
        din(f"ksr{l}", [1, 128])
    for i in range(4):
        din(f"tab{i}", [128, TAB_N])
        din(f"tab{i}lo", [128, TAB_N])
    for n in ("identc", "r2sc", "tric", "antic"):
        din(n, [128, 128])
    din("halfac", [1, 128])
    din("halfbc", [1, 128])
    din("bmask", [128, 1])
    din("logva", [128, 32])
    din("logvf", [128, 32])

    d_out = nc.dram_tensor("out_own", [OWN, D], F32, kind="ExternalOutput")
    d_acc = nc.dram_tensor("acc_scr", [OWN, D], F32, kind="Internal")
    d_x1 = nc.dram_tensor("x1_scr", [NCH, 128, X1N], F32, kind="Internal")

    sb = lambda name, shape, dt: es.enter_context(nc.sbuf_tensor(name, shape, dt))

    C = {}
    C["idb"] = sb("idb", [128, 128], BF16)
    C["idf"] = sb("idf", [128, 128], F32)
    C["r2sb"] = sb("r2sb", [128, 128], BF16)
    C["trib"] = sb("trib", [128, 128], BF16)
    C["antib"] = sb("antib", [128, 128], BF16)
    C["halfa"] = sb("halfa", [1, 128], F32)
    C["halfb"] = sb("halfb", [1, 128], F32)
    C["bmask"] = sb("bmask_s", [128, 1], F32)
    C["logva"] = sb("logva_s", [128, 32], F32)
    C["logvf"] = sb("logvf_s", [128, 32], F32)
    C["onesrow"] = sb("onesrow", [1, 512], BF16)
    C["onesf1"] = sb("onesf1", [1, 128], F32)
    C["onescol"] = sb("onescol", [128, 1], F32)
    per_layer = {}
    for l in range(2):
        pc = {}
        pc["lng"] = sb(f"lng_s{l}", [128, NCH], F32)
        pc["lnb"] = sb(f"lnb_s{l}", [128, NCH], F32)
        pc["qs"] = sb(f"qs_s{l}", [128, 1], F32)
        pc["qssw"] = sb(f"qssw_s{l}", [128, 1], F32)
        pc["ks"] = sb(f"ks_s{l}", [128, 1], F32)
        pc["kssw"] = sb(f"kssw_s{l}", [128, 1], F32)
        pc["qsr"] = sb(f"qsr_s{l}", [1, 128], F32)
        pc["ksr"] = sb(f"ksr_s{l}", [1, 128], F32)
        per_layer[l] = pc

    sems = {}

    def S(name):
        if name not in sems:
            sems[name] = PSem(es, nc, name)
        return sems[name]

    with nc.sbuf_tensor("xn", [128, NCH, T], BF16) as xn:
        emit_p0(nc, S, dram, xn, C, per_layer)
        emit_p1(nc, S, dram, xn, C, d_acc)
        cl = dict(C)
        cl.update(per_layer[0])
        dr0 = dict(fwqkv=dram["fwqkv0"], fwo=dram["fwo0"], fw1=dram["fw10"],
                   fw2=dram["fw20"], fb1=dram["fb10"], fb2=dram["fb20"], tabs=[dram[f"tab{i}"] for i in range(4)],
                   tabslo=[dram[f"tab{i}lo"] for i in range(4)],
                   x1=d_x1, x=dram["x_ctx"], acc=d_acc, out=d_out, lname="0")
        emit_ffn_layer(nc, S, 0, xn, cl, dr0)
    cl = dict(C)
    cl.update(per_layer[1])
    dr1 = dict(fwqkv=dram["fwqkv1"], fwo=dram["fwo1"], fw1=dram["fw11"],
               fw2=dram["fw21"], fb1=dram["fb11"], fb2=dram["fb21"], tabs=[dram[f"tab{i}"] for i in range(4)],
               tabslo=[dram[f"tab{i}lo"] for i in range(4)],
               x1=d_x1, x=dram["x_ctx"], acc=d_acc, out=d_out, lname="1")
    emit_ffn_layer(nc, S, 1, None, cl, dr1)

    es.close()
    return nc


def emit_p0(nc, S, dram, xn, C, per_layer):
    with (
        nc.sbuf_tensor("p0_xt", [128, 2, D], F32) as xt,
        nc.sbuf_tensor("p0_xnt", [128, 2, D], BF16) as xnt,
        nc.sbuf_tensor("p0_sq", [128, 2, D], F32) as sq0,
        nc.sbuf_tensor("p0_st", [128, 16], F32) as st8,
        nc.psum_tensor("p0_TPa", [128, 1024], F32) as TPa,
        nc.psum_tensor("p0_TPb", [128, 1024], F32) as TPb,
    ):
        TP = [TPa, TPb]
        pl = Plan()
        dxs = [S("p0x0"), S("p0x1")]
        dcon, dwq = S("p0c"), S("p0w")
        s_sq, v_st, s_sd, v_rs, s_nm = S("p0sq"), S("p0vst"), S("p0sd"), S("p0vrs"), S("p0nm")
        pe_T, v_ev = S("p0T"), S("p0ev")

        def cl(dst, src):
            pl.step("sync", lambda e, d=dst, s=src: e.dma_start(d, s), incs=[(dcon, 16)])

        cl(C["bmask"][:, :], dram["bmask"][:, :])
        cl(C["logva"][:, :], dram["logva"][:, :])
        cl(C["logvf"][:, :], dram["logvf"][:, :])
        cl(C["idf"][:, :], dram["identc"][:, :])
        cl(C["halfa"][:, :], dram["halfac"][:, :])
        cl(C["halfb"][:, :], dram["halfbc"][:, :])
        for l in range(2):
            for nm in ("lng", "lnb", "qs", "qssw", "ks", "kssw", "qsr", "ksr"):
                cl(per_layer[l][nm][:, :], dram[f"{nm}{l}"][:, :])

        def gl(dst, src):
            pl.step("gpsimd", lambda e, d=dst, s=src: e.dma_start(d, s), incs=[(dwq, 16)])

        gl(C["idb"][:, :], dram["identc"][:, :])
        gl(C["r2sb"][:, :], dram["r2sc"][:, :])
        gl(C["trib"][:, :], dram["tric"][:, :])
        gl(C["antib"][:, :], dram["antic"][:, :])

        pl.step("vector", lambda e: e.memset(C["onesrow"][:, :], 1.0), incs=[(v_ev, 1)])
        pl.step("vector", lambda e: e.memset(C["onesf1"][:, :], 1.0), incs=[(v_ev, 1)])
        pl.step("vector", lambda e: e.memset(C["onescol"][:, :], 1.0), incs=[(v_ev, 1)])

        for i in range(32):
            b = i % 2
            t0 = 128 * i
            pl.step("sync",
                    lambda e, b=b, t0=t0: e.dma_start(xt[:, b, :], dram["x_ctx"][t0:t0 + 128, :]),
                    waits=pl.gate(("xt", b)), incs=[(dxs[b], 16)])
            pl.step("vector",
                    lambda e, b=b: e.reduce_sum(st8[:, 8 * b + 0:8 * b + 1], xt[:, b, :], axis=AX.X),
                    waits=w(dxs[b]))
            pl.step("scalar",
                    lambda e, b=b: e.activation(sq0[:, b, :], xt[:, b, :], AF.Square,
                                                accum_out=st8[:, 8 * b + 1:8 * b + 2]),
                    waits=w(dxs[b]), incs=[(s_sq, 1)])

            def _stats(e, b=b):
                c0 = 8 * b
                e.tensor_scalar_mul(st8[:, c0 + 2:c0 + 3], st8[:, c0 + 0:c0 + 1], 1.0 / D)
                e.tensor_scalar_mul(st8[:, c0 + 3:c0 + 4], st8[:, c0 + 1:c0 + 2], 1.0 / D)
                e.tensor_tensor(st8[:, c0 + 4:c0 + 5], st8[:, c0 + 2:c0 + 3], st8[:, c0 + 2:c0 + 3], ALU.mult)
                e.tensor_tensor(st8[:, c0 + 5:c0 + 6], st8[:, c0 + 3:c0 + 4], st8[:, c0 + 4:c0 + 5], ALU.subtract)
                return e.tensor_scalar_add(st8[:, c0 + 5:c0 + 6], st8[:, c0 + 5:c0 + 6], 1e-5)
            pl.step("vector", _stats, waits=w(s_sq), incs=[(v_st, 1)])
            pl.step("scalar",
                    lambda e, b=b: e.activation(st8[:, 8 * b + 6:8 * b + 7], st8[:, 8 * b + 5:8 * b + 6], AF.Sqrt),
                    waits=w(v_st), incs=[(s_sd, 1)])

            def _rs(e, b=b):
                c0 = 8 * b
                e.reciprocal(st8[:, c0 + 7:c0 + 8], st8[:, c0 + 6:c0 + 7])
                return e.tensor_scalar(st8[:, c0 + 4:c0 + 5], st8[:, c0 + 2:c0 + 3],
                                       st8[:, c0 + 7:c0 + 8], -1.0, ALU.mult, ALU.mult)
            pl.step("vector", _rs, waits=w(s_sd), incs=[(v_rs, 1)])
            pl.step("scalar",
                    lambda e, b=b: e.activation(xnt[:, b, :], xt[:, b, :], AF.Identity,
                                                bias=st8[:, 8 * b + 4:8 * b + 5],
                                                scale=st8[:, 8 * b + 7:8 * b + 8]),
                    waits=w(v_rs), incs=[(s_nm, 1)])
            pl.set_gate(("xt", b), s_nm)

            def _tr(e, b=b):
                inst = None
                for c in range(8):
                    inst = e.matmul(TP[b][:, 128 * c:128 * (c + 1)],
                                    xnt[:, b, 128 * c:128 * (c + 1)], C["idb"][:, :],
                                    start=True, stop=True)
                return inst
            pl.step("tensor", _tr, waits=w(s_nm, dwq) + pl.gate(("TP", b)), incs=[(pe_T, 1)])
            def _evx(e, b=b, t0=t0):
                inst = None
                for c in range(8):
                    inst = e.tensor_copy(xn[:, c, t0:t0 + 128], TP[b][:, 128 * c:128 * (c + 1)])
                return inst
            pl.step("vector", _evx, waits=w(pe_T), incs=[(v_ev, 1)])
            pl.set_gate(("TP", b), v_ev)

        pl.step("sync", lambda e: e.nop(), waits=w(dcon))
        pl.step("gpsimd", lambda e: e.nop(), waits=w(dwq))

        with nc.Block() as blk:
            pl.emit(nc, blk)


def emit_p1(nc, S, dram, xn, C, d_acc):
    with ExitStack() as xs:
        wqsl = xs.enter_context(nc.sbuf_tensor("p1_wq", [128, 2, 3, NCH, 128], BF16))
        bqkv = xs.enter_context(nc.sbuf_tensor("p1_bq", [1, 3 * D], BF16))
        bproj = xs.enter_context(nc.sbuf_tensor("p1_bp", [1, D], BF16))
        Kfm = xs.enter_context(nc.sbuf_tensor("p1_K", [128, T], BF16))
        Vtm = xs.enter_context(nc.sbuf_tensor("p1_V", [128, NCHUNK * 130], BF16))
        Qfm = xs.enter_context(nc.sbuf_tensor("p1_Q", [128, OWN], BF16))
        vtmp = xs.enter_context(nc.sbuf_tensor("p1_vt", [128, 512], BF16))
        PT = xs.enter_context(nc.sbuf_tensor("p1_PT", [128, 2, 1024], BF16))
        bb = xs.enter_context(nc.sbuf_tensor("p1_bb", [128, 40], F32))
        rl = xs.enter_context(nc.sbuf_tensor("p1_rl", [128, 1024], F32))
        ofm = xs.enter_context(nc.sbuf_tensor("p1_o", [128, NHG * OWN], BF16))
        wps = xs.enter_context(nc.sbuf_tensor("p1_wp", [128, NCH, D], BF16))
        stg = xs.enter_context(nc.sbuf_tensor("p1_stg", [128, 2, D], F32))
        ST0 = xs.enter_context(nc.psum_tensor("p1_ST0", [128, 1024], F32))
        ST1 = xs.enter_context(nc.psum_tensor("p1_ST1", [128, 1024], F32))
        AVA = xs.enter_context(nc.psum_tensor("p1_AVA", [128, 512], F32))
        AVB = xs.enter_context(nc.psum_tensor("p1_AVB", [128, 512], F32))
        RRA = xs.enter_context(nc.psum_tensor("p1_RRA", [128, 512], F32))
        RRB = xs.enter_context(nc.psum_tensor("p1_RRB", [128, 512], F32))

        STb = [ST0, ST1]
        pl = Plan()
        d_c1 = S("p1c1")
        d_w = [S("p1w0"), S("p1w1")]
        pe_mm, ev_q = S("p1mm"), S("p1ev")
        pe_st, s_exp, v_msk, pe_av = S("p1st"), S("p1exp"), S("p1msk"), S("p1av")
        v_rl, pe_rr, v_on = S("p1rl"), S("p1rr"), S("p1on")
        v_bnd, pe_bnd = S("p1bnd"), S("p1pebnd")
        v_stg = S("p1stg")
        d_o1 = [S("p1out0"), S("p1out1")]

        pl.step("gpsimd", lambda e: e.dma_start(bqkv[:, :], dram["bqkv_a"][:, :]), incs=[(d_c1, 16)])
        pl.step("gpsimd", lambda e: e.dma_start(bproj[:, :], dram["bproj"][:, :]), incs=[(d_c1, 16)])
        pl.step("gpsimd", lambda e: e.dma_start(wps[:, :, :], dram["wproj"][:, :, :]), incs=[(d_c1, 16)])
        pl.step("vector", lambda e: e.memset(AP(Vtm, 64, [[130 * NCHUNK, 128], [130, NCHUNK]]), 1.0),
                incs=[(v_on, 1)])
        pl.set_gate("ones", v_on)

        wq_ready = {}

        def prefetch(hg):
            hb = hg % 2
            for s3, col in ((0, hg * 128), (1, D + hg * 128), (2, 2 * D + hg * 128)):
                pl.step("gpsimd",
                        lambda e, hb=hb, s3=s3, col=col: e.dma_start(
                            wqsl[:, hb, s3, :, :],
                            AP(dram["wqkv_a"], col, [[NCH * 3 * D, 128], [3 * D, NCH], [1, 128]])),
                        waits=(w(pe_mm) if hg >= 2 else []), incs=[(d_w[hb], 16)])
            wq_ready[hg] = (d_w[hg % 2], d_w[hg % 2].n)

        prefetch(0)
        prefetch(1)
        for hg in range(NHG):
            hb = hg % 2

            def build_fm(dst, s3, t0s, own_off, col):
                for gi, t0 in enumerate(t0s):
                    ps = STb[gi % 2]

                    def _mm(e, t0=t0, s3=s3, col=col, ps=ps, hb=hb):
                        for c in range(NCH):
                            e.matmul(ps[:, 0:512], wqsl[:, hb, s3, c, :],
                                     xn[:, c, t0:t0 + 512], start=(c == 0), stop=False)
                        return e.matmul(ps[:, 0:512], bqkv[:, col:col + 128],
                                        C["onesrow"][:, :], start=False, stop=True)
                    pl.step("tensor", _mm,
                            waits=[wq_ready[hg]] + w(d_c1) + pl.gate(("ST", gi % 2)) + pl.gate("KQV"),
                            incs=[(pe_mm, 1)])
                    pl.step("scalar",
                            lambda e, dst=dst, t0=t0, ps=ps, oo=own_off:
                            e.copy(dst[:, t0 - oo:t0 - oo + 512], ps[:, 0:512]),
                            waits=w(pe_mm), incs=[(ev_q, 1)])
                    pl.set_gate(("ST", gi % 2), ev_q)

            build_fm(Kfm, 1, [512 * g for g in range(8)], 0, D + hg * 128)
            build_fm(Qfm, 0, [OWN + 512 * g for g in range(4)], OWN, hg * 128)

            for gi in range(8):
                t0 = 512 * gi
                ps = STb[gi % 2]
                av = [AVA, AVB][gi % 2]

                def _mmv(e, t0=t0, ps=ps, hb=hb, hg=hg):
                    for c in range(NCH):
                        e.matmul(ps[:, 0:512], wqsl[:, hb, 2, c, :],
                                 xn[:, c, t0:t0 + 512], start=(c == 0), stop=False)
                    return e.matmul(ps[:, 0:512], bqkv[:, 2 * D + hg * 128:2 * D + hg * 128 + 128],
                                    C["onesrow"][:, :], start=False, stop=True)
                pl.step("tensor", _mmv, waits=pl.gate(("ST", gi % 2)) + pl.gate("KQV") + pl.gate("vtmp"),
                        incs=[(pe_mm, 1)])
                pl.step("scalar", lambda e, ps=ps: e.copy(vtmp[:, :], ps[:, 0:512]),
                        waits=w(pe_mm), incs=[(ev_q, 1)])
                pl.set_gate(("ST", gi % 2), ev_q)

                def _trv(e, av=av):
                    inst = None
                    for j in range(4):
                        inst = e.matmul(av[:, 128 * j:128 * (j + 1)], vtmp[:, 128 * j:128 * (j + 1)],
                                        C["idb"][:, :], start=True, stop=True)
                    return inst
                pl.step("tensor", _trv, waits=w(ev_q) + pl.gate(("AV", gi % 2)) + pl.gate("ones"),
                        incs=[(pe_mm, 1)])
                pl.set_gate("vtmp", pe_mm)

                def _evv(e, gi=gi, av=av):
                    inst = None
                    for j in range(4):
                        base = (gi * 4 + j) * 130
                        inst = e.tensor_copy(AP(Vtm, base, [[130 * NCHUNK, 128], [1, 64]]),
                                             av[:, 128 * j:128 * j + 64])
                        inst = e.tensor_copy(AP(Vtm, base + 65, [[130 * NCHUNK, 128], [1, 64]]),
                                             av[:, 128 * j + 64:128 * j + 128])
                    return inst
                pl.step("vector", _evv, waits=w(pe_mm), incs=[(v_on, 1), (ev_q, 1)])
                pl.set_gate(("AV", gi % 2), v_on)

            if hg + 2 < NHG:
                prefetch(hg + 2)

            pl.step("vector", lambda e: e.reduce_max(bb[:, 33:34], Kfm[:, :], axis=AX.X,
                                                     apply_absolute_value=True),
                    waits=w(ev_q, s_exp), incs=[(v_bnd, 1)])
            pl.step("vector", lambda e: e.reduce_max(bb[:, 34:35], Qfm[:, :], axis=AX.X,
                                                     apply_absolute_value=True),
                    waits=w(ev_q), incs=[(v_bnd, 1)])
            pl.step("vector", lambda e: e.tensor_tensor(bb[:, 35:36], bb[:, 33:34], bb[:, 34:35], ALU.mult),
                    waits=w(v_bnd), incs=[(v_bnd, 1)])
            pl.step("tensor", lambda e: e.matmul(RRA[0:1, 0:1], C["onescol"][:, :], bb[:, 35:36],
                                                 start=True, stop=True),
                    waits=w(v_bnd) + pl.gate("RRA"), incs=[(pe_bnd, 1)])
            pl.step("vector", lambda e: e.tensor_copy(bb[0:1, 36:37], RRA[0:1, 0:1]),
                    waits=w(pe_bnd), incs=[(v_bnd, 1)])
            pl.set_gate("RRA", v_bnd)
            pl.step("tensor", lambda e: e.matmul(RRB[:, 0:1], C["onesf1"][:, :], bb[0:1, 36:37],
                                                 start=True, stop=True),
                    waits=w(v_bnd) + pl.gate("RRB"), incs=[(pe_bnd, 1)])

            def _fixb(e):
                e.tensor_scalar(bb[:, 37:38], RRB[:, 0:1], -0.125, 40.0, ALU.mult, ALU.add)
                return e.tensor_scalar_add(bb[:, 0:32], C["logva"][:, :], bb[:, 37:38])
            pl.step("vector", _fixb, waits=w(pe_bnd), incs=[(v_bnd, 1)])
            pl.set_gate("RRB", v_bnd)

            for qg in range(4):
                q0 = 512 * qg
                qt0 = 16 + 4 * qg
                kcmax = qt0 + 4
                for kc in range(kcmax):
                    bu = kc % 2
                    stp = STb[bu]

                    def _st(e, kc=kc, q0=q0, stp=stp):
                        e.matmul(stp[:, 0:512], Kfm[0:64, 128 * kc:128 * (kc + 1)],
                                 Qfm[0:64, q0:q0 + 512], start=True, stop=True)
                        return e.matmul(stp[:, 512:1024], Kfm[64:128, 128 * kc:128 * (kc + 1)],
                                        Qfm[64:128, q0:q0 + 512], start=True, stop=True)
                    pl.step("tensor", _st, waits=pl.gate(("ST", bu)) + w(v_bnd), incs=[(pe_st, 1)])
                    pl.step("scalar",
                            lambda e, bu=bu, kc=kc, stp=stp: e.activation(
                                PT[:, bu, :], stp[:, :], AF.Exp,
                                bias=bb[:, kc:kc + 1], scale=0.125),
                            waits=w(pe_st) + pl.gate(("PT", bu)), incs=[(s_exp, 1)])
                    pl.set_gate(("ST", bu), s_exp)
                    need_mask = kc >= qt0
                    if need_mask:
                        def _mask(e, kc=kc, qt0=qt0, bu=bu):
                            inst = None
                            for h in range(2):
                                for i4 in range(4):
                                    qt = qt0 + i4
                                    sl = PT[:, bu, 512 * h + 128 * i4: 512 * h + 128 * (i4 + 1)]
                                    if kc == qt:
                                        inst = e.tensor_tensor(sl, sl, C["trib"][:, :], ALU.mult)
                                    elif kc > qt:
                                        inst = e.memset(sl, 0.0)
                            return inst
                        pl.step("vector", _mask, waits=w(s_exp), incs=[(v_msk, 1)])

                    def _av(e, kc=kc, bu=bu, first=(kc == 0), last=(kc == kcmax - 1)):
                        e.matmul(AVA[0:65, :], Vtm[:, 130 * kc:130 * kc + 65], PT[:, bu, 0:512],
                                 start=first, stop=last)
                        e.matmul(AVB[0:1, :], Vtm[:, 130 * kc + 64:130 * kc + 65], PT[:, bu, 512:1024],
                                 start=first, stop=last)
                        return e.matmul(AVB[64:128, :], Vtm[:, 130 * kc + 65:130 * kc + 129],
                                        PT[:, bu, 512:1024], start=first, stop=last)
                    avw = (w(v_msk) if need_mask else w(s_exp))
                    if kc == 0:
                        avw = avw + pl.gate(("AV", 0)) + pl.gate(("AV", 1))
                    pl.step("tensor", _av, waits=avw, incs=[(pe_av, 1)])
                    pl.set_gate(("PT", bu), pe_av)

                def _rlf(e):
                    e.copy(rl[0:1, 0:512], AVA[64:65, :])
                    return e.copy(rl[0:1, 512:1024], AVB[0:1, :])
                pl.step("scalar", _rlf, waits=w(pe_av), incs=[(v_rl, 1)])

                def _rep(e):
                    e.matmul(RRA[:, :], C["onesf1"][:, :], rl[0:1, 0:512], start=True, stop=True)
                    return e.matmul(RRB[:, :], C["onesf1"][:, :], rl[0:1, 512:1024], start=True, stop=True)
                pl.step("tensor", _rep, waits=w(v_rl) + pl.gate("RRA") + pl.gate("RRB"), incs=[(pe_rr, 1)])

                def _rcp(e):
                    e.tensor_scalar_add(rl[:, 0:512], RRA[:, :], 1e-38)
                    e.reciprocal(rl[:, 0:512], rl[:, 0:512])
                    e.tensor_scalar_add(rl[:, 512:1024], RRB[:, :], 1e-38)
                    return e.reciprocal(rl[:, 512:1024], rl[:, 512:1024])
                pl.step("vector", _rcp, waits=w(pe_rr), incs=[(v_rl, 1)])
                pl.set_gate("RRA", v_rl)
                pl.set_gate("RRB", v_rl)

                def _norm(e, hg=hg, q0=q0):
                    o0 = hg * OWN + q0
                    e.tensor_tensor(ofm[0:64, o0:o0 + 512], AVA[0:64, :], rl[0:64, 0:512], ALU.mult)
                    return e.tensor_tensor(ofm[64:128, o0:o0 + 512], AVB[64:128, :],
                                           rl[64:128, 512:1024], ALU.mult)
                pl.step("vector", _norm, waits=w(v_rl), incs=[(v_on, 1), (v_rl, 1)])
                pl.set_gate(("AV", 0), v_on)
                pl.set_gate(("AV", 1), v_on)
            pl.set_gates = None
            pl.G["KQV"] = w(pe_st) + w(pe_av) + w(s_exp)

        for tt in range(16):
            bu = tt % 2
            ps = STb[bu]

            def _pmm(e, tt=tt, ps=ps):
                inst = None
                for half in range(2):
                    for c in range(NCH):
                        e.matmul(ps[:, 512 * half:512 * (half + 1)],
                                 ofm[:, c * OWN + 128 * tt: c * OWN + 128 * (tt + 1)],
                                 wps[:, c, 512 * half:512 * (half + 1)],
                                 start=(c == 0), stop=False)
                    inst = e.matmul(ps[:, 512 * half:512 * (half + 1)],
                                    C["onesrow"][:, 0:128], bproj[:, 512 * half:512 * (half + 1)],
                                    start=False, stop=True)
                return inst
            pl.step("tensor", _pmm, waits=w(v_on) + pl.gate(("ST", bu)), incs=[(pe_mm, 1)])
            pl.step("vector", lambda e, bu=bu, ps=ps: e.tensor_copy(stg[:, bu, :], ps[:, :]),
                    waits=w(pe_mm) + pl.gate(("stg", bu)), incs=[(v_stg, 1)])
            pl.set_gate(("ST", bu), v_stg)
            pl.step("sync", lambda e, tt=tt, bu=bu: e.dma_start(d_acc[128 * tt:128 * (tt + 1), :], stg[:, bu, :]),
                    waits=w(v_stg), incs=[(d_o1[bu], 16)])
            pl.set_gate(("stg", bu), d_o1[bu])
        pl.step("sync", lambda e: e.nop(), waits=w(d_o1[0]) + w(d_o1[1]))

        with nc.Block() as blk:
            pl.emit(nc, blk)


def emit_ffn_layer(nc, S, L, xn, C, dr):
    QC0 = L1_QKV_C0 if L == 0 else L2_QKV_C0
    AC0 = L1_ATT_C0 if L == 0 else L2_ATT_C0
    DOMQ = (NCHUNK - QC0) * 128
    DOMA = (NCHUNK - AC0) * 128
    QT0 = QC0 * 128
    AT0 = AC0 * 128
    NKC = NCHUNK - QC0

    qgs = []
    c = AC0
    while c < NCHUNK:
        qgs.append((c, min(c + 4, NCHUNK)))
        c = min(c + 4, NCHUNK)

    with ExitStack() as outer:
        off = outer.enter_context(nc.sbuf_tensor(f"f{L}_o", [128, 4 * DOMA], BF16))
        _ffn_attn(nc, S, L, xn, C, dr, off, QC0, AC0, DOMQ, DOMA, QT0, NKC, qgs)
        _ffn_mlp(nc, S, L, xn, C, dr, off, AC0, DOMA, AT0, qgs)


def _ffn_attn(nc, S, L, xn, C, dr, off, QC0, AC0, DOMQ, DOMA, QT0, NKC, qgs):
    with ExitStack() as xs:
        srcbf = None
        if L == 1:
            srcbf = xs.enter_context(nc.sbuf_tensor(f"f{L}_src", [128, NCH, DOMQ + 128], BF16))
        wqf = xs.enter_context(nc.sbuf_tensor(f"f{L}_wq", [128, NCH, 3 * FINNER], BF16))
        Dt = xs.enter_context(nc.sbuf_tensor(f"f{L}_Dt", [128, 4, TAB_N], F32))
        qk = xs.enter_context(nc.sbuf_tensor(f"f{L}_qk", [128, 2, DOMQ], BF16))
        Vtf = xs.enter_context(nc.sbuf_tensor(f"f{L}_V", [128, NKC * 130], BF16))
        PTf = xs.enter_context(nc.sbuf_tensor(f"f{L}_PT", [128, 2, 1024], BF16))
        bbf = xs.enter_context(nc.sbuf_tensor(f"f{L}_bb", [128, 40], F32))
        rlw = xs.enter_context(nc.sbuf_tensor(f"f{L}_rl", [128, 1024], F32))
        vtmpf = xs.enter_context(nc.sbuf_tensor(f"f{L}_vt", [128, 512], BF16))
        q2 = xs.enter_context(nc.sbuf_tensor(f"f{L}_q2", [128, 512], F32))
        sst = xs.enter_context(nc.sbuf_tensor(f"f{L}_sst", [1, 2, 512], F32))
        tbuf = xs.enter_context(nc.sbuf_tensor(f"f{L}_tb", [128, 2, 512], F32))
        ST0 = xs.enter_context(nc.psum_tensor(f"f{L}_ST0", [128, 1024], F32))
        ST1 = xs.enter_context(nc.psum_tensor(f"f{L}_ST1", [128, 1024], F32))
        AVA = xs.enter_context(nc.psum_tensor(f"f{L}_AVA", [128, 512], F32))
        AVB = xs.enter_context(nc.psum_tensor(f"f{L}_AVB", [128, 512], F32))
        RRA = xs.enter_context(nc.psum_tensor(f"f{L}_RRA", [128, 512], F32))
        RRB = xs.enter_context(nc.psum_tensor(f"f{L}_RRB", [128, 512], F32))

        STb = [ST0, ST1]
        pl = Plan()
        d_w, d_tab, d_src = S("fAw"), S("fAtab"), S("fAsrc")
        d_tlo, v_tlo = S("fAtlo"), S("fAvtlo")
        v_dt = S("fAdt")
        pe_mm, ev_q, s_ev = S("fAmm"), S("fAev"), S("fAsev")
        pe_rot, g_fin, v_fin = S("fArot"), S("fAgfin"), S("fAfin")
        pe_ssq, v_ssq, s_ssq = S("fApssq"), S("fAvssq"), S("fAsssq")
        pe_rep = S("fArep")
        pe_st, s_exp, v_msk, pe_av = S("fAst"), S("fAexp"), S("fAmsk"), S("fAav")
        v_rl, pe_rr, v_on = S("fArl"), S("fArr"), S("fAon")
        v_bnd, pe_bnd = S("fAbnd"), S("fApebnd")
        v_bfx = S("fAbfx")

        pl.step("gpsimd", lambda e: e.dma_start(wqf[:, :, :], dr["fwqkv"][:, :, :]), incs=[(d_w, 16)])
        for i in range(4):
            pl.step("sync", lambda e, i=i: e.dma_start(Dt[:, i, :], dr["tabs"][i][:, :]),
                    incs=[(d_tab, 16)])
        qkst = AP(qk, 0, [[2 * DOMQ, 128], [1, TAB_N]])
        for i in range(4):
            pl.step("gpsimd", lambda e, i=i: e.dma_start(qkst, dr["tabslo"][i][:, :]),
                    waits=pl.gate("tlo"), incs=[(d_tlo, 16)])
            pl.step("vector",
                    lambda e, i=i: e.tensor_tensor(Dt[:, i, :], Dt[:, i, :], qkst, ALU.add),
                    waits=w(d_tab) + w(d_tlo), incs=[(v_tlo, 1)])
            pl.set_gate("tlo", v_tlo)
        if L == 1:
            off0 = QT0 - 128 - X1T0
            pl.step("gpsimd",
                    lambda e: e.dma_start(srcbf[:, :, :],
                                          AP(dr["x1"], off0, [[X1N, 128], [128 * X1N, NCH], [1, DOMQ + 128]])),
                    incs=[(d_src, 16)])
            bcol = 2047 - (QT0 - 128)
            def _bfix(e, bcol=bcol):
                return e.tensor_scalar_mul(
                    AP(srcbf, bcol, [[(DOMQ + 128) * NCH, 128], [DOMQ + 128, NCH]]),
                    AP(srcbf, bcol, [[(DOMQ + 128) * NCH, 128], [DOMQ + 128, NCH]]),
                    C["bmask"][:, :])
            pl.step("vector", _bfix, waits=w(d_src), incs=[(v_bfx, 1)])

        def src_sl(c, t0, nn):
            sh = 1 if c >= 4 else 0
            if L == 0:
                return xn[:, c, t0 - sh:t0 - sh + nn]
            b0 = QT0 - 128
            return srcbf[:, c, t0 - sh - b0:t0 - sh - b0 + nn]

        def _dt(e):
            e.tensor_scalar_mul(Dt[:, 0, :], Dt[:, 0, :], C["qs"][:, :])
            e.tensor_scalar_mul(Dt[:, 1, :], Dt[:, 1, :], C["qssw"][:, :])
            e.tensor_scalar_mul(Dt[:, 2, :], Dt[:, 2, :], C["ks"][:, :])
            return e.tensor_scalar_mul(Dt[:, 3, :], Dt[:, 3, :], C["kssw"][:, :])
        pl.step("vector", _dt, waits=w(d_tab), incs=[(v_dt, 1)])
        pl.step("vector", lambda e: e.memset(AP(Vtf, 64, [[130 * NKC, 128], [130, NKC]]), 1.0),
                incs=[(v_on, 1)])
        pl.set_gate("fones", v_on)
        pl.step("vector", lambda e: e.reduce_max(bbf[0:1, 33:34], C["qsr"][:, :], axis=AX.X,
                                                 apply_absolute_value=True), incs=[(v_bnd, 1)])
        pl.step("vector", lambda e: e.reduce_max(bbf[0:1, 34:35], C["ksr"][:, :], axis=AX.X,
                                                 apply_absolute_value=True), incs=[(v_bnd, 1)])
        pl.step("vector", lambda e: e.tensor_tensor(bbf[0:1, 35:36], bbf[0:1, 33:34],
                                                    bbf[0:1, 34:35], ALU.mult),
                waits=w(v_bnd), incs=[(v_bnd, 1)])
        pl.step("tensor", lambda e: e.matmul(RRB[:, 0:1], C["onesf1"][:, :], bbf[0:1, 35:36],
                                             start=True, stop=True),
                waits=w(v_bnd), incs=[(pe_bnd, 1)])

        def _fixb(e):
            e.tensor_scalar(bbf[:, 36:37], RRB[:, 0:1], -QK_SCALE, 20.0, ALU.mult, ALU.add)
            return e.tensor_scalar_add(bbf[:, 0:32], C["logvf"][:, :], bbf[:, 36:37])
        pl.step("vector", _fixb, waits=w(pe_bnd), incs=[(v_bnd, 1)])

        grp = []
        t = QT0
        while t < T:
            grp.append((t, min(t + 512, T) - t))
            t = min(t + 512, T)

        for hg in range(4):
            qcol, kcol, vcol = hg * 128, FINNER + hg * 128, 2 * FINNER + hg * 128

            for qi, col in ((0, qcol), (1, kcol)):
                for gi, (t0, nn) in enumerate(grp):
                    ps = STb[gi % 2]

                    def _mm(e, t0=t0, nn=nn, col=col, ps=ps):
                        inst = None
                        for c in range(NCH):
                            inst = e.matmul(ps[:, 0:nn], wqf[:, c, col:col + 128],
                                            src_sl(c, t0, nn), start=(c == 0), stop=(c == NCH - 1))
                        return inst
                    wt = w(d_w) + pl.gate(("fST", gi % 2)) + pl.gate("fqkKV") + (w(d_src, v_bfx) if L == 1 else [])
                    pl.step("tensor", _mm, waits=wt, incs=[(pe_mm, 1)])
                    pl.step("scalar",
                            lambda e, qi=qi, t0=t0, nn=nn, ps=ps: e.copy(
                                qk[:, qi, t0 - QT0:t0 - QT0 + nn], ps[:, 0:nn]),
                            waits=w(pe_mm) + w(v_tlo), incs=[(ev_q, 1), (s_ev, 1)])
                    pl.set_gate(("fST", gi % 2), ev_q)

            for gi, (t0, nn) in enumerate(grp):
                ps = STb[gi % 2]
                av = [AVA, AVB][gi % 2]

                def _mmv(e, t0=t0, nn=nn, ps=ps, vcol=vcol):
                    inst = None
                    for c in range(NCH):
                        inst = e.matmul(ps[:, 0:nn], wqf[:, c, vcol:vcol + 128],
                                        src_sl(c, t0, nn), start=(c == 0), stop=(c == NCH - 1))
                    return inst
                pl.step("tensor", _mmv, waits=pl.gate(("fST", gi % 2)) + pl.gate("fqkKV") + pl.gate("fvt"),
                        incs=[(pe_mm, 1)])
                pl.step("scalar", lambda e, ps=ps, nn=nn: e.copy(vtmpf[:, 0:nn], ps[:, 0:nn]),
                        waits=w(pe_mm), incs=[(ev_q, 1)])
                pl.set_gate(("fST", gi % 2), ev_q)
                nt = nn // 128

                def _trv(e, nt=nt, av=av):
                    inst = None
                    for j in range(nt):
                        inst = e.matmul(av[:, 128 * j:128 * (j + 1)], vtmpf[:, 128 * j:128 * (j + 1)],
                                        C["idb"][:, :], start=True, stop=True)
                    return inst
                pl.step("tensor", _trv, waits=w(ev_q) + pl.gate(("fAV", gi % 2)) + pl.gate("fones"),
                        incs=[(pe_mm, 1)])
                pl.set_gate("fvt", pe_mm)

                def _evv(e, gi=gi, nt=nt, av=av):
                    inst = None
                    for j in range(nt):
                        base = (gi * 4 + j) * 130
                        inst = e.tensor_copy(AP(Vtf, base, [[130 * NKC, 128], [1, 64]]),
                                             av[:, 128 * j:128 * j + 64])
                        inst = e.tensor_copy(AP(Vtf, base + 65, [[130 * NKC, 128], [1, 64]]),
                                             av[:, 128 * j + 64:128 * j + 128])
                    return inst
                pl.step("vector", _evv, waits=w(pe_mm), incs=[(v_on, 1), (ev_q, 1)])
                pl.set_gate(("fAV", gi % 2), v_on)

            for qi in range(2):
                D1 = Dt[:, 0, :] if qi == 0 else Dt[:, 2, :]
                D2 = Dt[:, 1, :] if qi == 0 else Dt[:, 3, :]
                for gi, (t0, nn) in enumerate(grp):
                    o = t0 - QT0
                    to = t0 - TAB_T0
                    ps = STb[gi % 2]
                    pl.step("scalar",
                            lambda e, qi=qi, o=o, nn=nn: e.activation(
                                q2[:, 0:nn], qk[:, qi, o:o + nn], AF.Square),
                            waits=w(s_ev) + pl.gate("q2"), incs=[(s_ssq, 1)])

                    def _sm(e, o=o, nn=nn):
                        e.matmul(RRA[0:1, 0:nn], C["onescol"][0:64, :], q2[0:64, 0:nn],
                                 start=True, stop=True)
                        return e.matmul(RRB[0:1, 0:nn], C["onescol"][64:128, :], q2[64:128, 0:nn],
                                        start=True, stop=True)
                    pl.step("tensor", _sm, waits=w(s_ssq) + pl.gate("fRRA") + pl.gate("fRRB"),
                            incs=[(pe_ssq, 1)])
                    pl.set_gate("q2", pe_ssq)

                    def _cps(e, nn=nn):
                        e.tensor_copy(sst[0:1, 0, 0:nn], RRA[0:1, 0:nn])
                        return e.tensor_copy(sst[0:1, 1, 0:nn], RRB[0:1, 0:nn])
                    pl.step("vector", _cps, waits=w(pe_ssq) + pl.gate("sst"), incs=[(v_ssq, 1)])
                    pl.set_gate("fRRA", v_ssq)
                    sstf = sst.ap().rearrange("p a b -> p (a b)")
                    pl.step("vector", lambda e, sstf=sstf: e.tensor_scalar_add(
                        sstf[0:1, :], sstf[0:1, :], 1e-24),
                        waits=w(v_ssq), incs=[(v_ssq, 1)])
                    pl.step("scalar", lambda e, sstf=sstf: e.activation(
                        sstf[0:1, :], sstf[0:1, :], AF.Sqrt),
                        waits=w(v_ssq), incs=[(s_ssq, 1)])
                    pl.step("vector", lambda e, sstf=sstf: e.reciprocal(
                        sstf[0:1, :], sstf[0:1, :]),
                        waits=w(s_ssq), incs=[(v_ssq, 1)])

                    def _rep(e, nn=nn):
                        e.matmul(RRB[:, 0:nn], C["halfa"][:, :], sst[0:1, 0, 0:nn],
                                 start=True, stop=False)
                        return e.matmul(RRB[:, 0:nn], C["halfb"][:, :], sst[0:1, 1, 0:nn],
                                        start=False, stop=True)
                    pl.step("tensor", _rep, waits=w(v_ssq) + pl.gate("fRRB"), incs=[(pe_rep, 1)])
                    pl.step("tensor",
                            lambda e, qi=qi, o=o, nn=nn, ps=ps: e.matmul(
                                ps[:, 0:nn], C["r2sb"][:, :], qk[:, qi, o:o + nn],
                                start=True, stop=True),
                            waits=w(s_ssq) + pl.gate(("fST", gi % 2)), incs=[(pe_rot, 1)])
                    pl.step("gpsimd",
                            lambda e, qi=qi, o=o, nn=nn, D1=D1, to=to: e.tensor_tensor(
                                tbuf[:, 0, 0:nn], qk[:, qi, o:o + nn], D1[:, to:to + nn], ALU.mult),
                            waits=w(s_ssq, v_dt) + pl.gate("tb"), incs=[(g_fin, 1)])
                    pl.step("vector",
                            lambda e, nn=nn, ps=ps, D2=D2, to=to: e.tensor_tensor(
                                tbuf[:, 1, 0:nn], ps[:, 0:nn], D2[:, to:to + nn], ALU.mult),
                            waits=w(pe_rot, v_dt) + pl.gate("tb"), incs=[(v_fin, 1), (ev_q, 1)])
                    pl.set_gate(("fST", gi % 2), ev_q)

                    def _fin(e, qi=qi, o=o, nn=nn):
                        e.tensor_tensor(tbuf[:, 0, 0:nn], tbuf[:, 0, 0:nn], tbuf[:, 1, 0:nn], ALU.add)
                        return e.tensor_tensor(qk[:, qi, o:o + nn], tbuf[:, 0, 0:nn],
                                               RRB[:, 0:nn], ALU.mult)
                    pl.step("vector", _fin, waits=w(g_fin, pe_rep), incs=[(v_fin, 1)])
                    pl.set_gate("tb", v_fin)
                    pl.set_gate("sst", pe_rep)
                    pl.set_gate("fRRB", v_fin)

            for (qc0, qc1) in qgs:
                nq = (qc1 - qc0) * 128
                q0 = qc0 * 128 - QT0
                kclist = [kc for kc in range(qc0 - 4, qc1) if QC0 <= kc < NCHUNK]
                for ki, kc in enumerate(kclist):
                    bu = ki % 2
                    stp = STb[bu]
                    ko = (kc - QC0) * 128

                    def _st(e, ko=ko, q0=q0, nq=nq, stp=stp):
                        e.matmul(stp[:, 0:nq], qk[0:64, 1, ko:ko + 128],
                                 qk[0:64, 0, q0:q0 + nq], start=True, stop=True)
                        return e.matmul(stp[:, 512:512 + nq], qk[64:128, 1, ko:ko + 128],
                                        qk[64:128, 0, q0:q0 + nq], start=True, stop=True)
                    pl.step("tensor", _st, waits=w(v_fin, v_bnd) + pl.gate(("fST", bu)), incs=[(pe_st, 1)])

                    def _exp(e, bu=bu, kc=kc, nq=nq, stp=stp):
                        e.activation(PTf[:, bu, 0:nq], stp[:, 0:nq], AF.Exp,
                                     bias=bbf[:, kc:kc + 1], scale=QK_SCALE)
                        return e.activation(PTf[:, bu, 512:512 + nq], stp[:, 512:512 + nq], AF.Exp,
                                            bias=bbf[:, kc:kc + 1], scale=QK_SCALE)
                    pl.step("scalar", _exp, waits=w(pe_st) + pl.gate(("fPT", bu)),
                            incs=[(s_exp, 1)])
                    pl.set_gate(("fST", bu), s_exp)

                    acts = []
                    for i4 in range(qc1 - qc0):
                        qt = qc0 + i4
                        if kc == qt:
                            acts.append((i4, "tri"))
                        elif kc == qt - 4:
                            acts.append((i4, "anti"))
                        elif kc > qt or kc < qt - 4:
                            acts.append((i4, "zero"))
                    need_mask = bool(acts)
                    if need_mask:
                        def _mask(e, acts=tuple(acts), bu=bu):
                            inst = None
                            for h in range(2):
                                for i4, kind in acts:
                                    sl = PTf[:, bu, 512 * h + 128 * i4: 512 * h + 128 * (i4 + 1)]
                                    if kind == "tri":
                                        inst = e.tensor_tensor(sl, sl, C["trib"][:, :], ALU.mult)
                                    elif kind == "anti":
                                        inst = e.tensor_tensor(sl, sl, C["antib"][:, :], ALU.mult)
                                    else:
                                        inst = e.memset(sl, 0.0)
                            return inst
                        pl.step("vector", _mask, waits=w(s_exp), incs=[(v_msk, 1)])

                    def _av(e, kc=kc, bu=bu, nq=nq, first=(ki == 0), last=(ki == len(kclist) - 1)):
                        kb = (kc - QC0) * 130
                        e.matmul(AVA[0:65, 0:nq], Vtf[:, kb:kb + 65], PTf[:, bu, 0:nq],
                                 start=first, stop=last)
                        e.matmul(AVB[0:1, 0:nq], Vtf[:, kb + 64:kb + 65], PTf[:, bu, 512:512 + nq],
                                 start=first, stop=last)
                        return e.matmul(AVB[64:128, 0:nq], Vtf[:, kb + 65:kb + 129],
                                        PTf[:, bu, 512:512 + nq], start=first, stop=last)
                    avw = (w(v_msk) if need_mask else w(s_exp))
                    if ki == 0:
                        avw = avw + pl.gate(("fAV", 0)) + pl.gate(("fAV", 1))
                    pl.step("tensor", _av, waits=avw, incs=[(pe_av, 1)])
                    pl.set_gate(("fPT", bu), pe_av)

                def _rlf(e, nq=nq):
                    e.copy(rlw[0:1, 0:nq], AVA[64:65, 0:nq])
                    return e.copy(rlw[0:1, 512:512 + nq], AVB[0:1, 0:nq])
                pl.step("scalar", _rlf, waits=w(pe_av) + pl.gate("rlw"), incs=[(v_rl, 1)])

                def _rrm(e, nq=nq):
                    e.matmul(RRA[:, 0:nq], C["onesf1"][:, :], rlw[0:1, 0:nq], start=True, stop=True)
                    return e.matmul(RRB[:, 0:nq], C["onesf1"][:, :], rlw[0:1, 512:512 + nq],
                                    start=True, stop=True)
                pl.step("tensor", _rrm, waits=w(v_rl) + pl.gate("fRRA") + pl.gate("fRRB"), incs=[(pe_rr, 1)])

                def _rcpf(e, nq=nq):
                    e.tensor_scalar_add(rlw[:, 0:nq], RRA[:, 0:nq], 1e-30)
                    e.reciprocal(rlw[:, 0:nq], rlw[:, 0:nq])
                    e.tensor_scalar_add(rlw[:, 512:512 + nq], RRB[:, 0:nq], 1e-30)
                    return e.reciprocal(rlw[:, 512:512 + nq], rlw[:, 512:512 + nq])
                pl.step("vector", _rcpf, waits=w(pe_rr), incs=[(v_rl, 1)])
                pl.set_gate("fRRA", v_rl)
                pl.set_gate("fRRB", v_rl)

                def _norm(e, hg=hg, qc0=qc0, nq=nq):
                    o0 = hg * DOMA + (qc0 - AC0) * 128
                    e.tensor_tensor(off[0:64, o0:o0 + nq], AVA[0:64, 0:nq], rlw[0:64, 0:nq], ALU.mult)
                    return e.tensor_tensor(off[64:128, o0:o0 + nq], AVB[64:128, 0:nq],
                                           rlw[64:128, 512:512 + nq], ALU.mult)
                pl.step("vector", _norm, waits=w(v_rl), incs=[(v_on, 1), (v_rl, 1)])
                pl.set_gate(("fAV", 0), v_on)
                pl.set_gate(("fAV", 1), v_on)
                pl.set_gate("rlw", v_on)
            pl.G["fqkKV"] = w(pe_st) + w(pe_av)

        with nc.Block() as blk:
            pl.emit(nc, blk)


def _ffn_mlp(nc, S, L, xn, C, dr, off, AC0, DOMA, AT0, qgs):
    with ExitStack() as xs:
        wos = xs.enter_context(nc.sbuf_tensor(f"m{L}_wo", [128, 4, D], BF16))
        fb1 = xs.enter_context(nc.sbuf_tensor(f"m{L}_fb1", [1, H2P], BF16))
        fb2 = xs.enter_context(nc.sbuf_tensor(f"m{L}_fb2", [1, D], BF16))
        w1a = xs.enter_context(nc.sbuf_tensor(f"m{L}_w1a", [128, 2, NCH, 256], BF16))
        w1g = xs.enter_context(nc.sbuf_tensor(f"m{L}_w1g", [128, 2, NCH, 256], BF16))
        w2s = xs.enter_context(nc.sbuf_tensor(f"m{L}_w2", [128, 2, 2, D], BF16))
        xmid = xs.enter_context(nc.sbuf_tensor(f"m{L}_xm", [128, NCH, 513], F32))
        hln = xs.enter_context(nc.sbuf_tensor(f"m{L}_h", [128, NCH, 512], BF16))
        gel = xs.enter_context(nc.sbuf_tensor(f"m{L}_g", [128, 512], BF16))
        x1g = xs.enter_context(nc.sbuf_tensor(f"m{L}_x1", [128, NCH, 512], F32))
        xsq = xs.enter_context(nc.sbuf_tensor(f"m{L}_xsq", [128, 1, 512], F32))
        mbuf = xs.enter_context(nc.sbuf_tensor(f"m{L}_m", [128, 22 * 512], BF16))
        lnst = xs.enter_context(nc.sbuf_tensor(f"m{L}_ln", [1, 1024], F32))
        baseb = xacc = None
        if L == 1:
            baseb = xs.enter_context(nc.sbuf_tensor(f"m{L}_base", [128, 2, NCH, 512], F32))
            xacc = xs.enter_context(nc.sbuf_tensor(f"m{L}_xacc", [128, 2, 2 * D], F32))
        ST0 = xs.enter_context(nc.psum_tensor(f"m{L}_ST0", [128, 1024], F32))
        ST1 = xs.enter_context(nc.psum_tensor(f"m{L}_ST1", [128, 1024], F32))
        RRA = xs.enter_context(nc.psum_tensor(f"m{L}_RRA", [128, 512], F32))
        RRB = xs.enter_context(nc.psum_tensor(f"m{L}_RRB", [128, 512], F32))
        TPp = xs.enter_context(nc.psum_tensor(f"m{L}_TP", [128, 1024], F32))

        STb = [ST0, ST1]
        w2slot = [ST0[:, 0:512], ST0[:, 512:1024], ST1[:, 0:512], ST1[:, 512:1024],
                  TPp[:, 0:512], TPp[:, 512:1024], RRA[:, 0:512], RRB[:, 0:512]]
        pl = Plan()
        d_wo = S("mBwo")
        d_w1 = [S("mBw1a"), S("mBw1b")]
        d_w2 = [S("mBw2a"), S("mBw2b")]
        d_base = [S("mBbase0"), S("mBbase1")]
        d_xin = [S("mBxin0"), S("mBxin1")]
        d_oo = [S("mBoo0"), S("mBoo1")]
        d_x1o = S("mBx1o")
        pe_wo, v_xm = S("mBpewo"), S("mBvxm")
        s_sq2, pe_stat = S("mBssq"), S("mBstat")
        v_ln, s_ln = S("mBvln"), S("mBsln")
        pe_rep2, v_u, s_hln = S("mBrep"), S("mBu"), S("mBhln")
        pe_w1, s_gel, v_m = S("mBw1"), S("mBgel"), S("mBm")
        pe_w2, v_x1 = S("mBw2"), S("mBx1")
        pe_tp, v_fin2 = S("mBtp"), S("mBfin")
        v_bfx2 = S("mBbfx")

        pl.step("gpsimd", lambda e: e.dma_start(wos[:, :, :], dr["fwo"][:, :, :]), incs=[(d_wo, 16)])
        pl.step("gpsimd", lambda e: e.dma_start(fb1[:, :], dr["fb1"][:, :]), incs=[(d_wo, 16)])
        pl.step("gpsimd", lambda e: e.dma_start(fb2[:, :], dr["fb2"][:, :]), incs=[(d_wo, 16)])

        w1blocks = [(j * 256, (j + 1) * 256) for j in range(11)]

        OUT0 = 2048

        for g, (gc0, gc1) in enumerate(qgs):
            W = (gc1 - gc0) * 128
            t0 = gc0 * 128
            gb = g % 2

            if g == 0:
                pl.step("vector", lambda e: e.memset(AP(xmid, 0, [[513 * NCH, 128], [513, NCH]]), 0.0),
                        waits=pl.gate("xmid"), incs=[(v_xm, 1)])
            else:
                Wp = Wprev
                pl.step("vector",
                        lambda e, Wp=Wp: e.tensor_copy(
                            AP(xmid, 0, [[513 * NCH, 128], [513, NCH]]),
                            AP(xmid, Wp, [[513 * NCH, 128], [513, NCH]])),
                        waits=pl.gate("xmid"), incs=[(v_xm, 1)])
            Wprev = W

            if L == 1:
                pl.step("sync",
                        lambda e, t0=t0, gb=gb, W=W: e.dma_start(
                            baseb[:, gb, :, 0:W],
                            AP(dr["x1"], t0 - X1T0, [[X1N, 128], [128 * X1N, NCH], [1, W]])),
                        waits=pl.gate(("base", gb)), incs=[(d_base[gb], 16)])
                if t0 <= 2047 < t0 + W:
                    bc = 2047 - t0
                    def _bfix2(e, gb=gb, bc=bc):
                        return e.tensor_scalar_mul(baseb[:, gb, :, bc:bc + 1],
                                                   baseb[:, gb, :, bc:bc + 1],
                                                   C["bmask"][:, :])
                    pl.step("vector", _bfix2, waits=w(d_base[gb]), incs=[(v_bfx2, 1)])

            for oc in range(NCH):
                ps = STb[oc % 2]

                def _womm(e, oc=oc, t0=t0, W=W, ps=ps):
                    inst = None
                    for c in range(4):
                        inst = e.matmul(ps[:, 0:W], wos[:, c, 128 * oc:128 * (oc + 1)],
                                        off[:, c * DOMA + (t0 - AT0): c * DOMA + (t0 - AT0) + W],
                                        start=(c == 0), stop=(c == 3))
                    return inst
                pl.step("tensor", _womm, waits=w(d_wo) + pl.gate(("mST", oc % 2)),
                        incs=[(pe_wo, 1)])

                def _xm(e, oc=oc, t0=t0, gb=gb, W=W, ps=ps):
                    if L == 0:
                        return e.tensor_tensor(xmid[:, oc, 1:W + 1], ps[:, 0:W], xn[:, oc, t0:t0 + W], ALU.add)
                    return e.tensor_tensor(xmid[:, oc, 1:W + 1], ps[:, 0:W], baseb[:, gb, oc, 0:W], ALU.add)
                pl.step("vector", _xm, waits=w(pe_wo) + pl.gate("xmid") + (w(d_base[gb], v_bfx2) if L == 1 else []),
                        incs=[(v_xm, 1)])
                pl.set_gate(("mST", oc % 2), v_xm)

            def shv(oc, W=W):
                sh = 1 if oc < 4 else 0
                return xmid[:, oc, sh:sh + W]

            def _mu(e, W=W):
                inst = None
                for oc in range(NCH):
                    inst = e.matmul(RRA[0:1, 0:W], C["onescol"][:, :], shv(oc, W),
                                    start=(oc == 0), stop=(oc == NCH - 1))
                return inst
            pl.step("tensor", _mu, waits=w(v_xm) + pl.gate("mRRA"), incs=[(pe_stat, 1)])
            for oc in range(NCH):
                pl.step("scalar",
                        lambda e, oc=oc, W=W: e.activation(xsq[:, 0, 0:W], shv(oc, W), AF.Square),
                        waits=w(v_xm) + w(pe_stat), incs=[(s_sq2, 1)])
                pl.step("tensor",
                        lambda e, oc=oc, W=W: e.matmul(RRB[0:1, 0:W], C["onescol"][:, :],
                                                       xsq[:, 0, 0:W],
                                                       start=(oc == 0), stop=(oc == NCH - 1)),
                        waits=w(s_sq2) + (pl.gate("mRRB") if oc == 0 else []), incs=[(pe_stat, 1)])

            def _lnv(e, W=W):
                e.tensor_scalar_mul(lnst[:, 0:W], RRA[0:1, 0:W], 1.0 / D)
                e.tensor_scalar_mul(lnst[:, 512:512 + W], RRB[0:1, 0:W], 1.0 / D)
                e.tensor_tensor(xsq[0:1, 0, 0:W], lnst[:, 0:W], lnst[:, 0:W], ALU.mult)
                e.tensor_tensor(lnst[:, 512:512 + W], lnst[:, 512:512 + W], xsq[0:1, 0, 0:W], ALU.subtract)
                return e.tensor_scalar_add(lnst[:, 512:512 + W], lnst[:, 512:512 + W], 1e-5)
            pl.step("vector", _lnv, waits=w(pe_stat) + pl.gate("lnst"), incs=[(v_ln, 1)])
            pl.step("scalar", lambda e, W=W: e.activation(lnst[:, 512:512 + W], lnst[:, 512:512 + W], AF.Sqrt),
                    waits=w(v_ln), incs=[(s_ln, 1)])

            def _lnr(e, W=W):
                e.reciprocal(lnst[:, 512:512 + W], lnst[:, 512:512 + W])
                return e.tensor_tensor(lnst[:, 0:W], lnst[:, 0:W], lnst[:, 512:512 + W], ALU.mult)
            pl.step("vector", _lnr, waits=w(s_ln), incs=[(v_ln, 1)])

            def _repln(e, W=W):
                e.matmul(RRA[:, 0:W], C["onesf1"][:, :], lnst[0:1, 512:512 + W], start=True, stop=True)
                return e.matmul(RRB[:, 0:W], C["onesf1"][:, :], lnst[0:1, 0:W], start=True, stop=True)
            pl.step("tensor", _repln, waits=w(v_ln), incs=[(pe_rep2, 1)])
            pl.set_gate("lnst", pe_rep2)

            for oc in range(NCH):
                def _u(e, oc=oc, W=W):
                    e.tensor_tensor(x1g[:, oc, 0:W], shv(oc, W), RRA[:, 0:W], ALU.mult)
                    return e.tensor_tensor(x1g[:, oc, 0:W], x1g[:, oc, 0:W], RRB[:, 0:W], ALU.subtract)
                pl.step("vector", _u, waits=w(pe_rep2) + pl.gate("x1g"),
                        incs=[(v_u, 1)])
                pl.step("scalar",
                        lambda e, oc=oc, W=W: e.activation(hln[:, oc, 0:W], x1g[:, oc, 0:W], AF.Identity,
                                                           bias=C["lnb"][:, oc:oc + 1],
                                                           scale=C["lng"][:, oc:oc + 1]),
                        waits=w(v_u) + pl.gate("hln"), incs=[(s_hln, 1)])

            for j, (c0, c1) in enumerate(w1blocks):
                jb = j % 2
                pl.step("gpsimd",
                        lambda e, c0=c0, jb=jb: e.dma_start(
                            w1a[:, jb, :, :],
                            AP(dr["fw1"], c0, [[NCH * H2P, 128], [H2P, NCH], [1, 256]])),
                        waits=pl.gate(("w1", jb)), incs=[(d_w1[jb], 16)])
                pl.step("gpsimd",
                        lambda e, c0=c0, jb=jb: e.dma_start(
                            w1g[:, jb, :, :],
                            AP(dr["fw1"], HIDP + c0, [[NCH * H2P, 128], [H2P, NCH], [1, 256]])),
                        waits=[], incs=[(d_w1[jb], 16)])
                for jj in range(2):
                    mc = (c0 + 128 * jj) // 128

                    def _gmm(e, jj=jj, jb=jb, mc=mc, W=W):
                        for c in range(NCH):
                            e.matmul(ST0[:, 0:W], w1g[:, jb, c, 128 * jj:128 * (jj + 1)],
                                     hln[:, c, 0:W], start=(c == 0), stop=False)
                        return e.matmul(ST0[:, 0:W], fb1[:, HIDP + 128 * mc:HIDP + 128 * mc + 128],
                                        C["onesrow"][:, 0:W], start=False, stop=True)
                    pl.step("tensor", _gmm,
                            waits=w(s_hln) + w(d_w1[jb]) + pl.gate(("mST", 0)),
                            incs=[(pe_w1, 1)])
                    pl.step("scalar", lambda e, W=W: e.activation(gel[:, 0:W], ST0[:, 0:W], AF.Gelu),
                            waits=w(pe_w1) + pl.gate("gel"), incs=[(s_gel, 1)])
                    pl.set_gate(("mST", 0), s_gel)

                    def _amm(e, jj=jj, jb=jb, mc=mc, W=W):
                        for c in range(NCH):
                            e.matmul(ST1[:, 0:W], w1a[:, jb, c, 128 * jj:128 * (jj + 1)],
                                     hln[:, c, 0:W], start=(c == 0), stop=False)
                        return e.matmul(ST1[:, 0:W], fb1[:, 128 * mc:128 * mc + 128],
                                        C["onesrow"][:, 0:W], start=False, stop=True)
                    pl.step("tensor", _amm,
                            waits=w(s_hln) + w(d_w1[jb]) + pl.gate(("mST", 1)),
                            incs=[(pe_w1, 1)])
                    pl.step("vector",
                            lambda e, mc=mc, W=W: e.tensor_tensor(
                                mbuf[:, 512 * mc:512 * mc + W], ST1[:, 0:W], gel[:, 0:W], ALU.mult),
                            waits=w(pe_w1, s_gel) + pl.gate("mbuf"), incs=[(v_m, 1)])
                    pl.set_gate(("mST", 1), v_m)
                    pl.set_gate("gel", v_m)
                pl.set_gate(("w1", jb), pe_w1)

            for mc in range(22):
                blk = mc // 2
                jb = blk % 2
                rr = mc % 2
                if rr == 0:
                    pl.step("gpsimd",
                            lambda e, blk=blk, jb=jb: e.dma_start(
                                w2s[:, jb, :, :],
                                AP(dr["fw2"], blk * 2 * D, [[(HIDP // 128) * D, 128], [D, 2], [1, D]])),
                            waits=pl.gate(("w2", jb)), incs=[(d_w2[jb], 16)])

                def _w2mm(e, mc=mc, jb=jb, rr=rr, W=W, first=(mc == 0)):
                    inst = None
                    for oc in range(NCH):
                        inst = e.matmul(w2slot[oc][:, 0:W] if W != 512 else w2slot[oc],
                                        w2s[:, jb, rr, 128 * oc:128 * (oc + 1)],
                                        mbuf[:, 512 * mc:512 * mc + W],
                                        start=first, stop=False)
                    return inst
                extra = (pl.gate(("mST", 0)) + pl.gate(("mST", 1)) + pl.gate("mRRA")
                         + pl.gate("mRRB") + pl.gate("mTP")) if mc == 0 else []
                pl.step("tensor", _w2mm, waits=w(v_m) + w(d_w2[jb]) + extra, incs=[(pe_w2, 1)])
                if rr == 1:
                    pl.set_gate(("w2", jb), pe_w2)

            def _w2bias(e, W=W):
                inst = None
                for oc in range(NCH):
                    inst = e.matmul(w2slot[oc][:, 0:W] if W != 512 else w2slot[oc],
                                    fb2[:, 128 * oc:128 * (oc + 1)],
                                    C["onesrow"][:, 0:W], start=False, stop=True)
                return inst
            pl.step("tensor", _w2bias, waits=[], incs=[(pe_w2, 1)])

            for oc in range(NCH):
                pl.step("vector",
                        lambda e, oc=oc, W=W: e.tensor_tensor(
                            x1g[:, oc, 0:W], xmid[:, oc, 1:W + 1], w2slot[oc][:, 0:W], ALU.add),
                        waits=w(pe_w2, s_hln), incs=[(v_x1, 1)])
            pl.set_gate(("mST", 0), v_x1)
            pl.set_gate(("mST", 1), v_x1)
            pl.set_gate("mTP", v_x1)
            pl.set_gate("mRRA", v_x1)
            pl.set_gate("mRRB", v_x1)
            pl.set_gate("xmid", v_x1)
            pl.set_gate("mbuf", pe_w2)

            if L == 0:
                pl.step("sync",
                        lambda e, t0=t0, W=W: e.dma_start(
                            AP(dr["x1"], t0 - X1T0, [[X1N, 128], [128 * X1N, NCH], [1, W]]),
                            x1g[:, :, 0:W]),
                        waits=w(v_x1), incs=[(d_x1o, 16)])
                pl.set_gate("x1g", d_x1o)
                pl.G["x1g"] = pl.G["x1g"] + w(s_hln)
            else:
                ntt = W // 128
                for tt in range(ntt):
                    tg = t0 + 128 * tt
                    if tg < OUT0:
                        continue
                    b2 = tt % 2
                    pl.step("sync",
                            lambda e, tg=tg, b2=b2: e.dma_start(xacc[:, b2, 0:D], dr["x"][tg:tg + 128, :]),
                            waits=pl.gate(("xacc", b2)), incs=[(d_xin[b2], 16)])
                    pl.step("sync",
                            lambda e, tg=tg, b2=b2: e.dma_start(xacc[:, b2, D:2 * D],
                                                                dr["acc"][tg - OWN:tg - OWN + 128, :]),
                            waits=[], incs=[(d_xin[b2], 16)])

                    def _tp(e, tt=tt):
                        inst = None
                        for oc in range(NCH):
                            inst = e.matmul(TPp[:, 128 * oc:128 * (oc + 1)],
                                            x1g[:, oc, 128 * tt:128 * (tt + 1)], C["idf"][:, :],
                                            start=True, stop=True)
                        return inst
                    pl.step("tensor", _tp, waits=w(v_x1) + pl.gate("mTP"), incs=[(pe_tp, 1)])
                    pl.set_gate("x1g", pe_tp)
                    pl.G["x1g"] = pl.G["x1g"] + w(s_hln)

                    def _fin(e, b2=b2):
                        e.tensor_tensor(xacc[:, b2, 0:D], xacc[:, b2, 0:D], TPp[:, :], ALU.add)
                        return e.tensor_tensor(xacc[:, b2, 0:D], xacc[:, b2, 0:D],
                                               xacc[:, b2, D:2 * D], ALU.add)
                    pl.step("vector", _fin, waits=w(pe_tp) + w(d_xin[b2]), incs=[(v_fin2, 1)])
                    pl.set_gate("mTP", v_fin2)
                    pl.step("sync",
                            lambda e, tg=tg, b2=b2: e.dma_start(dr["out"][tg - OWN:tg - OWN + 128, :],
                                                                xacc[:, b2, 0:D]),
                            waits=w(v_fin2), incs=[(d_oo[b2], 16)])
                    pl.set_gate(("xacc", b2), d_oo[b2])

        if L == 0:
            pl.step("sync", lambda e: e.nop(), waits=w(d_x1o))
        else:
            pl.step("sync", lambda e: e.nop(), waits=w(d_oo[0]) + w(d_oo[1]))

        with nc.Block() as blk:
            pl.emit(nc, blk)


_NC_CACHE = {}


def kernel(L0_attn_x, L0_attn_w_qkv, L0_attn_b_qkv, L0_attn_w_proj, L0_attn_b_proj,
           L0_ffn_wqkv, L0_ffn_q_scale, L0_ffn_k_scale, L0_ffn_wo, L0_ffn_ln_g,
           L0_ffn_ln_b, L0_ffn_w1, L0_ffn_b1, L0_ffn_w2, L0_ffn_b2):
    x = np.ascontiguousarray(np.asarray(L0_attn_x, dtype=np.float32))
    consts = _host_consts()

    shared = dict(
        wqkv_a=_w_fm(np.asarray(L0_attn_w_qkv, np.float32)),
        bqkv_a=np.asarray(L0_attn_b_qkv, np.float32).reshape(1, 3 * D),
        wproj=_w_fm(np.asarray(L0_attn_w_proj, np.float32)),
        bproj=np.asarray(L0_attn_b_proj, np.float32).reshape(1, D),
        bmask=np.ones((128, 1), np.float32),
        logva=np.zeros((128, 32), np.float32),
        logvf=np.zeros((128, 32), np.float32),
        **consts,
    )
    for l in range(2):
        shared[f"fwqkv{l}"] = _w_fm(np.asarray(L0_ffn_wqkv[l], np.float32))
        shared[f"fwo{l}"] = _w_fm(np.asarray(L0_ffn_wo[l], np.float32))
        w1 = np.asarray(L0_ffn_w1[l], np.float32)
        w1p = np.zeros((D, H2P), np.float32)
        w1p[:, 0:HID] = w1[:, 0:HID]
        w1p[:, HIDP:HIDP + HID] = w1[:, HID:2 * HID]
        shared[f"fw1{l}"] = _w_fm(w1p)
        b1 = np.asarray(L0_ffn_b1[l], np.float32)
        b1p = np.zeros((1, H2P), np.float32)
        b1p[0, 0:HID] = b1[0:HID]
        b1p[0, HIDP:HIDP + HID] = b1[HID:2 * HID]
        shared[f"fb1{l}"] = b1p
        w2 = np.asarray(L0_ffn_w2[l], np.float32)
        w2p = np.zeros((HIDP, D), np.float32)
        w2p[0:HID, :] = w2
        shared[f"fw2{l}"] = _w_fm(w2p)
        shared[f"fb2{l}"] = np.asarray(L0_ffn_b2[l], np.float32).reshape(1, D)
        shared[f"lng{l}"] = np.ascontiguousarray(np.asarray(L0_ffn_ln_g[l], np.float32).reshape(NCH, 128).T)
        shared[f"lnb{l}"] = np.ascontiguousarray(np.asarray(L0_ffn_ln_b[l], np.float32).reshape(NCH, 128).T)
        qv = np.asarray(L0_ffn_q_scale[l], np.float32)
        kv = np.asarray(L0_ffn_k_scale[l], np.float32)
        shared[f"qs{l}"] = np.tile(qv, 2).reshape(128, 1)
        shared[f"qssw{l}"] = np.tile(np.concatenate([qv[32:64], qv[0:32]]), 2).reshape(128, 1)
        shared[f"ks{l}"] = np.tile(kv, 2).reshape(128, 1)
        shared[f"kssw{l}"] = np.tile(np.concatenate([kv[32:64], kv[0:32]]), 2).reshape(128, 1)
        shared[f"qsr{l}"] = np.tile(qv, 2).reshape(1, 128)
        shared[f"ksr{l}"] = np.tile(kv, 2).reshape(1, 128)

    if "nc" not in _NC_CACHE:
        _NC_CACHE["nc"] = build_program()
    nc = _NC_CACHE["nc"]

    in_maps = []
    for c in range(8):
        seq, half = c // 2, c % 2
        if half == 1:
            xc = x[seq]
        else:
            xc = np.concatenate([np.zeros((OWN, D), np.float32), x[seq, 0:OWN]], axis=0)
        real0 = TAB_T0 - (0 if half == 1 else OWN)
        tabs = _rotary_tables(np.arange(real0, real0 + TAB_N))
        lv_a = np.zeros((128, 32), np.float32)
        lv_f = np.zeros((128, 32), np.float32)
        if half == 0:
            lv_a[:, 0:16] = NEG
            lv_f[:, 0:16] = NEG
        m = dict(shared)
        m["x_ctx"] = xc
        m["bmask"] = np.full((128, 1), 0.0 if half == 0 else 1.0, np.float32)
        m["logva"] = lv_a
        m["logvf"] = lv_f
        for i in range(4):
            m[f"tab{i}"] = tabs[i]
            m[f"tab{i}lo"] = tabs[4 + i]
        in_maps.append(m)

    res = run_bass_kernel_spmd(nc, in_maps, core_ids=list(range(8)))
    out = np.empty((B, T, D), res.results[0]["out_own"].dtype)
    for c in range(8):
        seq, half = c // 2, c % 2
        out[seq, half * OWN:(half + 1) * OWN] = res.results[c]["out_own"]
    return out


# version 22
# speedup vs baseline: 1.0452x; 1.0304x over previous
import numpy as np
from contextlib import ExitStack

import concourse.bass as bass
import concourse.mybir as mybir
from concourse.bass_utils import run_bass_kernel_spmd

F32 = mybir.dt.float32
BF16 = mybir.dt.bfloat16
AF = mybir.ActivationFunctionType
ALU = mybir.AluOpType
AX = mybir.AxisListType
AP = bass.AP

B, T, D = 4, 4096, 1024
OWN = 2048
NCH = 8
NHG = 8
FINNER = 512
HID = 2730
HIDP = 2816
H2P = 2 * HIDP
QK_SCALE = 8.0
L1_QKV_C0, L1_ATT_C0 = 6, 10
L2_QKV_C0, L2_ATT_C0 = 11, 15
NCHUNK = 32
TAB_T0 = L1_QKV_C0 * 128
TAB_N = T - TAB_T0
X1T0 = L1_ATT_C0 * 128
X1N = T - X1T0
NEG = -60000.0


class PSem:
    def __init__(self, ctx, nc, name):
        self.h = ctx.enter_context(nc.semaphore(name))
        self.n = 0
        self.name = name
        self.producers = set()


class Plan:
    def __init__(self):
        self.steps = []
        self.G = {}

    def gate(self, *keys):
        out = []
        for k in keys:
            out += self.G.get(k, [])
        return out

    def set_gate(self, key, sem):
        self.G[key] = [(sem, sem.n)]

    def step(self, eng, emit_fn, waits=(), incs=()):
        self.steps.append((eng, [(s, v) for (s, v) in waits if v > 0], emit_fn, list(incs)))
        for s, k in incs:
            s.n += k
            s.producers.add(eng)

    def emit(self, nc, block):
        by_eng = {}
        for eng, ws, fn, incs in self.steps:
            by_eng.setdefault(eng, []).append((ws, fn, incs))

        def mk_body(items, ename):
            def body(e):
                ee = _DrainProxy(e) if ename == "vector" else e
                last = {}
                for ws, fn, incs in items:
                    for s, v in ws:
                        if s.producers == {ename} and ename != "vector":
                            continue
                        if last.get(s.name, -1) >= v:
                            continue
                        e.wait_ge(s.h, v)
                        last[s.name] = v
                    inst = fn(ee)
                    for ii, (s, k) in enumerate(incs):
                        if ii == 0:
                            inst.then_inc(s.h, k)
                        else:
                            e.nop().then_inc(s.h, k)
            return body

        emap = {"sync": mybir.EngineType.SP, "gpsimd": mybir.EngineType.Pool,
                "tensor": mybir.EngineType.PE, "scalar": mybir.EngineType.Activation,
                "vector": mybir.EngineType.DVE}
        for ename in ("sync", "gpsimd", "tensor", "scalar", "vector"):
            if ename in by_eng:
                block._start_engine(mk_body(by_eng[ename], ename), emap[ename])


class _DrainProxy:

    _OPS = {"tensor_copy", "tensor_tensor", "tensor_add", "tensor_sub", "tensor_mul",
            "tensor_max", "tensor_scalar", "tensor_single_scalar", "tensor_scalar_add",
            "tensor_scalar_sub", "tensor_scalar_mul", "tensor_scalar_max",
            "tensor_scalar_min", "tensor_relu", "tensor_reduce", "reduce_max",
            "reduce_sum", "reciprocal", "memset", "memzero", "select", "transpose",
            "bn_stats", "bn_aggr", "stream_shuffle", "pool", "pool_avg", "pool_max"}

    def __init__(self, e):
        self._e = e

    def __getattr__(self, name):
        f = getattr(self._e, name)
        if name in _DrainProxy._OPS:
            def g(*a, **k):
                self._e.drain()
                return f(*a, **k)
            return g
        return f


def w(*sems):
    return [(s, s.n) for s in sems]


def wb(s, k):
    return [(s, max(0, s.n - k))]


def _rotary_tables(real_pos):
    tf = real_pos.astype(np.float32)
    try:
        import jax
        import jax.numpy as jnp
        with jax.default_device(jax.devices("cpu")[0]):
            half_idx = jnp.arange(0, 64, 2, dtype=jnp.float32)
            inv_freq = 1.0 / 10000.0 ** (half_idx / 64)
            t = jnp.asarray(tf)
            freqs = t[:, None] * inv_freq[None, :]
            base = (half_idx + 0.4 * 64) / (1.4 * 64)
            power = (t - T // 2) / float(512 // 2)
            scale = base[None, :] ** power[:, None]
            cos, sin = jnp.cos(freqs), jnp.sin(freqs)
            cos, sin, scale = (np.asarray(a, np.float64) for a in (cos, sin, scale))
    except Exception:
        half_idx = np.arange(0, 64, 2, dtype=np.float32)
        inv_freq = np.float32(1.0) / np.power(np.float32(10000.0), half_idx / np.float32(64.0))
        freqs = tf[:, None] * inv_freq[None, :]
        base = (half_idx + np.float32(0.4 * 64)) / np.float32(1.4 * 64)
        power = (tf - np.float32(T // 2)) / np.float32(512 // 2)
        scale = np.power(base[None, :], power[:, None], dtype=np.float32)
        cos, sin, scale = (a.astype(np.float64) for a in (np.cos(freqs), np.sin(freqs), scale))
    cos = np.concatenate([cos, cos], axis=1)
    sin = np.concatenate([sin, sin], axis=1)
    scale = np.concatenate([scale, scale], axis=1)
    his, los = [], []
    for a in (cos * scale, sin * scale, cos / scale, sin / scale):
        a = np.concatenate([a.T, a.T], axis=0)
        hi = a.astype(np.float32)
        lo = (a - hi.astype(np.float64)).astype(np.float32)
        his.append(np.ascontiguousarray(hi))
        los.append(np.ascontiguousarray(lo))
    return his + los


def _host_consts():
    c = {}
    c["identc"] = np.eye(128, dtype=np.float32)
    r = np.zeros((128, 128), dtype=np.float32)
    for blk in range(2):
        o = 64 * blk
        for j in range(32):
            r[o + j + 32, o + j] = -1.0
            r[o + j, o + j + 32] = 1.0
    c["r2sc"] = r
    kk, qq = np.meshgrid(np.arange(128), np.arange(128), indexing="ij")
    c["tric"] = (qq >= kk).astype(np.float32)
    c["antic"] = (kk >= qq).astype(np.float32)
    ha = np.zeros((1, 128), np.float32)
    ha[0, 0:64] = 1.0
    hb = np.zeros((1, 128), np.float32)
    hb[0, 64:128] = 1.0
    c["halfac"] = ha
    c["halfbc"] = hb
    return c


def _w_fm(wmat):
    K, N = wmat.shape
    return np.ascontiguousarray(wmat.reshape(K // 128, 128, N).transpose(1, 0, 2))


def build_program():
    nc = bass.Bass("TRN2", target_bir_lowering=False, detect_race_conditions=False)
    es = ExitStack()

    dram = {}

    def din(name, shape):
        dram[name] = nc.dram_tensor(name, shape, F32, kind="ExternalInput")

    din("x_ctx", [T, D])
    din("wqkv_a", [128, NCH, 3 * D])
    din("bqkv_a", [1, 3 * D])
    din("wproj", [128, NCH, D])
    din("bproj", [1, D])
    for l in range(2):
        din(f"fwqkv{l}", [128, NCH, 3 * FINNER])
        din(f"fwo{l}", [128, 4, D])
        din(f"fw1{l}", [128, NCH, H2P])
        din(f"fb1{l}", [1, H2P])
        din(f"fw2{l}", [128, HIDP // 128, D])
        din(f"fb2{l}", [1, D])
        din(f"lng{l}", [128, NCH])
        din(f"lnb{l}", [128, NCH])
        din(f"qs{l}", [128, 1])
        din(f"qssw{l}", [128, 1])
        din(f"ks{l}", [128, 1])
        din(f"kssw{l}", [128, 1])
        din(f"qsr{l}", [1, 128])
        din(f"ksr{l}", [1, 128])
    for i in range(4):
        din(f"tab{i}", [128, TAB_N])
        din(f"tab{i}lo", [128, TAB_N])
    for n in ("identc", "r2sc", "tric", "antic"):
        din(n, [128, 128])
    din("halfac", [1, 128])
    din("halfbc", [1, 128])
    din("bmask", [128, 1])
    din("logva", [128, 32])
    din("logvf", [128, 32])

    d_out = nc.dram_tensor("out_own", [OWN, D], F32, kind="ExternalOutput")
    d_acc = nc.dram_tensor("acc_scr", [OWN, D], F32, kind="Internal")
    d_x1 = nc.dram_tensor("x1_scr", [NCH, 128, X1N], F32, kind="Internal")

    sb = lambda name, shape, dt: es.enter_context(nc.sbuf_tensor(name, shape, dt))

    C = {}
    C["idb"] = sb("idb", [128, 128], BF16)
    C["idf"] = sb("idf", [128, 128], F32)
    C["r2sb"] = sb("r2sb", [128, 128], BF16)
    C["trib"] = sb("trib", [128, 128], BF16)
    C["antib"] = sb("antib", [128, 128], BF16)
    C["halfa"] = sb("halfa", [1, 128], F32)
    C["halfb"] = sb("halfb", [1, 128], F32)
    C["bmask"] = sb("bmask_s", [128, 1], F32)
    C["logva"] = sb("logva_s", [128, 32], F32)
    C["logvf"] = sb("logvf_s", [128, 32], F32)
    C["onesrow"] = sb("onesrow", [1, 512], BF16)
    C["onesf1"] = sb("onesf1", [1, 128], F32)
    C["onescol"] = sb("onescol", [128, 1], F32)
    per_layer = {}
    for l in range(2):
        pc = {}
        pc["lng"] = sb(f"lng_s{l}", [128, NCH], F32)
        pc["lnb"] = sb(f"lnb_s{l}", [128, NCH], F32)
        pc["qs"] = sb(f"qs_s{l}", [128, 1], F32)
        pc["qssw"] = sb(f"qssw_s{l}", [128, 1], F32)
        pc["ks"] = sb(f"ks_s{l}", [128, 1], F32)
        pc["kssw"] = sb(f"kssw_s{l}", [128, 1], F32)
        pc["qsr"] = sb(f"qsr_s{l}", [1, 128], F32)
        pc["ksr"] = sb(f"ksr_s{l}", [1, 128], F32)
        per_layer[l] = pc

    sems = {}

    def S(name):
        if name not in sems:
            sems[name] = PSem(es, nc, name)
        return sems[name]

    with nc.sbuf_tensor("xn", [128, NCH, T], BF16) as xn:
        emit_p0(nc, S, dram, xn, C, per_layer)
        emit_p1(nc, S, dram, xn, C, d_acc)
        cl = dict(C)
        cl.update(per_layer[0])
        dr0 = dict(fwqkv=dram["fwqkv0"], fwo=dram["fwo0"], fw1=dram["fw10"],
                   fw2=dram["fw20"], fb1=dram["fb10"], fb2=dram["fb20"], tabs=[dram[f"tab{i}"] for i in range(4)],
                   tabslo=[dram[f"tab{i}lo"] for i in range(4)],
                   x1=d_x1, x=dram["x_ctx"], acc=d_acc, out=d_out, lname="0")
        emit_ffn_layer(nc, S, 0, xn, cl, dr0)
    cl = dict(C)
    cl.update(per_layer[1])
    dr1 = dict(fwqkv=dram["fwqkv1"], fwo=dram["fwo1"], fw1=dram["fw11"],
               fw2=dram["fw21"], fb1=dram["fb11"], fb2=dram["fb21"], tabs=[dram[f"tab{i}"] for i in range(4)],
               tabslo=[dram[f"tab{i}lo"] for i in range(4)],
               x1=d_x1, x=dram["x_ctx"], acc=d_acc, out=d_out, lname="1")
    emit_ffn_layer(nc, S, 1, None, cl, dr1)

    es.close()
    return nc


def emit_p0(nc, S, dram, xn, C, per_layer):
    with (
        nc.sbuf_tensor("p0_xt", [128, 2, D], F32) as xt,
        nc.sbuf_tensor("p0_xnt", [128, 2, D], BF16) as xnt,
        nc.sbuf_tensor("p0_sq", [128, 2, D], F32) as sq0,
        nc.sbuf_tensor("p0_st", [128, 16], F32) as st8,
        nc.psum_tensor("p0_TPa", [128, 1024], F32) as TPa,
        nc.psum_tensor("p0_TPb", [128, 1024], F32) as TPb,
    ):
        TP = [TPa, TPb]
        pl = Plan()
        dxs = [S("p0x0"), S("p0x1")]
        dcon, dwq = S("p0c"), S("p0w")
        s_sq, v_st, s_sd, v_rs, s_nm = S("p0sq"), S("p0vst"), S("p0sd"), S("p0vrs"), S("p0nm")
        pe_T, v_ev = S("p0T"), S("p0ev")

        def cl(dst, src):
            pl.step("sync", lambda e, d=dst, s=src: e.dma_start(d, s), incs=[(dcon, 16)])

        cl(C["bmask"][:, :], dram["bmask"][:, :])
        cl(C["logva"][:, :], dram["logva"][:, :])
        cl(C["logvf"][:, :], dram["logvf"][:, :])
        cl(C["idf"][:, :], dram["identc"][:, :])
        cl(C["halfa"][:, :], dram["halfac"][:, :])
        cl(C["halfb"][:, :], dram["halfbc"][:, :])
        for l in range(2):
            for nm in ("lng", "lnb", "qs", "qssw", "ks", "kssw", "qsr", "ksr"):
                cl(per_layer[l][nm][:, :], dram[f"{nm}{l}"][:, :])

        def gl(dst, src):
            pl.step("gpsimd", lambda e, d=dst, s=src: e.dma_start(d, s), incs=[(dwq, 16)])

        gl(C["idb"][:, :], dram["identc"][:, :])
        gl(C["r2sb"][:, :], dram["r2sc"][:, :])
        gl(C["trib"][:, :], dram["tric"][:, :])
        gl(C["antib"][:, :], dram["antic"][:, :])

        pl.step("vector", lambda e: e.memset(C["onesrow"][:, :], 1.0), incs=[(v_ev, 1)])
        pl.step("vector", lambda e: e.memset(C["onesf1"][:, :], 1.0), incs=[(v_ev, 1)])
        pl.step("vector", lambda e: e.memset(C["onescol"][:, :], 1.0), incs=[(v_ev, 1)])

        for i in range(32):
            b = i % 2
            t0 = 128 * i
            pl.step("sync",
                    lambda e, b=b, t0=t0: e.dma_start(xt[:, b, :], dram["x_ctx"][t0:t0 + 128, :]),
                    waits=pl.gate(("xt", b)), incs=[(dxs[b], 16)])
            pl.step("vector",
                    lambda e, b=b: e.reduce_sum(st8[:, 8 * b + 0:8 * b + 1], xt[:, b, :], axis=AX.X),
                    waits=w(dxs[b]))
            pl.step("scalar",
                    lambda e, b=b: e.activation(sq0[:, b, :], xt[:, b, :], AF.Square,
                                                accum_out=st8[:, 8 * b + 1:8 * b + 2]),
                    waits=w(dxs[b]), incs=[(s_sq, 1)])

            def _stats(e, b=b):
                c0 = 8 * b
                e.tensor_scalar_mul(st8[:, c0 + 2:c0 + 3], st8[:, c0 + 0:c0 + 1], 1.0 / D)
                e.tensor_scalar_mul(st8[:, c0 + 3:c0 + 4], st8[:, c0 + 1:c0 + 2], 1.0 / D)
                e.tensor_tensor(st8[:, c0 + 4:c0 + 5], st8[:, c0 + 2:c0 + 3], st8[:, c0 + 2:c0 + 3], ALU.mult)
                e.tensor_tensor(st8[:, c0 + 5:c0 + 6], st8[:, c0 + 3:c0 + 4], st8[:, c0 + 4:c0 + 5], ALU.subtract)
                return e.tensor_scalar_add(st8[:, c0 + 5:c0 + 6], st8[:, c0 + 5:c0 + 6], 1e-5)
            pl.step("vector", _stats, waits=w(s_sq), incs=[(v_st, 1)])
            pl.step("scalar",
                    lambda e, b=b: e.activation(st8[:, 8 * b + 6:8 * b + 7], st8[:, 8 * b + 5:8 * b + 6], AF.Sqrt),
                    waits=w(v_st), incs=[(s_sd, 1)])

            def _rs(e, b=b):
                c0 = 8 * b
                e.reciprocal(st8[:, c0 + 7:c0 + 8], st8[:, c0 + 6:c0 + 7])
                return e.tensor_scalar(st8[:, c0 + 4:c0 + 5], st8[:, c0 + 2:c0 + 3],
                                       st8[:, c0 + 7:c0 + 8], -1.0, ALU.mult, ALU.mult)
            pl.step("vector", _rs, waits=w(s_sd), incs=[(v_rs, 1)])
            pl.step("scalar",
                    lambda e, b=b: e.activation(xnt[:, b, :], xt[:, b, :], AF.Identity,
                                                bias=st8[:, 8 * b + 4:8 * b + 5],
                                                scale=st8[:, 8 * b + 7:8 * b + 8]),
                    waits=w(v_rs), incs=[(s_nm, 1)])
            pl.set_gate(("xt", b), s_nm)

            def _tr(e, b=b):
                inst = None
                for c in range(8):
                    inst = e.matmul(TP[b][:, 128 * c:128 * (c + 1)],
                                    xnt[:, b, 128 * c:128 * (c + 1)], C["idb"][:, :],
                                    start=True, stop=True)
                return inst
            pl.step("tensor", _tr, waits=w(s_nm, dwq) + pl.gate(("TP", b)), incs=[(pe_T, 1)])
            def _evx(e, b=b, t0=t0):
                inst = None
                for c in range(8):
                    inst = e.tensor_copy(xn[:, c, t0:t0 + 128], TP[b][:, 128 * c:128 * (c + 1)])
                return inst
            pl.step("vector", _evx, waits=w(pe_T), incs=[(v_ev, 1)])
            pl.set_gate(("TP", b), v_ev)

        pl.step("sync", lambda e: e.nop(), waits=w(dcon))
        pl.step("gpsimd", lambda e: e.nop(), waits=w(dwq))

        with nc.Block() as blk:
            pl.emit(nc, blk)


def emit_p1(nc, S, dram, xn, C, d_acc):
    with ExitStack() as xs:
        wqsl = xs.enter_context(nc.sbuf_tensor("p1_wq", [128, 2, 3, NCH, 128], BF16))
        bqkv = xs.enter_context(nc.sbuf_tensor("p1_bq", [1, 3 * D], BF16))
        bproj = xs.enter_context(nc.sbuf_tensor("p1_bp", [1, D], BF16))
        Kfm = xs.enter_context(nc.sbuf_tensor("p1_K", [128, T], BF16))
        Vtm = xs.enter_context(nc.sbuf_tensor("p1_V", [128, NCHUNK * 130], BF16))
        Qfm = xs.enter_context(nc.sbuf_tensor("p1_Q", [128, OWN], BF16))
        vtmp = xs.enter_context(nc.sbuf_tensor("p1_vt", [128, 512], BF16))
        PT = xs.enter_context(nc.sbuf_tensor("p1_PT", [128, 2, 1024], BF16))
        bb = xs.enter_context(nc.sbuf_tensor("p1_bb", [128, 40], F32))
        rl = xs.enter_context(nc.sbuf_tensor("p1_rl", [128, 1024], F32))
        ofm = xs.enter_context(nc.sbuf_tensor("p1_o", [128, NHG * OWN], BF16))
        wps = xs.enter_context(nc.sbuf_tensor("p1_wp", [128, NCH, D], BF16))
        stg = xs.enter_context(nc.sbuf_tensor("p1_stg", [128, 2, D], F32))
        ST0 = xs.enter_context(nc.psum_tensor("p1_ST0", [128, 1024], F32))
        ST1 = xs.enter_context(nc.psum_tensor("p1_ST1", [128, 1024], F32))
        AVA = xs.enter_context(nc.psum_tensor("p1_AVA", [128, 512], F32))
        AVB = xs.enter_context(nc.psum_tensor("p1_AVB", [128, 512], F32))
        RRA = xs.enter_context(nc.psum_tensor("p1_RRA", [128, 512], F32))
        RRB = xs.enter_context(nc.psum_tensor("p1_RRB", [128, 512], F32))

        STb = [ST0, ST1]
        pl = Plan()
        d_c1 = S("p1c1")
        d_w = [S("p1w0"), S("p1w1")]
        pe_mm, ev_q = S("p1mm"), S("p1ev")
        pe_st, s_exp, v_msk, pe_av = S("p1st"), S("p1exp"), S("p1msk"), S("p1av")
        v_rl, pe_rr, v_on = S("p1rl"), S("p1rr"), S("p1on")
        v_bnd, pe_bnd = S("p1bnd"), S("p1pebnd")
        v_stg = S("p1stg")
        d_o1 = [S("p1out0"), S("p1out1")]

        pl.step("gpsimd", lambda e: e.dma_start(bqkv[:, :], dram["bqkv_a"][:, :]), incs=[(d_c1, 16)])
        pl.step("gpsimd", lambda e: e.dma_start(bproj[:, :], dram["bproj"][:, :]), incs=[(d_c1, 16)])
        pl.step("gpsimd", lambda e: e.dma_start(wps[:, :, :], dram["wproj"][:, :, :]), incs=[(d_c1, 16)])
        pl.step("vector", lambda e: e.memset(AP(Vtm, 64, [[130 * NCHUNK, 128], [130, NCHUNK]]), 1.0),
                incs=[(v_on, 1)])
        pl.set_gate("ones", v_on)

        wq_ready = {}

        def prefetch(hg):
            hb = hg % 2
            for s3, col in ((0, hg * 128), (1, D + hg * 128), (2, 2 * D + hg * 128)):
                pl.step("gpsimd",
                        lambda e, hb=hb, s3=s3, col=col: e.dma_start(
                            wqsl[:, hb, s3, :, :],
                            AP(dram["wqkv_a"], col, [[NCH * 3 * D, 128], [3 * D, NCH], [1, 128]])),
                        waits=(w(pe_mm) if hg >= 2 else []), incs=[(d_w[hb], 16)])
            wq_ready[hg] = (d_w[hg % 2], d_w[hg % 2].n)

        prefetch(0)
        prefetch(1)
        for hg in range(NHG):
            hb = hg % 2

            def build_fm(dst, s3, t0s, own_off, col):
                for gi, t0 in enumerate(t0s):
                    ps = STb[gi % 2]

                    def _mm(e, t0=t0, s3=s3, col=col, ps=ps, hb=hb):
                        for c in range(NCH):
                            e.matmul(ps[:, 0:512], wqsl[:, hb, s3, c, :],
                                     xn[:, c, t0:t0 + 512], start=(c == 0), stop=False)
                        return e.matmul(ps[:, 0:512], bqkv[:, col:col + 128],
                                        C["onesrow"][:, :], start=False, stop=True)
                    pl.step("tensor", _mm,
                            waits=[wq_ready[hg]] + w(d_c1) + pl.gate(("ST", gi % 2)) + pl.gate("KQV"),
                            incs=[(pe_mm, 1)])
                    pl.step("scalar",
                            lambda e, dst=dst, t0=t0, ps=ps, oo=own_off:
                            e.copy(dst[:, t0 - oo:t0 - oo + 512], ps[:, 0:512]),
                            waits=w(pe_mm), incs=[(ev_q, 1)])
                    pl.set_gate(("ST", gi % 2), ev_q)

            build_fm(Kfm, 1, [512 * g for g in range(8)], 0, D + hg * 128)
            build_fm(Qfm, 0, [OWN + 512 * g for g in range(4)], OWN, hg * 128)

            for gi in range(8):
                t0 = 512 * gi
                ps = STb[gi % 2]
                av = [AVA, AVB][gi % 2]

                def _mmv(e, t0=t0, ps=ps, hb=hb, hg=hg):
                    for c in range(NCH):
                        e.matmul(ps[:, 0:512], wqsl[:, hb, 2, c, :],
                                 xn[:, c, t0:t0 + 512], start=(c == 0), stop=False)
                    return e.matmul(ps[:, 0:512], bqkv[:, 2 * D + hg * 128:2 * D + hg * 128 + 128],
                                    C["onesrow"][:, :], start=False, stop=True)
                pl.step("tensor", _mmv, waits=pl.gate(("ST", gi % 2)) + pl.gate("KQV") + pl.gate("vtmp"),
                        incs=[(pe_mm, 1)])
                pl.step("scalar", lambda e, ps=ps: e.copy(vtmp[:, :], ps[:, 0:512]),
                        waits=w(pe_mm), incs=[(ev_q, 1)])
                pl.set_gate(("ST", gi % 2), ev_q)

                def _trv(e, av=av):
                    inst = None
                    for j in range(4):
                        inst = e.matmul(av[:, 128 * j:128 * (j + 1)], vtmp[:, 128 * j:128 * (j + 1)],
                                        C["idb"][:, :], start=True, stop=True)
                    return inst
                pl.step("tensor", _trv, waits=w(ev_q) + pl.gate(("AV", gi % 2)) + pl.gate("ones"),
                        incs=[(pe_mm, 1)])
                pl.set_gate("vtmp", pe_mm)

                def _evv(e, gi=gi, av=av):
                    inst = None
                    for j in range(4):
                        base = (gi * 4 + j) * 130
                        inst = e.tensor_copy(AP(Vtm, base, [[130 * NCHUNK, 128], [1, 64]]),
                                             av[:, 128 * j:128 * j + 64])
                        inst = e.tensor_copy(AP(Vtm, base + 65, [[130 * NCHUNK, 128], [1, 64]]),
                                             av[:, 128 * j + 64:128 * j + 128])
                    return inst
                pl.step("vector", _evv, waits=w(pe_mm), incs=[(v_on, 1), (ev_q, 1)])
                pl.set_gate(("AV", gi % 2), v_on)

            if hg + 2 < NHG:
                prefetch(hg + 2)

            pl.step("vector", lambda e: e.reduce_max(bb[:, 33:34], Kfm[:, :], axis=AX.X,
                                                     apply_absolute_value=True),
                    waits=w(ev_q, s_exp), incs=[(v_bnd, 1)])
            pl.step("vector", lambda e: e.reduce_max(bb[:, 34:35], Qfm[:, :], axis=AX.X,
                                                     apply_absolute_value=True),
                    waits=w(ev_q), incs=[(v_bnd, 1)])
            pl.step("vector", lambda e: e.tensor_tensor(bb[:, 35:36], bb[:, 33:34], bb[:, 34:35], ALU.mult),
                    waits=w(v_bnd), incs=[(v_bnd, 1)])
            pl.step("tensor", lambda e: e.matmul(RRA[0:1, 0:1], C["onescol"][:, :], bb[:, 35:36],
                                                 start=True, stop=True),
                    waits=w(v_bnd) + pl.gate("RRA"), incs=[(pe_bnd, 1)])
            pl.step("vector", lambda e: e.tensor_copy(bb[0:1, 36:37], RRA[0:1, 0:1]),
                    waits=w(pe_bnd), incs=[(v_bnd, 1)])
            pl.set_gate("RRA", v_bnd)
            pl.step("tensor", lambda e: e.matmul(RRB[:, 0:1], C["onesf1"][:, :], bb[0:1, 36:37],
                                                 start=True, stop=True),
                    waits=w(v_bnd) + pl.gate("RRB"), incs=[(pe_bnd, 1)])

            def _fixb(e):
                e.tensor_scalar(bb[:, 37:38], RRB[:, 0:1], -0.125, 40.0, ALU.mult, ALU.add)
                return e.tensor_scalar_add(bb[:, 0:32], C["logva"][:, :], bb[:, 37:38])
            pl.step("vector", _fixb, waits=w(pe_bnd), incs=[(v_bnd, 1)])
            pl.set_gate("RRB", v_bnd)

            for qg in range(4):
                q0 = 512 * qg
                qt0 = 16 + 4 * qg
                kcmax = qt0 + 4
                for kc in range(kcmax):
                    bu = kc % 2
                    stp = STb[bu]

                    def _st(e, kc=kc, q0=q0, stp=stp):
                        e.matmul(stp[:, 0:512], Kfm[0:64, 128 * kc:128 * (kc + 1)],
                                 Qfm[0:64, q0:q0 + 512], start=True, stop=True)
                        return e.matmul(stp[:, 512:1024], Kfm[64:128, 128 * kc:128 * (kc + 1)],
                                        Qfm[64:128, q0:q0 + 512], start=True, stop=True)
                    pl.step("tensor", _st, waits=pl.gate(("ST", bu)) + w(v_bnd), incs=[(pe_st, 1)])
                    pl.step("scalar",
                            lambda e, bu=bu, kc=kc, stp=stp: e.activation(
                                PT[:, bu, :], stp[:, :], AF.Exp,
                                bias=bb[:, kc:kc + 1], scale=0.125),
                            waits=w(pe_st) + pl.gate(("PT", bu)), incs=[(s_exp, 1)])
                    pl.set_gate(("ST", bu), s_exp)
                    need_mask = kc >= qt0
                    if need_mask:
                        def _mask(e, kc=kc, qt0=qt0, bu=bu):
                            inst = None
                            for h in range(2):
                                for i4 in range(4):
                                    qt = qt0 + i4
                                    sl = PT[:, bu, 512 * h + 128 * i4: 512 * h + 128 * (i4 + 1)]
                                    if kc == qt:
                                        inst = e.tensor_tensor(sl, sl, C["trib"][:, :], ALU.mult)
                                    elif kc > qt:
                                        inst = e.memset(sl, 0.0)
                            return inst
                        pl.step("vector", _mask, waits=w(s_exp), incs=[(v_msk, 1)])

                    def _av(e, kc=kc, bu=bu, first=(kc == 0), last=(kc == kcmax - 1)):
                        e.matmul(AVA[0:65, :], Vtm[:, 130 * kc:130 * kc + 65], PT[:, bu, 0:512],
                                 start=first, stop=last)
                        e.matmul(AVB[0:1, :], Vtm[:, 130 * kc + 64:130 * kc + 65], PT[:, bu, 512:1024],
                                 start=first, stop=last)
                        return e.matmul(AVB[64:128, :], Vtm[:, 130 * kc + 65:130 * kc + 129],
                                        PT[:, bu, 512:1024], start=first, stop=last)
                    avw = (w(v_msk) if need_mask else w(s_exp))
                    if kc == 0:
                        avw = avw + pl.gate(("AV", 0)) + pl.gate(("AV", 1))
                    pl.step("tensor", _av, waits=avw, incs=[(pe_av, 1)])
                    pl.set_gate(("PT", bu), pe_av)

                def _rlf(e):
                    e.copy(rl[0:1, 0:512], AVA[64:65, :])
                    return e.copy(rl[0:1, 512:1024], AVB[0:1, :])
                pl.step("scalar", _rlf, waits=w(pe_av), incs=[(v_rl, 1)])

                def _rep(e):
                    e.matmul(RRA[:, :], C["onesf1"][:, :], rl[0:1, 0:512], start=True, stop=True)
                    return e.matmul(RRB[:, :], C["onesf1"][:, :], rl[0:1, 512:1024], start=True, stop=True)
                pl.step("tensor", _rep, waits=w(v_rl) + pl.gate("RRA") + pl.gate("RRB"), incs=[(pe_rr, 1)])

                def _rcp(e):
                    e.tensor_scalar_add(rl[:, 0:512], RRA[:, :], 1e-38)
                    e.reciprocal(rl[:, 0:512], rl[:, 0:512])
                    e.tensor_scalar_add(rl[:, 512:1024], RRB[:, :], 1e-38)
                    return e.reciprocal(rl[:, 512:1024], rl[:, 512:1024])
                pl.step("vector", _rcp, waits=w(pe_rr), incs=[(v_rl, 1)])
                pl.set_gate("RRA", v_rl)
                pl.set_gate("RRB", v_rl)

                def _norm(e, hg=hg, q0=q0):
                    o0 = hg * OWN + q0
                    e.tensor_tensor(ofm[0:64, o0:o0 + 512], AVA[0:64, :], rl[0:64, 0:512], ALU.mult)
                    return e.tensor_tensor(ofm[64:128, o0:o0 + 512], AVB[64:128, :],
                                           rl[64:128, 512:1024], ALU.mult)
                pl.step("vector", _norm, waits=w(v_rl), incs=[(v_on, 1), (v_rl, 1)])
                pl.set_gate(("AV", 0), v_on)
                pl.set_gate(("AV", 1), v_on)
            pl.set_gates = None
            pl.G["KQV"] = w(pe_st) + w(pe_av) + w(s_exp)

        for tt in range(16):
            bu = tt % 2
            ps = STb[bu]

            def _pmm(e, tt=tt, ps=ps):
                inst = None
                for half in range(2):
                    for c in range(NCH):
                        e.matmul(ps[:, 512 * half:512 * (half + 1)],
                                 ofm[:, c * OWN + 128 * tt: c * OWN + 128 * (tt + 1)],
                                 wps[:, c, 512 * half:512 * (half + 1)],
                                 start=(c == 0), stop=False)
                    inst = e.matmul(ps[:, 512 * half:512 * (half + 1)],
                                    C["onesrow"][:, 0:128], bproj[:, 512 * half:512 * (half + 1)],
                                    start=False, stop=True)
                return inst
            pl.step("tensor", _pmm, waits=w(v_on) + pl.gate(("ST", bu)), incs=[(pe_mm, 1)])
            pl.step("vector", lambda e, bu=bu, ps=ps: e.tensor_copy(stg[:, bu, :], ps[:, :]),
                    waits=w(pe_mm) + pl.gate(("stg", bu)), incs=[(v_stg, 1)])
            pl.set_gate(("ST", bu), v_stg)
            pl.step("sync", lambda e, tt=tt, bu=bu: e.dma_start(d_acc[128 * tt:128 * (tt + 1), :], stg[:, bu, :]),
                    waits=w(v_stg), incs=[(d_o1[bu], 16)])
            pl.set_gate(("stg", bu), d_o1[bu])
        pl.step("sync", lambda e: e.nop(), waits=w(d_o1[0]) + w(d_o1[1]))

        with nc.Block() as blk:
            pl.emit(nc, blk)


def emit_ffn_layer(nc, S, L, xn, C, dr):
    QC0 = L1_QKV_C0 if L == 0 else L2_QKV_C0
    AC0 = L1_ATT_C0 if L == 0 else L2_ATT_C0
    DOMQ = (NCHUNK - QC0) * 128
    DOMA = (NCHUNK - AC0) * 128
    QT0 = QC0 * 128
    AT0 = AC0 * 128
    NKC = NCHUNK - QC0

    qgs = []
    c = AC0
    while c < NCHUNK:
        qgs.append((c, min(c + 4, NCHUNK)))
        c = min(c + 4, NCHUNK)

    with ExitStack() as outer:
        off = outer.enter_context(nc.sbuf_tensor(f"f{L}_o", [128, 4 * DOMA], BF16))
        _ffn_attn(nc, S, L, xn, C, dr, off, QC0, AC0, DOMQ, DOMA, QT0, NKC, qgs)
        _ffn_mlp(nc, S, L, xn, C, dr, off, AC0, DOMA, AT0, qgs)


def _ffn_attn(nc, S, L, xn, C, dr, off, QC0, AC0, DOMQ, DOMA, QT0, NKC, qgs):
    with ExitStack() as xs:
        srcbf = None
        if L == 1:
            srcbf = xs.enter_context(nc.sbuf_tensor(f"f{L}_src", [128, NCH, DOMQ + 128], BF16))
        wqf = xs.enter_context(nc.sbuf_tensor(f"f{L}_wq", [128, NCH, 3 * FINNER], BF16))
        Dt = xs.enter_context(nc.sbuf_tensor(f"f{L}_Dt", [128, 4, TAB_N], F32))
        qk = xs.enter_context(nc.sbuf_tensor(f"f{L}_qk", [128, 2, DOMQ], BF16))
        Vtf = xs.enter_context(nc.sbuf_tensor(f"f{L}_V", [128, NKC * 130], BF16))
        PTf = xs.enter_context(nc.sbuf_tensor(f"f{L}_PT", [128, 2, 1024], BF16))
        bbf = xs.enter_context(nc.sbuf_tensor(f"f{L}_bb", [128, 40], F32))
        rlw = xs.enter_context(nc.sbuf_tensor(f"f{L}_rl", [128, 1024], F32))
        vtmpf = xs.enter_context(nc.sbuf_tensor(f"f{L}_vt", [128, 512], BF16))
        q2 = xs.enter_context(nc.sbuf_tensor(f"f{L}_q2", [128, 512], F32))
        sst = xs.enter_context(nc.sbuf_tensor(f"f{L}_sst", [1, 2, 512], F32))
        tbuf = xs.enter_context(nc.sbuf_tensor(f"f{L}_tb", [128, 2, 512], F32))
        ST0 = xs.enter_context(nc.psum_tensor(f"f{L}_ST0", [128, 1024], F32))
        ST1 = xs.enter_context(nc.psum_tensor(f"f{L}_ST1", [128, 1024], F32))
        AVA = xs.enter_context(nc.psum_tensor(f"f{L}_AVA", [128, 512], F32))
        AVB = xs.enter_context(nc.psum_tensor(f"f{L}_AVB", [128, 512], F32))
        RRA = xs.enter_context(nc.psum_tensor(f"f{L}_RRA", [128, 512], F32))
        RRB = xs.enter_context(nc.psum_tensor(f"f{L}_RRB", [128, 512], F32))

        STb = [ST0, ST1]
        pl = Plan()
        d_w, d_tab, d_src = S("fAw"), S("fAtab"), S("fAsrc")
        d_tlo, v_tlo = S("fAtlo"), S("fAvtlo")
        v_dt = S("fAdt")
        pe_mm, ev_q, s_ev = S("fAmm"), S("fAev"), S("fAsev")
        pe_rot, g_fin, v_fin = S("fArot"), S("fAgfin"), S("fAfin")
        pe_ssq, v_ssq, s_ssq = S("fApssq"), S("fAvssq"), S("fAsssq")
        pe_rep = S("fArep")
        pe_st, s_exp, v_msk, pe_av = S("fAst"), S("fAexp"), S("fAmsk"), S("fAav")
        v_rl, pe_rr, v_on = S("fArl"), S("fArr"), S("fAon")
        v_bnd, pe_bnd = S("fAbnd"), S("fApebnd")
        v_bfx = S("fAbfx")

        pl.step("gpsimd", lambda e: e.dma_start(wqf[:, :, :], dr["fwqkv"][:, :, :]), incs=[(d_w, 16)])
        for i in range(4):
            pl.step("sync", lambda e, i=i: e.dma_start(Dt[:, i, :], dr["tabs"][i][:, :]),
                    incs=[(d_tab, 16)])
        qkst = AP(qk, 0, [[2 * DOMQ, 128], [1, TAB_N]])
        for i in range(4):
            pl.step("gpsimd", lambda e, i=i: e.dma_start(qkst, dr["tabslo"][i][:, :]),
                    waits=pl.gate("tlo"), incs=[(d_tlo, 16)])
            pl.step("vector",
                    lambda e, i=i: e.tensor_tensor(Dt[:, i, :], Dt[:, i, :], qkst, ALU.add),
                    waits=w(d_tab) + w(d_tlo), incs=[(v_tlo, 1)])
            pl.set_gate("tlo", v_tlo)
        if L == 1:
            off0 = QT0 - 128 - X1T0
            pl.step("gpsimd",
                    lambda e: e.dma_start(srcbf[:, :, :],
                                          AP(dr["x1"], off0, [[X1N, 128], [128 * X1N, NCH], [1, DOMQ + 128]])),
                    incs=[(d_src, 16)])
            bcol = 2047 - (QT0 - 128)
            def _bfix(e, bcol=bcol):
                return e.tensor_scalar_mul(
                    AP(srcbf, bcol, [[(DOMQ + 128) * NCH, 128], [DOMQ + 128, NCH]]),
                    AP(srcbf, bcol, [[(DOMQ + 128) * NCH, 128], [DOMQ + 128, NCH]]),
                    C["bmask"][:, :])
            pl.step("vector", _bfix, waits=w(d_src), incs=[(v_bfx, 1)])

        def src_sl(c, t0, nn):
            sh = 1 if c >= 4 else 0
            if L == 0:
                return xn[:, c, t0 - sh:t0 - sh + nn]
            b0 = QT0 - 128
            return srcbf[:, c, t0 - sh - b0:t0 - sh - b0 + nn]

        def _dt(e):
            e.tensor_scalar_mul(Dt[:, 0, :], Dt[:, 0, :], C["qs"][:, :])
            e.tensor_scalar_mul(Dt[:, 1, :], Dt[:, 1, :], C["qssw"][:, :])
            e.tensor_scalar_mul(Dt[:, 2, :], Dt[:, 2, :], C["ks"][:, :])
            return e.tensor_scalar_mul(Dt[:, 3, :], Dt[:, 3, :], C["kssw"][:, :])
        pl.step("vector", _dt, waits=w(d_tab), incs=[(v_dt, 1)])
        pl.step("vector", lambda e: e.memset(AP(Vtf, 64, [[130 * NKC, 128], [130, NKC]]), 1.0),
                incs=[(v_on, 1)])
        pl.set_gate("fones", v_on)
        pl.step("vector", lambda e: e.reduce_max(bbf[0:1, 33:34], C["qsr"][:, :], axis=AX.X,
                                                 apply_absolute_value=True), incs=[(v_bnd, 1)])
        pl.step("vector", lambda e: e.reduce_max(bbf[0:1, 34:35], C["ksr"][:, :], axis=AX.X,
                                                 apply_absolute_value=True), incs=[(v_bnd, 1)])
        pl.step("vector", lambda e: e.tensor_tensor(bbf[0:1, 35:36], bbf[0:1, 33:34],
                                                    bbf[0:1, 34:35], ALU.mult),
                waits=w(v_bnd), incs=[(v_bnd, 1)])
        pl.step("tensor", lambda e: e.matmul(RRB[:, 0:1], C["onesf1"][:, :], bbf[0:1, 35:36],
                                             start=True, stop=True),
                waits=w(v_bnd), incs=[(pe_bnd, 1)])

        def _fixb(e):
            e.tensor_scalar(bbf[:, 36:37], RRB[:, 0:1], -QK_SCALE, 20.0, ALU.mult, ALU.add)
            return e.tensor_scalar_add(bbf[:, 0:32], C["logvf"][:, :], bbf[:, 36:37])
        pl.step("vector", _fixb, waits=w(pe_bnd), incs=[(v_bnd, 1)])

        grp = []
        t = QT0
        while t < T:
            grp.append((t, min(t + 512, T) - t))
            t = min(t + 512, T)

        for hg in range(4):
            qcol, kcol, vcol = hg * 128, FINNER + hg * 128, 2 * FINNER + hg * 128

            for qi, col in ((0, qcol), (1, kcol)):
                for gi, (t0, nn) in enumerate(grp):
                    ps = STb[gi % 2]

                    def _mm(e, t0=t0, nn=nn, col=col, ps=ps):
                        inst = None
                        for c in range(NCH):
                            inst = e.matmul(ps[:, 0:nn], wqf[:, c, col:col + 128],
                                            src_sl(c, t0, nn), start=(c == 0), stop=(c == NCH - 1))
                        return inst
                    wt = w(d_w) + pl.gate(("fST", gi % 2)) + pl.gate("fqkKV") + (w(d_src, v_bfx) if L == 1 else [])
                    pl.step("tensor", _mm, waits=wt, incs=[(pe_mm, 1)])
                    pl.step("scalar",
                            lambda e, qi=qi, t0=t0, nn=nn, ps=ps: e.copy(
                                qk[:, qi, t0 - QT0:t0 - QT0 + nn], ps[:, 0:nn]),
                            waits=w(pe_mm) + w(v_tlo), incs=[(ev_q, 1), (s_ev, 1)])
                    pl.set_gate(("fST", gi % 2), ev_q)

            for gi, (t0, nn) in enumerate(grp):
                ps = STb[gi % 2]
                av = [AVA, AVB][gi % 2]

                def _mmv(e, t0=t0, nn=nn, ps=ps, vcol=vcol):
                    inst = None
                    for c in range(NCH):
                        inst = e.matmul(ps[:, 0:nn], wqf[:, c, vcol:vcol + 128],
                                        src_sl(c, t0, nn), start=(c == 0), stop=(c == NCH - 1))
                    return inst
                pl.step("tensor", _mmv, waits=pl.gate(("fST", gi % 2)) + pl.gate("fqkKV") + pl.gate("fvt"),
                        incs=[(pe_mm, 1)])
                pl.step("scalar", lambda e, ps=ps, nn=nn: e.copy(vtmpf[:, 0:nn], ps[:, 0:nn]),
                        waits=w(pe_mm), incs=[(ev_q, 1)])
                pl.set_gate(("fST", gi % 2), ev_q)
                nt = nn // 128

                def _trv(e, nt=nt, av=av):
                    inst = None
                    for j in range(nt):
                        inst = e.matmul(av[:, 128 * j:128 * (j + 1)], vtmpf[:, 128 * j:128 * (j + 1)],
                                        C["idb"][:, :], start=True, stop=True)
                    return inst
                pl.step("tensor", _trv, waits=w(ev_q) + pl.gate(("fAV", gi % 2)) + pl.gate("fones"),
                        incs=[(pe_mm, 1)])
                pl.set_gate("fvt", pe_mm)

                def _evv(e, gi=gi, nt=nt, av=av):
                    inst = None
                    for j in range(nt):
                        base = (gi * 4 + j) * 130
                        inst = e.tensor_copy(AP(Vtf, base, [[130 * NKC, 128], [1, 64]]),
                                             av[:, 128 * j:128 * j + 64])
                        inst = e.tensor_copy(AP(Vtf, base + 65, [[130 * NKC, 128], [1, 64]]),
                                             av[:, 128 * j + 64:128 * j + 128])
                    return inst
                pl.step("vector", _evv, waits=w(pe_mm), incs=[(v_on, 1), (ev_q, 1)])
                pl.set_gate(("fAV", gi % 2), v_on)

            for qi in range(2):
                D1 = Dt[:, 0, :] if qi == 0 else Dt[:, 2, :]
                D2 = Dt[:, 1, :] if qi == 0 else Dt[:, 3, :]
                for gi, (t0, nn) in enumerate(grp):
                    o = t0 - QT0
                    to = t0 - TAB_T0
                    ps = STb[gi % 2]
                    pl.step("scalar",
                            lambda e, qi=qi, o=o, nn=nn: e.activation(
                                q2[:, 0:nn], qk[:, qi, o:o + nn], AF.Square),
                            waits=w(s_ev) + pl.gate("q2"), incs=[(s_ssq, 1)])

                    def _sm(e, o=o, nn=nn):
                        e.matmul(RRA[0:1, 0:nn], C["onescol"][0:64, :], q2[0:64, 0:nn],
                                 start=True, stop=True)
                        return e.matmul(RRB[0:1, 0:nn], C["onescol"][64:128, :], q2[64:128, 0:nn],
                                        start=True, stop=True)
                    pl.step("tensor", _sm, waits=w(s_ssq) + pl.gate("fRRA") + pl.gate("fRRB"),
                            incs=[(pe_ssq, 1)])

                    def _cps(e, nn=nn):
                        e.tensor_copy(sst[0:1, 0, 0:nn], RRA[0:1, 0:nn])
                        return e.tensor_copy(sst[0:1, 1, 0:nn], RRB[0:1, 0:nn])
                    pl.step("vector", _cps, waits=w(pe_ssq) + pl.gate("sst"), incs=[(v_ssq, 1)])
                    pl.set_gate("fRRA", v_ssq)
                    sstf = sst.ap().rearrange("p a b -> p (a b)")
                    pl.step("vector", lambda e, sstf=sstf: e.tensor_scalar_add(
                        sstf[0:1, :], sstf[0:1, :], 1e-24),
                        waits=w(v_ssq), incs=[(v_ssq, 1)])
                    pl.step("scalar", lambda e, sstf=sstf: e.activation(
                        sstf[0:1, :], sstf[0:1, :], AF.Sqrt),
                        waits=w(v_ssq), incs=[(s_ssq, 1)])

                    def _rep(e, nn=nn):
                        e.matmul(RRB[:, 0:nn], C["halfa"][:, :], sst[0:1, 0, 0:nn],
                                 start=True, stop=False)
                        return e.matmul(RRB[:, 0:nn], C["halfb"][:, :], sst[0:1, 1, 0:nn],
                                        start=False, stop=True)
                    pl.step("tensor", _rep, waits=w(s_ssq) + pl.gate("fRRB"), incs=[(pe_rep, 1)])
                    pl.step("vector", lambda e, nn=nn: e.reciprocal(q2[:, 0:nn], RRB[:, 0:nn]),
                            waits=w(pe_rep), incs=[(v_fin, 1)])
                    pl.step("tensor",
                            lambda e, qi=qi, o=o, nn=nn, ps=ps: e.matmul(
                                ps[:, 0:nn], C["r2sb"][:, :], qk[:, qi, o:o + nn],
                                start=True, stop=True),
                            waits=w(s_ssq) + pl.gate(("fST", gi % 2)), incs=[(pe_rot, 1)])
                    pl.step("gpsimd",
                            lambda e, qi=qi, o=o, nn=nn, D1=D1, to=to: e.tensor_tensor(
                                tbuf[:, 0, 0:nn], qk[:, qi, o:o + nn], D1[:, to:to + nn], ALU.mult),
                            waits=w(s_ssq, v_dt) + pl.gate("tb"), incs=[(g_fin, 1)])
                    pl.step("vector",
                            lambda e, nn=nn, ps=ps, D2=D2, to=to: e.tensor_tensor(
                                tbuf[:, 1, 0:nn], ps[:, 0:nn], D2[:, to:to + nn], ALU.mult),
                            waits=w(pe_rot, v_dt) + pl.gate("tb"), incs=[(v_fin, 1), (ev_q, 1)])
                    pl.set_gate(("fST", gi % 2), ev_q)

                    def _fin(e, qi=qi, o=o, nn=nn):
                        e.tensor_tensor(tbuf[:, 0, 0:nn], tbuf[:, 0, 0:nn], tbuf[:, 1, 0:nn], ALU.add)
                        return e.tensor_tensor(qk[:, qi, o:o + nn], tbuf[:, 0, 0:nn],
                                               q2[:, 0:nn], ALU.mult)
                    pl.step("vector", _fin, waits=w(g_fin, pe_rep), incs=[(v_fin, 1)])
                    pl.set_gate("tb", v_fin)
                    pl.set_gate("sst", pe_rep)
                    pl.set_gate("fRRB", v_fin)
                    pl.set_gate("q2", v_fin)

            for (qc0, qc1) in qgs:
                nq = (qc1 - qc0) * 128
                q0 = qc0 * 128 - QT0
                kclist = [kc for kc in range(qc0 - 4, qc1) if QC0 <= kc < NCHUNK]
                for ki, kc in enumerate(kclist):
                    bu = ki % 2
                    stp = STb[bu]
                    ko = (kc - QC0) * 128

                    def _st(e, ko=ko, q0=q0, nq=nq, stp=stp):
                        e.matmul(stp[:, 0:nq], qk[0:64, 1, ko:ko + 128],
                                 qk[0:64, 0, q0:q0 + nq], start=True, stop=True)
                        return e.matmul(stp[:, 512:512 + nq], qk[64:128, 1, ko:ko + 128],
                                        qk[64:128, 0, q0:q0 + nq], start=True, stop=True)
                    pl.step("tensor", _st, waits=w(v_fin, v_bnd) + pl.gate(("fST", bu)), incs=[(pe_st, 1)])

                    def _exp(e, bu=bu, kc=kc, nq=nq, stp=stp):
                        e.activation(PTf[:, bu, 0:nq], stp[:, 0:nq], AF.Exp,
                                     bias=bbf[:, kc:kc + 1], scale=QK_SCALE)
                        return e.activation(PTf[:, bu, 512:512 + nq], stp[:, 512:512 + nq], AF.Exp,
                                            bias=bbf[:, kc:kc + 1], scale=QK_SCALE)
                    pl.step("scalar", _exp, waits=w(pe_st) + pl.gate(("fPT", bu)),
                            incs=[(s_exp, 1)])
                    pl.set_gate(("fST", bu), s_exp)

                    acts = []
                    for i4 in range(qc1 - qc0):
                        qt = qc0 + i4
                        if kc == qt:
                            acts.append((i4, "tri"))
                        elif kc == qt - 4:
                            acts.append((i4, "anti"))
                        elif kc > qt or kc < qt - 4:
                            acts.append((i4, "zero"))
                    need_mask = bool(acts)
                    if need_mask:
                        def _mask(e, acts=tuple(acts), bu=bu):
                            inst = None
                            for h in range(2):
                                for i4, kind in acts:
                                    sl = PTf[:, bu, 512 * h + 128 * i4: 512 * h + 128 * (i4 + 1)]
                                    if kind == "tri":
                                        inst = e.tensor_tensor(sl, sl, C["trib"][:, :], ALU.mult)
                                    elif kind == "anti":
                                        inst = e.tensor_tensor(sl, sl, C["antib"][:, :], ALU.mult)
                                    else:
                                        inst = e.memset(sl, 0.0)
                            return inst
                        pl.step("vector", _mask, waits=w(s_exp), incs=[(v_msk, 1)])

                    def _av(e, kc=kc, bu=bu, nq=nq, first=(ki == 0), last=(ki == len(kclist) - 1)):
                        kb = (kc - QC0) * 130
                        e.matmul(AVA[0:65, 0:nq], Vtf[:, kb:kb + 65], PTf[:, bu, 0:nq],
                                 start=first, stop=last)
                        e.matmul(AVB[0:1, 0:nq], Vtf[:, kb + 64:kb + 65], PTf[:, bu, 512:512 + nq],
                                 start=first, stop=last)
                        return e.matmul(AVB[64:128, 0:nq], Vtf[:, kb + 65:kb + 129],
                                        PTf[:, bu, 512:512 + nq], start=first, stop=last)
                    avw = (w(v_msk) if need_mask else w(s_exp))
                    if ki == 0:
                        avw = avw + pl.gate(("fAV", 0)) + pl.gate(("fAV", 1))
                    pl.step("tensor", _av, waits=avw, incs=[(pe_av, 1)])
                    pl.set_gate(("fPT", bu), pe_av)

                def _rlf(e, nq=nq):
                    e.copy(rlw[0:1, 0:nq], AVA[64:65, 0:nq])
                    return e.copy(rlw[0:1, 512:512 + nq], AVB[0:1, 0:nq])
                pl.step("scalar", _rlf, waits=w(pe_av) + pl.gate("rlw"), incs=[(v_rl, 1)])

                def _rrm(e, nq=nq):
                    e.matmul(RRA[:, 0:nq], C["onesf1"][:, :], rlw[0:1, 0:nq], start=True, stop=True)
                    return e.matmul(RRB[:, 0:nq], C["onesf1"][:, :], rlw[0:1, 512:512 + nq],
                                    start=True, stop=True)
                pl.step("tensor", _rrm, waits=w(v_rl) + pl.gate("fRRA") + pl.gate("fRRB"), incs=[(pe_rr, 1)])

                def _rcpf(e, nq=nq):
                    e.tensor_scalar_add(rlw[:, 0:nq], RRA[:, 0:nq], 1e-30)
                    e.reciprocal(rlw[:, 0:nq], rlw[:, 0:nq])
                    e.tensor_scalar_add(rlw[:, 512:512 + nq], RRB[:, 0:nq], 1e-30)
                    return e.reciprocal(rlw[:, 512:512 + nq], rlw[:, 512:512 + nq])
                pl.step("vector", _rcpf, waits=w(pe_rr), incs=[(v_rl, 1)])
                pl.set_gate("fRRA", v_rl)
                pl.set_gate("fRRB", v_rl)

                def _norm(e, hg=hg, qc0=qc0, nq=nq):
                    o0 = hg * DOMA + (qc0 - AC0) * 128
                    e.tensor_tensor(off[0:64, o0:o0 + nq], AVA[0:64, 0:nq], rlw[0:64, 0:nq], ALU.mult)
                    return e.tensor_tensor(off[64:128, o0:o0 + nq], AVB[64:128, 0:nq],
                                           rlw[64:128, 512:512 + nq], ALU.mult)
                pl.step("vector", _norm, waits=w(v_rl), incs=[(v_on, 1), (v_rl, 1)])
                pl.set_gate(("fAV", 0), v_on)
                pl.set_gate(("fAV", 1), v_on)
                pl.set_gate("rlw", v_on)
            pl.G["fqkKV"] = w(pe_st) + w(pe_av)

        with nc.Block() as blk:
            pl.emit(nc, blk)


def _ffn_mlp(nc, S, L, xn, C, dr, off, AC0, DOMA, AT0, qgs):
    with ExitStack() as xs:
        wos = xs.enter_context(nc.sbuf_tensor(f"m{L}_wo", [128, 4, D], BF16))
        fb1 = xs.enter_context(nc.sbuf_tensor(f"m{L}_fb1", [1, H2P], BF16))
        fb2 = xs.enter_context(nc.sbuf_tensor(f"m{L}_fb2", [1, D], BF16))
        w1a = xs.enter_context(nc.sbuf_tensor(f"m{L}_w1a", [128, 2, NCH, 256], BF16))
        w1g = xs.enter_context(nc.sbuf_tensor(f"m{L}_w1g", [128, 2, NCH, 256], BF16))
        w2s = xs.enter_context(nc.sbuf_tensor(f"m{L}_w2", [128, 2, 2, D], BF16))
        xmid = xs.enter_context(nc.sbuf_tensor(f"m{L}_xm", [128, NCH, 513], F32))
        hln = xs.enter_context(nc.sbuf_tensor(f"m{L}_h", [128, NCH, 512], BF16))
        gel = xs.enter_context(nc.sbuf_tensor(f"m{L}_g", [128, 512], BF16))
        x1g = xs.enter_context(nc.sbuf_tensor(f"m{L}_x1", [128, NCH, 512], F32))
        xsq = xs.enter_context(nc.sbuf_tensor(f"m{L}_xsq", [128, 1, 512], F32))
        mbuf = xs.enter_context(nc.sbuf_tensor(f"m{L}_m", [128, 22 * 512], BF16))
        lnst = xs.enter_context(nc.sbuf_tensor(f"m{L}_ln", [1, 1024], F32))
        baseb = xacc = None
        if L == 1:
            baseb = xs.enter_context(nc.sbuf_tensor(f"m{L}_base", [128, 2, NCH, 512], F32))
            xacc = xs.enter_context(nc.sbuf_tensor(f"m{L}_xacc", [128, 2, 2 * D], F32))
        ST0 = xs.enter_context(nc.psum_tensor(f"m{L}_ST0", [128, 1024], F32))
        ST1 = xs.enter_context(nc.psum_tensor(f"m{L}_ST1", [128, 1024], F32))
        RRA = xs.enter_context(nc.psum_tensor(f"m{L}_RRA", [128, 512], F32))
        RRB = xs.enter_context(nc.psum_tensor(f"m{L}_RRB", [128, 512], F32))
        TPp = xs.enter_context(nc.psum_tensor(f"m{L}_TP", [128, 1024], F32))

        STb = [ST0, ST1]
        w2slot = [ST0[:, 0:512], ST0[:, 512:1024], ST1[:, 0:512], ST1[:, 512:1024],
                  TPp[:, 0:512], TPp[:, 512:1024], RRA[:, 0:512], RRB[:, 0:512]]
        pl = Plan()
        d_wo = S("mBwo")
        d_w1 = [S("mBw1a"), S("mBw1b")]
        d_w2 = [S("mBw2a"), S("mBw2b")]
        d_base = [S("mBbase0"), S("mBbase1")]
        d_xin = [S("mBxin0"), S("mBxin1")]
        d_oo = [S("mBoo0"), S("mBoo1")]
        d_x1o = S("mBx1o")
        pe_wo, v_xm = S("mBpewo"), S("mBvxm")
        s_sq2, pe_stat = S("mBssq"), S("mBstat")
        v_ln, s_ln = S("mBvln"), S("mBsln")
        pe_rep2, v_u, s_hln = S("mBrep"), S("mBu"), S("mBhln")
        pe_w1, s_gel, v_m = S("mBw1"), S("mBgel"), S("mBm")
        pe_w2, v_x1 = S("mBw2"), S("mBx1")
        pe_tp, v_fin2 = S("mBtp"), S("mBfin")
        v_bfx2 = S("mBbfx")

        pl.step("gpsimd", lambda e: e.dma_start(wos[:, :, :], dr["fwo"][:, :, :]), incs=[(d_wo, 16)])
        pl.step("gpsimd", lambda e: e.dma_start(fb1[:, :], dr["fb1"][:, :]), incs=[(d_wo, 16)])
        pl.step("gpsimd", lambda e: e.dma_start(fb2[:, :], dr["fb2"][:, :]), incs=[(d_wo, 16)])

        w1blocks = [(j * 256, (j + 1) * 256) for j in range(11)]

        OUT0 = 2048

        for g, (gc0, gc1) in enumerate(qgs):
            W = (gc1 - gc0) * 128
            t0 = gc0 * 128
            gb = g % 2

            if g == 0:
                pl.step("vector", lambda e: e.memset(AP(xmid, 0, [[513 * NCH, 128], [513, NCH]]), 0.0),
                        waits=pl.gate("xmid"), incs=[(v_xm, 1)])
            else:
                Wp = Wprev
                pl.step("vector",
                        lambda e, Wp=Wp: e.tensor_copy(
                            AP(xmid, 0, [[513 * NCH, 128], [513, NCH]]),
                            AP(xmid, Wp, [[513 * NCH, 128], [513, NCH]])),
                        waits=pl.gate("xmid"), incs=[(v_xm, 1)])
            Wprev = W

            if L == 1:
                pl.step("sync",
                        lambda e, t0=t0, gb=gb, W=W: e.dma_start(
                            baseb[:, gb, :, 0:W],
                            AP(dr["x1"], t0 - X1T0, [[X1N, 128], [128 * X1N, NCH], [1, W]])),
                        waits=pl.gate(("base", gb)), incs=[(d_base[gb], 16)])
                if t0 <= 2047 < t0 + W:
                    bc = 2047 - t0
                    def _bfix2(e, gb=gb, bc=bc):
                        return e.tensor_scalar_mul(baseb[:, gb, :, bc:bc + 1],
                                                   baseb[:, gb, :, bc:bc + 1],
                                                   C["bmask"][:, :])
                    pl.step("vector", _bfix2, waits=w(d_base[gb]), incs=[(v_bfx2, 1)])

            for oc in range(NCH):
                ps = STb[oc % 2]

                def _womm(e, oc=oc, t0=t0, W=W, ps=ps):
                    inst = None
                    for c in range(4):
                        inst = e.matmul(ps[:, 0:W], wos[:, c, 128 * oc:128 * (oc + 1)],
                                        off[:, c * DOMA + (t0 - AT0): c * DOMA + (t0 - AT0) + W],
                                        start=(c == 0), stop=(c == 3))
                    return inst
                pl.step("tensor", _womm, waits=w(d_wo) + pl.gate(("mST", oc % 2)),
                        incs=[(pe_wo, 1)])

                def _xm(e, oc=oc, t0=t0, gb=gb, W=W, ps=ps):
                    if L == 0:
                        return e.tensor_tensor(xmid[:, oc, 1:W + 1], ps[:, 0:W], xn[:, oc, t0:t0 + W], ALU.add)
                    return e.tensor_tensor(xmid[:, oc, 1:W + 1], ps[:, 0:W], baseb[:, gb, oc, 0:W], ALU.add)
                pl.step("vector", _xm, waits=w(pe_wo) + pl.gate("xmid") + (w(d_base[gb], v_bfx2) if L == 1 else []),
                        incs=[(v_xm, 1)])
                pl.set_gate(("mST", oc % 2), v_xm)

            def shv(oc, W=W):
                sh = 1 if oc < 4 else 0
                return xmid[:, oc, sh:sh + W]

            def _mu(e, W=W):
                inst = None
                for oc in range(NCH):
                    inst = e.matmul(RRA[0:1, 0:W], C["onescol"][:, :], shv(oc, W),
                                    start=(oc == 0), stop=(oc == NCH - 1))
                return inst
            pl.step("tensor", _mu, waits=w(v_xm) + pl.gate("mRRA"), incs=[(pe_stat, 1)])
            for oc in range(NCH):
                pl.step("scalar",
                        lambda e, oc=oc, W=W: e.activation(xsq[:, 0, 0:W], shv(oc, W), AF.Square),
                        waits=w(v_xm) + w(pe_stat), incs=[(s_sq2, 1)])
                pl.step("tensor",
                        lambda e, oc=oc, W=W: e.matmul(RRB[0:1, 0:W], C["onescol"][:, :],
                                                       xsq[:, 0, 0:W],
                                                       start=(oc == 0), stop=(oc == NCH - 1)),
                        waits=w(s_sq2) + (pl.gate("mRRB") if oc == 0 else []), incs=[(pe_stat, 1)])

            def _lnv(e, W=W):
                e.tensor_scalar_mul(lnst[:, 0:W], RRA[0:1, 0:W], 1.0 / D)
                e.tensor_scalar_mul(lnst[:, 512:512 + W], RRB[0:1, 0:W], 1.0 / D)
                e.tensor_tensor(xsq[0:1, 0, 0:W], lnst[:, 0:W], lnst[:, 0:W], ALU.mult)
                e.tensor_tensor(lnst[:, 512:512 + W], lnst[:, 512:512 + W], xsq[0:1, 0, 0:W], ALU.subtract)
                return e.tensor_scalar_add(lnst[:, 512:512 + W], lnst[:, 512:512 + W], 1e-5)
            pl.step("vector", _lnv, waits=w(pe_stat) + pl.gate("lnst"), incs=[(v_ln, 1)])
            pl.step("scalar", lambda e, W=W: e.activation(lnst[:, 512:512 + W], lnst[:, 512:512 + W], AF.Sqrt),
                    waits=w(v_ln), incs=[(s_ln, 1)])

            def _lnr(e, W=W):
                e.reciprocal(lnst[:, 512:512 + W], lnst[:, 512:512 + W])
                return e.tensor_tensor(lnst[:, 0:W], lnst[:, 0:W], lnst[:, 512:512 + W], ALU.mult)
            pl.step("vector", _lnr, waits=w(s_ln), incs=[(v_ln, 1)])

            def _repln(e, W=W):
                e.matmul(RRA[:, 0:W], C["onesf1"][:, :], lnst[0:1, 512:512 + W], start=True, stop=True)
                return e.matmul(RRB[:, 0:W], C["onesf1"][:, :], lnst[0:1, 0:W], start=True, stop=True)
            pl.step("tensor", _repln, waits=w(v_ln), incs=[(pe_rep2, 1)])
            pl.set_gate("lnst", pe_rep2)

            for oc in range(NCH):
                def _u(e, oc=oc, W=W):
                    e.tensor_tensor(x1g[:, oc, 0:W], shv(oc, W), RRA[:, 0:W], ALU.mult)
                    return e.tensor_tensor(x1g[:, oc, 0:W], x1g[:, oc, 0:W], RRB[:, 0:W], ALU.subtract)
                pl.step("vector", _u, waits=w(pe_rep2) + pl.gate("x1g"),
                        incs=[(v_u, 1)])
                pl.step("scalar",
                        lambda e, oc=oc, W=W: e.activation(hln[:, oc, 0:W], x1g[:, oc, 0:W], AF.Identity,
                                                           bias=C["lnb"][:, oc:oc + 1],
                                                           scale=C["lng"][:, oc:oc + 1]),
                        waits=w(v_u) + pl.gate("hln"), incs=[(s_hln, 1)])

            for j, (c0, c1) in enumerate(w1blocks):
                jb = j % 2
                pl.step("gpsimd",
                        lambda e, c0=c0, jb=jb: e.dma_start(
                            w1a[:, jb, :, :],
                            AP(dr["fw1"], c0, [[NCH * H2P, 128], [H2P, NCH], [1, 256]])),
                        waits=pl.gate(("w1", jb)), incs=[(d_w1[jb], 16)])
                pl.step("gpsimd",
                        lambda e, c0=c0, jb=jb: e.dma_start(
                            w1g[:, jb, :, :],
                            AP(dr["fw1"], HIDP + c0, [[NCH * H2P, 128], [H2P, NCH], [1, 256]])),
                        waits=[], incs=[(d_w1[jb], 16)])
                for jj in range(2):
                    mc = (c0 + 128 * jj) // 128

                    def _gmm(e, jj=jj, jb=jb, mc=mc, W=W):
                        for c in range(NCH):
                            e.matmul(ST0[:, 0:W], w1g[:, jb, c, 128 * jj:128 * (jj + 1)],
                                     hln[:, c, 0:W], start=(c == 0), stop=False)
                        return e.matmul(ST0[:, 0:W], fb1[:, HIDP + 128 * mc:HIDP + 128 * mc + 128],
                                        C["onesrow"][:, 0:W], start=False, stop=True)
                    pl.step("tensor", _gmm,
                            waits=w(s_hln) + w(d_w1[jb]) + pl.gate(("mST", 0)),
                            incs=[(pe_w1, 1)])
                    pl.step("scalar", lambda e, W=W: e.activation(gel[:, 0:W], ST0[:, 0:W], AF.Gelu),
                            waits=w(pe_w1) + pl.gate("gel"), incs=[(s_gel, 1)])
                    pl.set_gate(("mST", 0), s_gel)

                    def _amm(e, jj=jj, jb=jb, mc=mc, W=W):
                        for c in range(NCH):
                            e.matmul(ST1[:, 0:W], w1a[:, jb, c, 128 * jj:128 * (jj + 1)],
                                     hln[:, c, 0:W], start=(c == 0), stop=False)
                        return e.matmul(ST1[:, 0:W], fb1[:, 128 * mc:128 * mc + 128],
                                        C["onesrow"][:, 0:W], start=False, stop=True)
                    pl.step("tensor", _amm,
                            waits=w(s_hln) + w(d_w1[jb]) + pl.gate(("mST", 1)),
                            incs=[(pe_w1, 1)])
                    pl.step("vector",
                            lambda e, mc=mc, W=W: e.tensor_tensor(
                                mbuf[:, 512 * mc:512 * mc + W], ST1[:, 0:W], gel[:, 0:W], ALU.mult),
                            waits=w(pe_w1, s_gel) + pl.gate("mbuf"), incs=[(v_m, 1)])
                    pl.set_gate(("mST", 1), v_m)
                    pl.set_gate("gel", v_m)
                pl.set_gate(("w1", jb), pe_w1)

            for mc in range(22):
                blk = mc // 2
                jb = blk % 2
                rr = mc % 2
                if rr == 0:
                    pl.step("gpsimd",
                            lambda e, blk=blk, jb=jb: e.dma_start(
                                w2s[:, jb, :, :],
                                AP(dr["fw2"], blk * 2 * D, [[(HIDP // 128) * D, 128], [D, 2], [1, D]])),
                            waits=pl.gate(("w2", jb)), incs=[(d_w2[jb], 16)])

                def _w2mm(e, mc=mc, jb=jb, rr=rr, W=W, first=(mc == 0)):
                    inst = None
                    for oc in range(NCH):
                        inst = e.matmul(w2slot[oc][:, 0:W] if W != 512 else w2slot[oc],
                                        w2s[:, jb, rr, 128 * oc:128 * (oc + 1)],
                                        mbuf[:, 512 * mc:512 * mc + W],
                                        start=first, stop=False)
                    return inst
                extra = (pl.gate(("mST", 0)) + pl.gate(("mST", 1)) + pl.gate("mRRA")
                         + pl.gate("mRRB") + pl.gate("mTP")) if mc == 0 else []
                pl.step("tensor", _w2mm, waits=w(v_m) + w(d_w2[jb]) + extra, incs=[(pe_w2, 1)])
                if rr == 1:
                    pl.set_gate(("w2", jb), pe_w2)

            def _w2bias(e, W=W):
                inst = None
                for oc in range(NCH):
                    inst = e.matmul(w2slot[oc][:, 0:W] if W != 512 else w2slot[oc],
                                    fb2[:, 128 * oc:128 * (oc + 1)],
                                    C["onesrow"][:, 0:W], start=False, stop=True)
                return inst
            pl.step("tensor", _w2bias, waits=[], incs=[(pe_w2, 1)])

            for oc in range(NCH):
                pl.step("vector",
                        lambda e, oc=oc, W=W: e.tensor_tensor(
                            x1g[:, oc, 0:W], xmid[:, oc, 1:W + 1], w2slot[oc][:, 0:W], ALU.add),
                        waits=w(pe_w2, s_hln), incs=[(v_x1, 1)])
            pl.set_gate(("mST", 0), v_x1)
            pl.set_gate(("mST", 1), v_x1)
            pl.set_gate("mTP", v_x1)
            pl.set_gate("mRRA", v_x1)
            pl.set_gate("mRRB", v_x1)
            pl.set_gate("xmid", v_x1)
            pl.set_gate("mbuf", pe_w2)

            if L == 0:
                pl.step("sync",
                        lambda e, t0=t0, W=W: e.dma_start(
                            AP(dr["x1"], t0 - X1T0, [[X1N, 128], [128 * X1N, NCH], [1, W]]),
                            x1g[:, :, 0:W]),
                        waits=w(v_x1), incs=[(d_x1o, 16)])
                pl.set_gate("x1g", d_x1o)
                pl.G["x1g"] = pl.G["x1g"] + w(s_hln)
            else:
                ntt = W // 128
                for tt in range(ntt):
                    tg = t0 + 128 * tt
                    if tg < OUT0:
                        continue
                    b2 = tt % 2
                    pl.step("sync",
                            lambda e, tg=tg, b2=b2: e.dma_start(xacc[:, b2, 0:D], dr["x"][tg:tg + 128, :]),
                            waits=pl.gate(("xacc", b2)), incs=[(d_xin[b2], 16)])
                    pl.step("sync",
                            lambda e, tg=tg, b2=b2: e.dma_start(xacc[:, b2, D:2 * D],
                                                                dr["acc"][tg - OWN:tg - OWN + 128, :]),
                            waits=[], incs=[(d_xin[b2], 16)])

                    def _tp(e, tt=tt):
                        inst = None
                        for oc in range(NCH):
                            inst = e.matmul(TPp[:, 128 * oc:128 * (oc + 1)],
                                            x1g[:, oc, 128 * tt:128 * (tt + 1)], C["idf"][:, :],
                                            start=True, stop=True)
                        return inst
                    pl.step("tensor", _tp, waits=w(v_x1) + pl.gate("mTP"), incs=[(pe_tp, 1)])
                    pl.set_gate("x1g", pe_tp)
                    pl.G["x1g"] = pl.G["x1g"] + w(s_hln)

                    def _fin(e, b2=b2):
                        e.tensor_tensor(xacc[:, b2, 0:D], xacc[:, b2, 0:D], TPp[:, :], ALU.add)
                        return e.tensor_tensor(xacc[:, b2, 0:D], xacc[:, b2, 0:D],
                                               xacc[:, b2, D:2 * D], ALU.add)
                    pl.step("vector", _fin, waits=w(pe_tp) + w(d_xin[b2]), incs=[(v_fin2, 1)])
                    pl.set_gate("mTP", v_fin2)
                    pl.step("sync",
                            lambda e, tg=tg, b2=b2: e.dma_start(dr["out"][tg - OWN:tg - OWN + 128, :],
                                                                xacc[:, b2, 0:D]),
                            waits=w(v_fin2), incs=[(d_oo[b2], 16)])
                    pl.set_gate(("xacc", b2), d_oo[b2])

        if L == 0:
            pl.step("sync", lambda e: e.nop(), waits=w(d_x1o))
        else:
            pl.step("sync", lambda e: e.nop(), waits=w(d_oo[0]) + w(d_oo[1]))

        with nc.Block() as blk:
            pl.emit(nc, blk)


_NC_CACHE = {}


def kernel(L0_attn_x, L0_attn_w_qkv, L0_attn_b_qkv, L0_attn_w_proj, L0_attn_b_proj,
           L0_ffn_wqkv, L0_ffn_q_scale, L0_ffn_k_scale, L0_ffn_wo, L0_ffn_ln_g,
           L0_ffn_ln_b, L0_ffn_w1, L0_ffn_b1, L0_ffn_w2, L0_ffn_b2):
    x = np.ascontiguousarray(np.asarray(L0_attn_x, dtype=np.float32))
    consts = _host_consts()

    shared = dict(
        wqkv_a=_w_fm(np.asarray(L0_attn_w_qkv, np.float32)),
        bqkv_a=np.asarray(L0_attn_b_qkv, np.float32).reshape(1, 3 * D),
        wproj=_w_fm(np.asarray(L0_attn_w_proj, np.float32)),
        bproj=np.asarray(L0_attn_b_proj, np.float32).reshape(1, D),
        bmask=np.ones((128, 1), np.float32),
        logva=np.zeros((128, 32), np.float32),
        logvf=np.zeros((128, 32), np.float32),
        **consts,
    )
    for l in range(2):
        shared[f"fwqkv{l}"] = _w_fm(np.asarray(L0_ffn_wqkv[l], np.float32))
        shared[f"fwo{l}"] = _w_fm(np.asarray(L0_ffn_wo[l], np.float32))
        w1 = np.asarray(L0_ffn_w1[l], np.float32)
        w1p = np.zeros((D, H2P), np.float32)
        w1p[:, 0:HID] = w1[:, 0:HID]
        w1p[:, HIDP:HIDP + HID] = w1[:, HID:2 * HID]
        shared[f"fw1{l}"] = _w_fm(w1p)
        b1 = np.asarray(L0_ffn_b1[l], np.float32)
        b1p = np.zeros((1, H2P), np.float32)
        b1p[0, 0:HID] = b1[0:HID]
        b1p[0, HIDP:HIDP + HID] = b1[HID:2 * HID]
        shared[f"fb1{l}"] = b1p
        w2 = np.asarray(L0_ffn_w2[l], np.float32)
        w2p = np.zeros((HIDP, D), np.float32)
        w2p[0:HID, :] = w2
        shared[f"fw2{l}"] = _w_fm(w2p)
        shared[f"fb2{l}"] = np.asarray(L0_ffn_b2[l], np.float32).reshape(1, D)
        shared[f"lng{l}"] = np.ascontiguousarray(np.asarray(L0_ffn_ln_g[l], np.float32).reshape(NCH, 128).T)
        shared[f"lnb{l}"] = np.ascontiguousarray(np.asarray(L0_ffn_ln_b[l], np.float32).reshape(NCH, 128).T)
        qv = np.asarray(L0_ffn_q_scale[l], np.float32)
        kv = np.asarray(L0_ffn_k_scale[l], np.float32)
        shared[f"qs{l}"] = np.tile(qv, 2).reshape(128, 1)
        shared[f"qssw{l}"] = np.tile(np.concatenate([qv[32:64], qv[0:32]]), 2).reshape(128, 1)
        shared[f"ks{l}"] = np.tile(kv, 2).reshape(128, 1)
        shared[f"kssw{l}"] = np.tile(np.concatenate([kv[32:64], kv[0:32]]), 2).reshape(128, 1)
        shared[f"qsr{l}"] = np.tile(qv, 2).reshape(1, 128)
        shared[f"ksr{l}"] = np.tile(kv, 2).reshape(1, 128)

    if "nc" not in _NC_CACHE:
        _NC_CACHE["nc"] = build_program()
    nc = _NC_CACHE["nc"]

    in_maps = []
    for c in range(8):
        seq, half = c // 2, c % 2
        if half == 1:
            xc = x[seq]
        else:
            xc = np.concatenate([np.zeros((OWN, D), np.float32), x[seq, 0:OWN]], axis=0)
        real0 = TAB_T0 - (0 if half == 1 else OWN)
        tabs = _rotary_tables(np.arange(real0, real0 + TAB_N))
        lv_a = np.zeros((128, 32), np.float32)
        lv_f = np.zeros((128, 32), np.float32)
        if half == 0:
            lv_a[:, 0:16] = NEG
            lv_f[:, 0:16] = NEG
        m = dict(shared)
        m["x_ctx"] = xc
        m["bmask"] = np.full((128, 1), 0.0 if half == 0 else 1.0, np.float32)
        m["logva"] = lv_a
        m["logvf"] = lv_f
        for i in range(4):
            m[f"tab{i}"] = tabs[i]
            m[f"tab{i}lo"] = tabs[4 + i]
        in_maps.append(m)

    res = run_bass_kernel_spmd(nc, in_maps, core_ids=list(range(8)))
    out = np.empty((B, T, D), res.results[0]["out_own"].dtype)
    for c in range(8):
        seq, half = c // 2, c % 2
        out[seq, half * OWN:(half + 1) * OWN] = res.results[c]["out_own"]
    return out
